# version 11
# speedup vs baseline: 1.2328x; 1.0350x over previous
import numpy as np
import ml_dtypes
from contextlib import ExitStack
import concourse.bass as bass
import concourse.mybir as mybir
from concourse.bass import IndirectOffsetOnAxis
from concourse.bass_utils import run_bass_kernel_spmd

F32 = mybir.dt.float32
BF16 = mybir.dt.bfloat16
U32 = mybir.dt.uint32
AF = mybir.ActivationFunctionType
ALU = mybir.AluOpType
AX = mybir.AxisListType

B = 8
T = 2048
D = 1024
H = 16
DH = 64
NPAIR = 8
E = 8
FF = 4096
CAP = 2560
NT = 16
NCK = 8
NG = 20
NTOK = B * T
SCALE = float(1.0 / np.sqrt(32.0))
EPS = 1e-5

ENGS = ("sync", "gpsimd", "tensor", "scalar", "vector")


class Em:
    def __init__(self):
        self.ops = {k: [] for k in ENGS}
        self.cnt = {}
        self.sems = {}

    def val(self, sem):
        return self.cnt.get(sem, 0)

    def wait(self, eng, sem, v=None):
        if v is None:
            v = self.val(sem)
        if v <= 0:
            return
        self.ops[eng].append(lambda e, s=sem, vv=v: e.wait_ge(self.sems[s], vv))

    def op(self, eng, fn, inc=None, amt=1):
        if inc is not None:
            self.cnt[inc] = self.cnt.get(inc, 0) + amt
            self.ops[eng].append(
                lambda e, f=fn, s=inc, a=amt: f(e).then_inc(self.sems[s], a))
        else:
            self.ops[eng].append(lambda e, f=fn: f(e))

    def v(self, fn, waits=()):
        self.wait("vector", "VSER")
        for s, vv in waits:
            self.wait("vector", s, vv)
        self.op("vector", fn, inc="VSER")
        return self.val("VSER")

    def s(self, fn, waits=()):
        self.wait("scalar", "SSER")
        for s, vv in waits:
            self.wait("scalar", s, vv)
        self.op("scalar", fn, inc="SSER")
        return self.val("SSER")

    def t(self, fn, waits=(), inc=True):
        for s, vv in waits:
            self.wait("tensor", s, vv)
        self.op("tensor", fn, inc="TSER" if inc else None)
        return self.val("TSER")

    def g_dma(self, fn, waits=()):
        for s, vv in waits:
            self.wait("gpsimd", s, vv)
        self.op("gpsimd", fn, inc="GDMA", amt=16)
        return self.val("GDMA")

    def g_c(self, fn, waits=()):
        self.wait("gpsimd", "GSER")
        for s, vv in waits:
            self.wait("gpsimd", s, vv)
        self.op("gpsimd", fn, inc="GSER")
        return self.val("GSER")

    def g_cc(self, fn, ccsem, waits=()):
        for s, vv in waits:
            self.wait("gpsimd", s, vv)
        self.cnt[ccsem] = self.cnt.get(ccsem, 0) + 1
        self.ops["gpsimd"].append(
            lambda e, f=fn, s=ccsem: f(e).then_inc(self.sems[s]))

    def sy(self, fn, waits=()):
        for s, vv in waits:
            self.wait("sync", s, vv)
        self.op("sync", fn, inc="SDMA", amt=16)
        return self.val("SDMA")


def build(dbg=False):
    nc = bass.Bass("TRN2", target_bir_lowering=False, debug=False)

    def din(name, shape, dt=F32):
        return nc.dram_tensor(name, shape, dt, kind="ExternalInput").ap()

    x_own = din("x_own", [T, D])
    x_full = din("x_full", [NTOK, D])
    wq = din("wq", [D, D])
    wk = din("wk", [D, D])
    wv = din("wv", [D, D])
    wo = din("wo", [D, D])
    qw = din("qw", [1, 4])
    wr = din("wr", [D, E])
    br = din("br", [1, E])
    w1e = din("w1e", [D, FF])
    b1e = din("b1e", [1, FF])
    w2e = din("w2e", [FF, D])
    b2e = din("b2e", [1, D])
    cid = din("cid", [1, 1])
    maskb = din("maskb", [E, 1])
    ident = din("ident", [128, 128])
    ustrict = din("ustrict", [128, 128])
    m16p = din("m16p", [128, 128])
    mbdm = din("mbdm", [512, 128])
    selmask = din("selmask", [2, 128])

    y_out = nc.dram_tensor("y", [T, D], F32, kind="ExternalOutput").ap()

    dbg_o = {}
    if dbg:
        for name, shape, dt in [
            ("d_logT", [E, T], F32), ("d_e1", [128, NT], F32), ("d_e2", [128, NT], F32),
            ("d_lp1", [128, NT], F32), ("d_lp2", [128, NT], F32),
            ("d_g1", [128, NT], F32), ("d_g2", [128, NT], F32),
            ("d_sl1", [128, NT], F32), ("d_sl2", [128, NT], F32),
            ("d_G", [E, E], F32), ("d_offr", [1, E], F32),
            ("d_src", [128, 32], F32),
            ("d_xnT0", [128, T], BF16), ("d_qT0", [128, T], BF16),
            ("d_kT0", [128, T], BF16), ("d_vt0", [128, H * (DH + 1)], BF16),
            ("d_bufT0", [128, CAP], BF16),
            ("d_yf0", [128, D], F32), ("d_mv0", [128, 2], F32),
            ("d_gp1", [128, NT], F32), ("d_gp2", [128, NT], F32),
            ("d_r1u", [128, NT], U32), ("d_r2u", [128, NT], U32),
            ("d_k1", [128, NT], F32), ("d_k2", [128, NT], F32),
            ("d_contrib", [CAP, D], F32), ("d_outall", [4096, D], F32),
            ("d_w1bf", [32 * 128, D], BF16),
            ("d_h0", [128, 512], BF16), ("d_w1f0", [128, D], BF16),
            ("d_buf0", [128, 512], BF16), ("d_fout", [4 * 128, D], F32),
            ("d_w2f", [128, D], BF16),
        ]:
            dbg_o[name] = nc.dram_tensor(name, shape, dt, kind="ExternalOutput").ap()

    g_in_d = nc.dram_tensor("g_in_d", [1, E], F32)
    g_out_d = nc.dram_tensor("g_out_d", [E, E], F32)
    crow_d = nc.dram_tensor("crow_d", [1, E], F32)
    crow2_d = nc.dram_tensor("crow2_d", [1, E], F32)
    offr_d = nc.dram_tensor("offr_d", [1, E], F32)
    srcp_d = nc.dram_tensor("srcp_d", [E * CAP, 1], F32)
    srca_d = nc.dram_tensor("srca_d", [E * CAP, 1], F32)
    contrib_d = nc.dram_tensor("contrib_d", [CAP, D], BF16)
    w1bf_d = nc.dram_tensor("w1bf_d", [32, 128, D], BF16)
    w2bf_d = nc.dram_tensor("w2bf_d", [32, 128, D], BF16)
    outall_d = nc.dram_tensor("outall_d", [E * CAP, D], BF16, addr_space="Shared")

    em = Em()
    es = ExitStack()
    esL1 = ExitStack()
    esL2 = ExitStack()
    esR0 = ExitStack()
    esR1 = ExitStack()
    esR2 = ExitStack()
    esR3 = ExitStack()
    esR4 = ExitStack()
    esR5 = ExitStack()
    esR6 = ExitStack()

    def sb(stk, name, shape, dt=F32, side="left"):
        return stk.enter_context(nc.sbuf_tensor(name, shape, dt, side=side))

    s_ident = sb(es, "s_ident", [128, 128])
    s_ustrict = sb(es, "s_ustrict", [128, 128])
    s_m16p = sb(es, "s_m16p", [128, 128])
    s_mbdm = sb(es, "s_mbdm", [128, 512], BF16)
    s_sello = sb(es, "s_sello", [1, 128])
    s_selhi = sb(es, "s_selhi", [1, 128])
    s_maskb = sb(es, "s_maskb", [E, 1])
    s_cid = sb(es, "s_cid", [1, 1])
    s_qw = sb(es, "s_qw", [1, 4])
    s_brT = sb(es, "s_brT", [E, 1])
    s_ones = sb(es, "s_ones", [128, 128])
    s_wrhi = sb(es, "s_wrhi", [128, NCK * E], BF16)
    s_wrlo = sb(es, "s_wrlo", [128, NCK * E], BF16)
    s_wrf = sb(es, "s_wrf", [128, NCK * E])
    s_b2 = sb(es, "s_b2", [1, D], BF16)
    s_onesb = sb(es, "s_onesb", [1, 128], BF16)
    s_b1r = sb(es, "s_b1r", [32, 128])
    s_b1T = sb(es, "s_b1T", [128, 32])
    s_mv = [sb(es, f"s_mv{i}", [128, 2]) for i in range(NT)]
    s_rstd = [sb(es, f"s_rstd{i}", [128, 1]) for i in range(NT)]
    s_qT = [sb(es, f"s_qT{c}", [128, T], BF16) for c in range(NCK)]
    s_Ltok = sb(es, "s_Ltok", [128, 128])
    s_max1 = sb(es, "s_max1", [128, NT])
    s_max2 = sb(es, "s_max2", [128, NT])
    s_oh1 = sb(es, "s_oh1", [128, 128])
    s_oh2 = sb(es, "s_oh2", [128, 128])
    s_msk = sb(es, "s_msk", [128, 128])
    s_iotae = sb(es, "s_iotae", [128, 128])
    s_tmp = sb(es, "s_tmp", [128, 128])
    s_cnt = sb(es, "s_cnt", [128, 128])
    s_pref = sb(es, "s_pref", [128, 128])
    s_totr = sb(es, "s_totr", [1, 128])
    s_totc = sb(es, "s_totc", [128, 1])
    s_tbc = sb(es, "s_tbc", [128, 1])
    s_tbr = sb(es, "s_tbr", [1, 128])
    s_lb = sb(es, "s_lb", [128, 128])
    s_lp1 = sb(es, "s_lp1", [128, NT])
    s_lp2 = sb(es, "s_lp2", [128, NT])
    s_e1 = sb(es, "s_e1", [128, NT])
    s_e2 = sb(es, "s_e2", [128, NT])
    s_eL = sb(es, "s_eL", [128, 128])
    s_ssum = sb(es, "s_ssum", [128, NT])
    s_rs = sb(es, "s_rs", [128, NT])
    s_em1 = sb(es, "s_em1", [128, NT])
    s_em2 = sb(es, "s_em2", [128, NT])
    s_g1 = sb(es, "s_g1", [128, NT])
    s_g2 = sb(es, "s_g2", [128, NT])
    s_tot8 = sb(es, "s_tot8", [1, E])
    s_G = sb(es, "s_G", [E, E])
    s_offr = sb(es, "s_offr", [1, E])
    s_ie8 = sb(es, "s_ie8", [1, E])
    s_ie512 = sb(es, "s_ie512", [1, E])
    s_crow2 = sb(es, "s_crow2", [1, E])
    s_cbcr2 = sb(es, "s_cbcr2", [1, 128])
    s_cbc2 = sb(es, "s_cbc2", [128, 128])
    s_blk = sb(es, "s_blk", [128, NT])
    s_crow = sb(es, "s_crow", [1, E])
    s_cbcr = sb(es, "s_cbcr", [1, 128])
    s_offbcr = sb(es, "s_offbcr", [1, 128])
    s_cbc = sb(es, "s_cbc", [128, 128])
    s_offbc = sb(es, "s_offbc", [128, 128])
    s_t16a = sb(es, "s_t16a", [128, NT])
    s_t16b = sb(es, "s_t16b", [128, NT])
    s_gp1 = sb(es, "s_gp1", [128, NT])
    s_gp2 = sb(es, "s_gp2", [128, NT])
    s_k1 = sb(es, "s_k1", [128, NT])
    s_k2 = sb(es, "s_k2", [128, NT])
    s_sl1 = sb(es, "s_sl1", [128, NT])
    s_sl2 = sb(es, "s_sl2", [128, NT])
    s_sl1u = sb(es, "s_sl1u", [128, NT], U32)
    s_sl2u = sb(es, "s_sl2u", [128, NT], U32)
    s_r1u = sb(es, "s_r1u", [128, NT], U32)
    s_r2u = sb(es, "s_r2u", [128, NT], U32)
    s_ptok = sb(es, "s_ptok", [128, NT])
    s_cid128 = sb(es, "s_cid128", [128, 1])
    s_cidtok = sb(es, "s_cidtok", [128, 1])
    s_val1 = sb(es, "s_val1", [128, NT])
    s_zero = sb(es, "s_zero", [128, 160])
    s_ws = sb(es, "s_ws", [128, NT])
    s_rn = sb(es, "s_rn", [128, NT])
    s_s1 = sb(es, "s_s1", [128, NT])
    s_s2 = sb(es, "s_s2", [128, NT])
    s_w4 = sb(es, "s_w4", [1, 4])
    s_w4a = sb(es, "s_w4a", [1, 4])
    s_w128 = sb(es, "s_w128", [128, 4])
    s_mbdA = sb(es, "s_mbdA", [128, 128])
    s_mbdB = sb(es, "s_mbdB", [128, 128])
    s_mbdT = sb(es, "s_mbdT", [128, 128], BF16)
    s_sc1 = sb(es, "s_sc1", [1, 1])
    s_sc2 = sb(es, "s_sc2", [1, 1])
    s_recA = sb(es, "s_recA", [1, 512])
    s_recB = sb(es, "s_recB", [1, 512])
    s_lntmp = sb(es, "s_lntmp", [1, 512])
    s_stage = sb(es, "s_stage", [128, 512], BF16)
    s_segidx = sb(es, "s_segidx", [128, 1], U32)
    s_segT = sb(es, "s_segT", [128, 32])
    s_srcidx = sb(es, "s_srcidx", [128, 32], U32)
    s_pcol = sb(es, "s_pcol", [128, 1])
    s_cid20 = sb(es, "s_cid20", [128, 1])
    s_xt = [sb(es, f"s_xt{i}", [128, D]) for i in range(2)]
    s_bn6 = sb(es, "s_bn6", [128, 12])
    s_veps = sb(es, "s_veps", [128, 1])
    s_sqv = sb(es, "s_sqv", [128, 1])
    s_mvg = sb(es, "s_mvg", [128, 2])
    s_rstg = sb(es, "s_rstg", [128, 1])
    s_P = [sb(es, f"s_P{i}", [128, 512], BF16) for i in range(4)]
    s_lgs = [sb(es, f"s_lgs{i}", [E, 128]) for i in range(2)]
    s_wc = [sb(es, f"s_wc{i}", [128, D], BF16) for i in range(2)]
    s_lo = [sb(es, f"s_lo{i}", [128, 128], BF16) for i in range(2)]

    s_kT = [sb(esL1, f"s_kT{c}", [128, T], BF16) for c in range(NCK)]
    s_vtok = [sb(esL1, f"s_vtok{i}", [128, H * (DH + 1)], BF16) for i in range(NT)]
    s_xnT = [sb(esL2, f"s_xnT{c}", [128, T], BF16) for c in range(NCK)]
    s_xn = [sb(esL2, f"s_xn{i}", [128, D]) for i in range(2)]

    s_qraw = sb(esR0, "s_qraw", [128, T], BF16, side="right")
    s_wqs = [sb(esR1, f"s_wqs{i}", [128, D], BF16, side="right") for i in range(2)]
    s_wks = [sb(esR1, f"s_wks{i}", [128, D], BF16, side="right") for i in range(2)]
    s_wvb = None
    s_bufT = None

    p_a = [es.enter_context(nc.psum_tensor(f"p_a{i}", [128, 512], F32))
           for i in range(8)]

    for name in ("SDMA", "GDMA", "GSER", "TSER", "SSER", "VSER",
                 "CC1", "CC2", "CC3_0", "CC3_1", "CC3_2", "CC3_3", "CC3_4"):
        em.sems[name] = es.enter_context(nc.semaphore(name))

    em.sy(lambda e: e.dma_start(out=s_ident[:], in_=ident))
    em.sy(lambda e: e.dma_start(out=s_ustrict[:], in_=ustrict))
    em.sy(lambda e: e.dma_start(out=s_m16p[:], in_=m16p))
    em.sy(lambda e: e.dma_start(out=s_sello[:], in_=selmask[0:1, :]))
    em.sy(lambda e: e.dma_start(out=s_selhi[:], in_=selmask[1:2, :]))
    em.sy(lambda e: e.dma_start(out=s_maskb[:], in_=maskb))
    em.sy(lambda e: e.dma_start(out=s_cid[:], in_=cid))
    em.sy(lambda e: e.dma_start(out=s_qw[:], in_=qw))
    em.sy(lambda e: e.dma_start(out=s_brT[:], in_=br.rearrange("a b -> b a")))
    em.sy(lambda e: e.dma_start(out=s_wrf[:].rearrange("p (c e) -> p c e", e=E), in_=wr.rearrange("(c p) e -> p c e", p=128)))
    em.sy(lambda e: e.dma_start(out=s_b1r[:], in_=b1e.rearrange("a (r c) -> (a r) c", c=128)))
    ld_consts = em.val("SDMA")

    em.g_dma(lambda e: e.dma_start(out=s_wrhi[:].rearrange("p (c e) -> p c e", e=E),
                                   in_=wr.rearrange("(c p) e -> p c e", p=128)))
    em.g_dma(lambda e: e.dma_start(out=s_mbdm[:].rearrange("p (k c) -> p k c", c=128),
                                   in_=mbdm.rearrange("(k p) c -> p k c", p=128)))
    em.g_dma(lambda e: e.dma_start(out=s_b2[:], in_=b2e))
    ld_wbf = em.val("GDMA")

    regs = {}
    def _mkregs(e):
        regs["slots"] = e.to_reg(E * CAP - 1)
        regs["tok"] = e.to_reg(NTOK - 1)
        regs["seg"] = e.to_reg(E * NG - 1)
    em.op("gpsimd", _mkregs, inc=None)
    em.g_c(lambda e: e.memset(s_ones[:], 1.0))
    em.g_c(lambda e: e.memset(s_onesb[:], 1.0))
    em.g_c(lambda e: e.iota(s_iotae[:], pattern=[[0, 16], [1, 8]], base=0,
                            channel_multiplier=0, allow_small_or_imprecise_dtypes=True))
    em.g_c(lambda e: e.iota(s_ptok[:], pattern=[[128, NT]], base=0,
                            channel_multiplier=1, allow_small_or_imprecise_dtypes=True))
    em.g_c(lambda e: e.iota(s_pcol[:], pattern=[[0, 1]], base=0,
                            channel_multiplier=1, allow_small_or_imprecise_dtypes=True))
    em.g_c(lambda e: e.iota(s_ie8[:], pattern=[[CAP, E]], base=0,
                            channel_multiplier=0, allow_small_or_imprecise_dtypes=True))
    em.g_c(lambda e: e.iota(s_ie512[:], pattern=[[512, E]], base=0,
                            channel_multiplier=0, allow_small_or_imprecise_dtypes=True))
    em.g_c(lambda e: e.memset(s_zero[:], 0.0))
    g_consts = em.val("GSER")
    em.g_dma(lambda e: e.dma_start(
        out=srcp_d.ap().rearrange("(p f) a -> p (f a)", p=128), in_=s_zero[:]),
        waits=[("GSER", g_consts)])
    zr_done = em.val("GDMA")

    em.v(lambda e: e.tensor_tensor(s_wrlo[:], s_wrf[:], s_wrhi[:], op=ALU.subtract),
         waits=[("GDMA", ld_wbf), ("SDMA", ld_consts)])
    em.t(lambda e: e.transpose(p_a[0][0:128, 0:32], s_b1r[:], s_ident[0:32, 0:32]),
         waits=[("SDMA", ld_consts)])
    v_b1T = em.v(lambda e: e.tensor_copy(s_b1T[:], p_a[0][0:128, 0:32]),
                 waits=[("TSER", em.val("TSER"))])
    t10 = em.t(lambda e: e.matmul(p_a[6][0:128, 0:1], s_ones[0:1, :], s_cid[:],
                                  start=True, stop=True, skip_group_check=True),
               waits=[("SDMA", ld_consts), ("GSER", g_consts)])
    em.v(lambda e: e.tensor_copy(s_cid128[:], p_a[6][0:128, 0:1]), waits=[("TSER", t10)])
    em.v(lambda e: e.tensor_scalar(s_cidtok[:], s_cid128[:], float(T), 1.0,
                                   ALU.mult, ALU.add))
    v_cid = em.val("VSER")

    v_apply = [0] * NT
    t_tile = [0] * NT
    ev_s = {}
    ev_lo = {}
    t_rmm = {}
    xload = [0] * NT
    for i in range(NT):
        wl = []
        if i >= 2:
            wl.append(("VSER", v_apply[i - 2]))
        xload[i] = em.sy(lambda e, i=i: e.dma_start(
            out=s_xt[i % 2][:], in_=x_own[i * 128:(i + 1) * 128, :]), waits=wl)
        xt = s_xt[i % 2]
        xn = s_xn[i % 2]
        em.v(lambda e, xt=xt: e.bn_stats(s_bn6[:, 0:6], xt[:, 0:512]),
             waits=[("SDMA", xload[i])])
        em.v(lambda e, xt=xt: e.bn_stats(s_bn6[:, 6:12], xt[:, 512:1024]))
        em.v(lambda e, i=i: e.bn_aggr(s_mv[i][:], s_bn6[:]))
        em.v(lambda e, i=i: e.tensor_scalar(s_veps[:], s_mv[i][:, 1:2], EPS, None, ALU.add))
        em.s(lambda e: e.activation(s_sqv[:], s_veps[:], AF.Sqrt),
             waits=[("VSER", em.val("VSER"))])
        em.v(lambda e, i=i: e.reciprocal(s_rstd[i][:], s_sqv[:]),
             waits=[("SSER", em.val("SSER"))])
        wl = [("TSER", t_tile[i - 2])] if i >= 2 else []
        v_apply[i] = em.v(lambda e, xt=xt, xn=xn, i=i: e.tensor_scalar(
            xn[:], xt[:], s_mv[i][:, 0:1], s_rstd[i][:], ALU.subtract, ALU.mult),
            waits=wl)
        for c in range(NCK):
            k = i * NCK + c
            pb = p_a[k % 2][0:128, ((k // 2) % 2) * 128:((k // 2) % 2) * 128 + 128]
            wl = [("VSER", v_apply[i])]
            if k >= 4:
                wl.append(("VSER", ev_lo[k - 4]))
            tk = em.t(lambda e, xn=xn, c=c, pb=pb: e.transpose(
                pb, xn[:, c * 128:(c + 1) * 128], s_ident[:]), waits=wl)
            ev_s[k] = em.v(lambda e, c=c, i=i, pb=pb: e.tensor_copy(
                s_xnT[c][:, i * 128:(i + 1) * 128], pb),
                waits=[("TSER", tk)])
            wl = []
            if k >= 2:
                wl.append(("TSER", t_rmm[k - 2]))
            ev_lo[k] = em.v(lambda e, k=k, pb=pb, c=c, i=i: e.tensor_tensor(
                s_lo[k % 2][:], pb,
                s_xnT[c][:, i * 128:(i + 1) * 128], op=ALU.subtract), waits=wl)
            pdst = p_a[4 + i // 4][0:8, (i % 4) * 128:(i % 4) * 128 + 128]
            l_hi = s_wrhi[:, c * E:(c + 1) * E]
            l_lo = s_wrlo[:, c * E:(c + 1) * E]
            rhs_hi = s_xnT[c][:, i * 128:(i + 1) * 128]
            wl = [("VSER", ev_lo[k])]
            if k == 0:
                wl.append(("GDMA", ld_wbf))
            em.t(lambda e, pdst=pdst, l=l_hi, r=rhs_hi, c=c: e.matmul(
                pdst, l, r, start=(c == 0), stop=False, skip_group_check=True),
                waits=wl, inc=False)
            em.t(lambda e, pdst=pdst, l=l_hi, k=k: e.matmul(
                pdst, l, s_lo[k % 2][:], start=False, stop=False,
                skip_group_check=True), inc=False)
            t_rmm[k] = em.t(lambda e, pdst=pdst, l=l_lo, r=rhs_hi, c=c: e.matmul(
                pdst, l, r, start=False, stop=(c == NCK - 1), skip_group_check=True),
                inc=True)
        t_tile[i] = em.val("TSER")
    p1_v = em.val("VSER")
    p1_s = em.val("SSER")
    p1_t = em.val("TSER")

    tr_ev = {}
    for i in range(NT):
        em.v(lambda e, i=i: e.tensor_scalar(
            s_lgs[i % 2][0:8, 0:128],
            p_a[4 + i // 4][0:8, (i % 4) * 128:(i % 4) * 128 + 128],
            s_brT[:, 0:1], None, ALU.add),
            waits=[("TSER", t_tile[i]), ("SDMA", ld_consts)]
            + ([("TSER", tr_ev[("t", i - 2)])] if i >= 2 else []))
        v_lg = em.val("VSER")
        pb = p_a[2 + (i % 2)]
        wl = [("VSER", v_lg)]
        if i >= 2:
            wl.append(("VSER", tr_ev[i - 2]))
        if i < 2:
            wl.append(("SSER", p1_s))
        tk = em.t(lambda e, i=i, pb=pb: e.transpose(
            pb[0:128, 0:8], s_lgs[i % 2][0:8, 0:128], s_ident[0:8, 0:8]), waits=wl)
        tr_ev[("t", i)] = tk
        tr_ev[i] = em.v(lambda e, i=i, pb=pb: e.tensor_copy(
            s_Ltok[:, i * 8:(i + 1) * 8], pb[0:128, 0:8]), waits=[("TSER", tk)])

    L3 = s_Ltok[:].rearrange("p (t e) -> p t e", e=8)
    oh1_3 = s_oh1[:].rearrange("p (t e) -> p t e", e=8)
    oh2_3 = s_oh2[:].rearrange("p (t e) -> p t e", e=8)
    msk3 = s_msk[:].rearrange("p (t e) -> p t e", e=8)
    tmp3 = s_tmp[:].rearrange("p (t e) -> p t e", e=8)
    io3 = s_iotae[:].rearrange("p (t e) -> p t e", e=8)
    lb3 = s_lb[:].rearrange("p (t e) -> p t e", e=8)
    cbc3 = s_cbc[:].rearrange("p (t e) -> p t e", e=8)
    off3 = s_offbc[:].rearrange("p (t e) -> p t e", e=8)

    em.v(lambda e: e.reduce_max(s_max1[:], L3, axis=AX.X))
    em.v(lambda e: e.tensor_tensor(
        oh1_3, L3, s_max1[:].unsqueeze(2).broadcast_to((128, NT, 8)), op=ALU.is_equal))
    em.v(lambda e: e.tensor_tensor(tmp3, oh1_3, io3, op=ALU.mult),
         waits=[("GSER", g_consts)])
    em.v(lambda e: e.reduce_sum(s_e1[:], tmp3, axis=AX.X))
    em.v(lambda e: e.scalar_tensor_tensor(
        s_msk[:], s_oh1[:], -1.0e9, s_Ltok[:], op0=ALU.mult, op1=ALU.add))
    em.v(lambda e: e.reduce_max(s_max2[:], msk3, axis=AX.X))
    em.v(lambda e: e.tensor_tensor(
        oh2_3, msk3, s_max2[:].unsqueeze(2).broadcast_to((128, NT, 8)), op=ALU.is_equal))
    em.v(lambda e: e.tensor_tensor(tmp3, oh2_3, io3, op=ALU.mult))
    em.v(lambda e: e.reduce_sum(s_e2[:], tmp3, axis=AX.X))
    v_top2 = em.val("VSER")
    em.s(lambda e: e.activation(s_eL[:], s_Ltok[:], AF.Exp), waits=[("VSER", v_top2)])
    em.s(lambda e: e.activation(s_em1[:], s_max1[:], AF.Exp))
    em.s(lambda e: e.activation(s_em2[:], s_max2[:], AF.Exp))
    s_gate = em.val("SSER")
    em.v(lambda e: e.reduce_sum(s_ssum[:], s_eL[:].rearrange("p (t e) -> p t e", e=8),
                                axis=AX.X), waits=[("SSER", s_gate)])
    em.v(lambda e: e.reciprocal(s_rs[:], s_ssum[:]))
    em.v(lambda e: e.tensor_tensor(s_g1[:], s_em1[:], s_rs[:], op=ALU.mult))
    em.v(lambda e: e.tensor_tensor(s_g2[:], s_em2[:], s_rs[:], op=ALU.mult))
    v_cnt = em.v(lambda e: e.tensor_tensor(s_cnt[:], s_oh1[:], s_oh2[:], op=ALU.add))
    t1 = em.t(lambda e: e.matmul(p_a[6][0:128, 0:128], s_ustrict[:], s_cnt[:],
                                 start=True, stop=True, skip_group_check=True),
              waits=[("VSER", v_cnt)])
    t2 = em.t(lambda e: e.matmul(p_a[7][0:1, 0:128], s_ones[:, 0:1], s_cnt[:],
                                 start=True, stop=True, skip_group_check=True))
    em.v(lambda e: e.tensor_copy(s_pref[:], p_a[6][0:128, 0:128]), waits=[("TSER", t2)])
    v_totr = em.v(lambda e: e.tensor_copy(s_totr[:], p_a[7][0:1, 0:128]))
    t3 = em.t(lambda e: e.transpose(p_a[6][0:128, 0:1], s_totr[:], s_ident[0:1, 0:1]),
              waits=[("VSER", v_totr)])
    v_totc = em.v(lambda e: e.tensor_copy(s_totc[:], p_a[6][0:128, 0:1]),
                  waits=[("TSER", t3)])
    t4 = em.t(lambda e: e.matmul(p_a[7][0:128, 0:1], s_m16p[:], s_totc[:],
                                 start=True, stop=True, skip_group_check=True),
              waits=[("VSER", v_totc)])
    v_tbc = em.v(lambda e: e.tensor_copy(s_tbc[:], p_a[7][0:128, 0:1]),
                 waits=[("TSER", t4)])
    t5 = em.t(lambda e: e.transpose(p_a[6][0:1, 0:128], s_tbc[:], s_ident[:]),
              waits=[("VSER", v_tbc)])
    v_tbr = em.v(lambda e: e.tensor_copy(s_tbr[:], p_a[6][0:1, 0:128]),
                 waits=[("TSER", t5)])
    t6 = em.t(lambda e: e.matmul(p_a[7][0:128, 0:128], s_ones[0:1, :], s_tbr[:],
                                 start=True, stop=True, skip_group_check=True),
              waits=[("VSER", v_tbr)])
    em.v(lambda e: e.tensor_tensor(s_lb[:], p_a[7][0:128, 0:128], s_pref[:],
                                   op=ALU.add), waits=[("TSER", t6)])
    em.v(lambda e: e.tensor_tensor(tmp3, lb3, oh1_3, op=ALU.mult))
    em.v(lambda e: e.reduce_sum(s_lp1[:], tmp3, axis=AX.X))
    em.v(lambda e: e.tensor_tensor(tmp3, lb3, oh2_3, op=ALU.mult))
    em.v(lambda e: e.reduce_sum(s_lp2[:], tmp3, axis=AX.X))
    v_tots = em.v(lambda e: e.tensor_tensor(
        s_tot8[:], s_tbr[0:1, 120:128], s_totr[0:1, 120:128], op=ALU.add))

    em.g_dma(lambda e: e.dma_start(out=g_in_d.ap(), in_=s_tot8[:]),
             waits=[("VSER", v_tots)])
    em.g_cc(lambda e: e.collective_compute(
        "AllGather", ALU.bypass, replica_groups=[list(range(B))],
        ins=[g_in_d.ap().opt()], outs=[g_out_d.ap().opt()]), "CC1",
        waits=[("GDMA", em.val("GDMA"))])
    em.wait("gpsimd", "CC1", 1)
    g_G = em.g_dma(lambda e: e.dma_start(out=s_G[:], in_=g_out_d.ap()))
    t7 = em.t(lambda e: e.matmul(p_a[6][0:1, 0:8], s_maskb[:], s_G[:],
                                 start=True, stop=True, skip_group_check=True),
              waits=[("GDMA", g_G), ("SDMA", ld_consts)])
    v_offr = em.v(lambda e: e.tensor_copy(s_offr[:], p_a[6][0:1, 0:8]),
                  waits=[("TSER", t7)])
    v_crow = em.v(lambda e: e.tensor_tensor(s_crow[:], s_offr[:], s_ie8[:],
                                            op=ALU.add), waits=[("GSER", g_consts)])
    em.v(lambda e: e.tensor_tensor(s_crow2[:], s_offr[:], s_ie512[:], op=ALU.add))
    g_w1 = em.g_dma(lambda e: e.dma_start(out=crow_d.ap(), in_=s_crow[:]),
                    waits=[("VSER", em.val("VSER"))])
    g_w1b = em.g_dma(lambda e: e.dma_start(out=crow2_d.ap(), in_=s_crow2[:]))
    g_w2 = em.g_dma(lambda e: e.dma_start(out=offr_d.ap(), in_=s_offr[:]))
    em.g_dma(lambda e: e.dma_start(
        out=s_cbcr[:].rearrange("p (a b) -> p a b", b=8),
        in_=crow_d.ap().unsqueeze(1).broadcast_to((1, NT, 8))),
        waits=[("GDMA", g_w2)])
    em.g_dma(lambda e: e.dma_start(
        out=s_cbcr2[:].rearrange("p (a b) -> p a b", b=8),
        in_=crow2_d.ap().unsqueeze(1).broadcast_to((1, NT, 8))))
    g_bcr = em.g_dma(lambda e: e.dma_start(
        out=s_offbcr[:].rearrange("p (a b) -> p a b", b=8),
        in_=offr_d.ap().unsqueeze(1).broadcast_to((1, NT, 8))))
    t8 = em.t(lambda e: e.matmul(p_a[6][0:128, 0:128], s_ones[0:1, :], s_cbcr[:],
                                 start=True, stop=True, skip_group_check=True),
              waits=[("GDMA", g_bcr)])
    em.v(lambda e: e.tensor_copy(s_cbc[:], p_a[6][0:128, 0:128]), waits=[("TSER", t8)])
    t9 = em.t(lambda e: e.matmul(p_a[7][0:128, 0:128], s_ones[0:1, :], s_offbcr[:],
                                 start=True, stop=True, skip_group_check=True))
    v9 = em.v(lambda e: e.tensor_copy(s_offbc[:], p_a[7][0:128, 0:128]), waits=[("TSER", t9)])
    t9b = em.t(lambda e: e.matmul(p_a[7][0:128, 0:128], s_ones[0:1, :], s_cbcr2[:],
                                  start=True, stop=True, skip_group_check=True),
               waits=[("VSER", v9)])
    em.v(lambda e: e.tensor_copy(s_cbc2[:], p_a[7][0:128, 0:128]), waits=[("TSER", t9b)])

    cbc2_3 = s_cbc2[:].rearrange("p (t e) -> p t e", e=8)
    for (ohx, gpx, kx, slx, slux, rux, lpx) in (
            (oh1_3, s_gp1, s_k1, s_sl1, s_sl1u, s_r1u, s_lp1),
            (oh2_3, s_gp2, s_k2, s_sl2, s_sl2u, s_r2u, s_lp2)):
        em.v(lambda e, ohx=ohx: e.tensor_tensor(tmp3, ohx, off3, op=ALU.mult))
        em.v(lambda e: e.reduce_sum(s_t16a[:], tmp3, axis=AX.X))
        em.v(lambda e, gpx=gpx, lpx=lpx: e.tensor_tensor(gpx[:], s_t16a[:], lpx[:], op=ALU.add))
        em.v(lambda e, kx=kx, gpx=gpx: e.tensor_scalar(kx[:], gpx[:], float(CAP), None, ALU.is_lt))
        em.v(lambda e, ohx=ohx: e.tensor_tensor(tmp3, ohx, cbc3, op=ALU.mult))
        em.v(lambda e: e.reduce_sum(s_t16a[:], tmp3, axis=AX.X))
        em.v(lambda e, lpx=lpx: e.tensor_tensor(s_t16b[:], s_t16a[:], lpx[:], op=ALU.add))
        em.v(lambda e, kx=kx: e.tensor_scalar(s_t16a[:], kx[:], -1.0e7, 1.0e7, ALU.mult, ALU.add))
        em.v(lambda e, slx=slx: e.tensor_tensor(slx[:], s_t16b[:], s_t16a[:], op=ALU.add))
        em.v(lambda e, slux=slux, slx=slx: e.tensor_copy(slux[:], slx[:]))
        em.v(lambda e, gpx=gpx: e.tensor_scalar(s_blk[:], gpx[:], 512.0, None, ALU.is_ge))
        for kk in (1024.0, 1536.0, 2048.0):
            em.v(lambda e, gpx=gpx, kk=kk: e.scalar_tensor_tensor(
                s_blk[:], gpx[:], kk, s_blk[:], op0=ALU.is_ge, op1=ALU.add))
        em.v(lambda e, ohx=ohx: e.tensor_tensor(tmp3, ohx, cbc2_3, op=ALU.mult))
        em.v(lambda e: e.reduce_sum(s_t16a[:], tmp3, axis=AX.X))
        em.v(lambda e, lpx=lpx: e.tensor_tensor(s_t16b[:], s_t16a[:], lpx[:], op=ALU.add))
        em.v(lambda e: e.scalar_tensor_tensor(
            s_t16b[:], s_blk[:], 3584.0, s_t16b[:], op0=ALU.mult, op1=ALU.add))
        em.v(lambda e, kx=kx: e.tensor_scalar(s_t16a[:], kx[:], -1.0e7, 1.0e7, ALU.mult, ALU.add))
        em.v(lambda e: e.tensor_tensor(s_blk[:], s_t16b[:], s_t16a[:], op=ALU.add))
        em.v(lambda e, rux=rux: e.tensor_copy(rux[:], s_blk[:]))
    em.v(lambda e: e.tensor_tensor(s_t16a[:], s_g1[:], s_k1[:], op=ALU.mult))
    em.v(lambda e: e.tensor_tensor(s_t16b[:], s_g2[:], s_k2[:], op=ALU.mult))
    em.v(lambda e: e.tensor_tensor(s_ws[:], s_t16a[:], s_t16b[:], op=ALU.add))
    em.v(lambda e: e.tensor_scalar(s_rn[:], s_ws[:], 1.0e-6, None, ALU.max))
    em.v(lambda e: e.reciprocal(s_rn[:], s_rn[:]))
    em.v(lambda e: e.tensor_tensor(s_s1[:], s_t16a[:], s_rn[:], op=ALU.mult))
    em.v(lambda e: e.tensor_tensor(s_s2[:], s_t16b[:], s_rn[:], op=ALU.mult))
    v_vals = em.v(lambda e: e.tensor_scalar(s_val1[:], s_ptok[:], s_cidtok[:, 0:1],
                                            None, ALU.add), waits=[("GSER", g_consts),
                                                                   ("VSER", v_cid)])
    v_slots = em.val("VSER")
    for j in range(NT):
        em.g_dma(lambda e, j=j: e.indirect_dma_start(
            out=srcp_d.ap(), out_offset=IndirectOffsetOnAxis(s_sl1u[:, j:j + 1], 0),
            in_=s_val1[:, j:j + 1], in_offset=None,
            bounds_check=regs["slots"], oob_is_err=False),
            waits=[("VSER", v_slots), ("GDMA", zr_done)] if j == 0 else [])
    for j in range(NT):
        em.g_dma(lambda e, j=j: e.indirect_dma_start(
            out=srcp_d.ap(), out_offset=IndirectOffsetOnAxis(s_sl2u[:, j:j + 1], 0),
            in_=s_val1[:, j:j + 1], in_offset=None,
            bounds_check=regs["slots"], oob_is_err=False))
    g_scat = em.val("GDMA")
    em.g_cc(lambda e: e.collective_compute(
        "AllReduce", ALU.add, replica_groups=[list(range(B))],
        ins=[srcp_d.ap().rearrange("a b -> b a").opt()],
        outs=[srca_d.ap().rearrange("a b -> b a").opt()]), "CC2",
        waits=[("GDMA", g_scat)])


    em.v(lambda e: e.reduce_max(s_sc1[:], s_qw[:], axis=AX.X), waits=[("SDMA", ld_consts)])
    em.v(lambda e: e.tensor_scalar(s_w4a[:], s_qw[:], s_sc1[:, 0:1], None, ALU.subtract))
    em.s(lambda e: e.activation(s_w4[:], s_w4a[:], AF.Exp), waits=[("VSER", em.val("VSER"))])
    em.v(lambda e: e.reduce_sum(s_sc2[:], s_w4[:], axis=AX.X), waits=[("SSER", em.val("SSER"))])
    em.v(lambda e: e.reciprocal(s_sc2[:], s_sc2[:]))
    em.v(lambda e: e.tensor_scalar(s_w4[:], s_w4[:], s_sc2[:, 0:1], None, ALU.mult))
    v_w4 = em.val("VSER")
    t11 = em.t(lambda e: e.matmul(p_a[6][0:128, 0:4], s_ones[0:1, :], s_w4[:],
                                  start=True, stop=True, skip_group_check=True),
               waits=[("VSER", v_w4)])
    em.v(lambda e: e.tensor_copy(s_w128[:], p_a[6][0:128, 0:4]), waits=[("TSER", t11)])
    em.v(lambda e: e.tensor_scalar(s_mbdA[:], s_mbdm[:, 0:128], s_w128[:, 0:1],
                                   None, ALU.mult))
    em.v(lambda e: e.scalar_tensor_tensor(s_mbdB[:], s_mbdm[:, 128:256], s_w128[:, 1:2],
                                          s_mbdA[:], op0=ALU.mult, op1=ALU.add))
    em.v(lambda e: e.scalar_tensor_tensor(s_mbdA[:], s_mbdm[:, 256:384], s_w128[:, 2:3],
                                          s_mbdB[:], op0=ALU.mult, op1=ALU.add))
    em.v(lambda e: e.scalar_tensor_tensor(s_mbdB[:], s_mbdm[:, 384:512], s_w128[:, 3:4],
                                          s_mbdA[:], op0=ALU.mult, op1=ALU.add))
    v_mbd = em.v(lambda e: e.tensor_copy(s_mbdT[:], s_mbdB[:]))

    ev = {}
    gidx = 0
    t_kj = {}
    for j in range(NCK):
        wl = [("TSER", t_kj[j - 2])] if j >= 2 else []
        g_wk = em.g_dma(lambda e, j=j: e.dma_start(
            out=s_wks[j % 2][:].rearrange("p (c f) -> p c f", f=128),
            in_=wk.rearrange("(c p) d -> p c d", p=128)[:, :, j * 128:(j + 1) * 128]),
            waits=wl)
        for sblk in range(4):
            pb = p_a[gidx % 4]
            wl = [("GDMA", g_wk)]
            if gidx >= 4:
                wl.append(("SSER", ev[gidx - 4]))
            if gidx == 0:
                wl += [("GDMA", ld_wbf), ("VSER", p1_v), ("TSER", p1_t)]
            for c in range(NCK):
                em.t(lambda e, j=j, c=c, sblk=sblk, pb=pb: e.matmul(
                    pb[0:128, 0:512],
                    s_wks[j % 2][:, c * 128:(c + 1) * 128],
                    s_xnT[c][:, sblk * 512:(sblk + 1) * 512],
                    start=(c == 0), stop=(c == NCK - 1), skip_group_check=True),
                    waits=wl if c == 0 else [], inc=(c == NCK - 1))
            tk = em.val("TSER")
            ev[gidx] = em.s(lambda e, j=j, sblk=sblk, pb=pb: e.copy(
                s_kT[j][:, sblk * 512:(sblk + 1) * 512], pb[0:128, 0:512]),
                waits=[("TSER", tk)])
            gidx += 1
        t_kj[j] = em.val("TSER")
    t_qj = {}
    for j in range(NCK):
        wl = [("TSER", t_qj[j - 2])] if j >= 2 else []
        g_wq = em.g_dma(lambda e, j=j: e.dma_start(
            out=s_wqs[j % 2][:].rearrange("p (c f) -> p c f", f=128),
            in_=wq.rearrange("(c p) d -> p c d", p=128)[:, :, j * 128:(j + 1) * 128]),
            waits=wl)
        for sblk in range(4):
            pb = p_a[gidx % 4]
            wl = [("SSER", ev[gidx - 4]), ("GDMA", g_wq)]
            for c in range(NCK):
                em.t(lambda e, j=j, c=c, sblk=sblk, pb=pb: e.matmul(
                    pb[0:128, 0:512],
                    s_wqs[j % 2][:, c * 128:(c + 1) * 128],
                    s_xnT[c][:, sblk * 512:(sblk + 1) * 512],
                    start=(c == 0), stop=(c == NCK - 1), skip_group_check=True),
                    waits=wl if c == 0 else [], inc=(c == NCK - 1))
            tk = em.val("TSER")
            ev[gidx] = em.s(lambda e, sblk=sblk, pb=pb: e.copy(
                s_qraw[:, sblk * 512:(sblk + 1) * 512], pb[0:128, 0:512]),
                waits=[("TSER", tk)])
            gidx += 1
        vq = ev[gidx - 1]
        for sblk in range(4):
            pb = p_a[gidx % 4]
            wl = [("SSER", ev[gidx - 4]), ("SSER", vq)]
            if j == 0 and sblk == 0:
                wl.append(("VSER", v_mbd))
            em.t(lambda e, sblk=sblk, pb=pb: e.matmul(
                pb[0:128, 0:512], s_mbdT[:], s_qraw[:, sblk * 512:(sblk + 1) * 512],
                start=True, stop=True, skip_group_check=True), waits=wl)
            tk = em.val("TSER")
            ev[gidx] = em.s(lambda e, j=j, sblk=sblk, pb=pb: e.copy(
                s_qT[j][:, sblk * 512:(sblk + 1) * 512], pb[0:128, 0:512]),
                waits=[("TSER", tk)])
            gidx += 1
        t_qj[j] = em.val("TSER")
    t_q_done = em.val("TSER")
    esR1.close()
    s_wvb = sb(esR2, "s_wvb", [128, NCK * D], BF16, side="right")
    g_wv = em.g_dma(lambda e: e.dma_start(
        out=s_wvb[:].rearrange("p (c d) -> p c d", d=D),
        in_=wv.rearrange("(c p) d -> p c d", p=128)),
        waits=[("TSER", t_q_done)])
    vt_ms = [0] * NT
    for i in range(NT):
        vt_ms[i] = em.v(lambda e, i=i: e.memset(s_vtok[i][:], 1.0))
    for i in range(NT):
        for half in range(2):
            pb = p_a[gidx % 4]
            wl = [("SSER", ev[gidx - 4]), ("VSER", vt_ms[i])]
            if i == 0 and half == 0:
                wl.append(("GDMA", g_wv))
            for c in range(NCK):
                em.t(lambda e, i=i, c=c, half=half, pb=pb: e.matmul(
                    pb[0:128, 0:512],
                    s_xnT[c][:, i * 128:(i + 1) * 128],
                    s_wvb[:, c * D + half * 512: c * D + (half + 1) * 512],
                    start=(c == 0), stop=(c == NCK - 1), skip_group_check=True),
                    waits=wl if c == 0 else [], inc=(c == NCK - 1))
            tk = em.val("TSER")
            dst3 = s_vtok[i][:].rearrange("p (h x) -> p h x", x=DH + 1)[:, half * 8:(half + 1) * 8, 0:DH]
            src3 = pb[0:128, 0:512].rearrange("p (h x) -> p h x", x=DH)
            ev[gidx] = em.s(lambda e, dst3=dst3, src3=src3: e.copy(dst3, src3),
                            waits=[("TSER", tk)])
            gidx += 1
    p3_v = em.val("VSER")
    p3_s2 = em.val("SSER")
    p3_t = em.val("TSER")
    esR2.close()
    esR0.close()
    esL2.close()
    st_wc = {}
    for fc in range(32):
        wl = [("SDMA", st_wc[fc - 2])] if fc >= 2 else []
        gld = em.g_dma(lambda e, fc=fc: e.dma_start(
            out=s_wc[fc % 2][:].rearrange("p (c f) -> p c f", f=128),
            in_=w1e.rearrange("(c p) f -> p c f", p=128)[:, :, fc * 128:(fc + 1) * 128]),
            waits=wl)
        st_wc[fc] = em.sy(lambda e, fc=fc: e.dma_start(
            out=w1bf_d.ap()[fc, :, :], in_=s_wc[fc % 2][:]), waits=[("GDMA", gld)])
    for fc in range(32):
        wl = [("SDMA", st_wc[(32, fc - 2)])] if fc >= 2 else [("SDMA", st_wc[31])]
        gld = em.g_dma(lambda e, fc=fc: e.dma_start(
            out=s_wc[fc % 2][:], in_=w2e[fc * 128:(fc + 1) * 128, :]),
            waits=wl)
        st_wc[(32, fc)] = em.sy(lambda e, fc=fc: e.dma_start(
            out=w2bf_d.ap()[fc, :, :], in_=s_wc[fc % 2][:]), waits=[("GDMA", gld)])
    st_wcast = em.val("SDMA")

    s_bufT = [sb(esR3, f"s_bufT{c}", [128, CAP], BF16, side="right") for c in range(NCK)]
    s_xg = [sb(esR4, f"s_xg{i}", [128, D], side="right") for i in range(2)]
    s_xgn = [sb(esR4, f"s_xgn{i}", [128, D], BF16, side="right") for i in range(3)]

    em.wait("gpsimd", "CC2", 1)
    em.v(lambda e: e.tensor_scalar(s_cid20[:], s_cid128[:], float(NG), None, ALU.mult))
    em.v(lambda e: e.tensor_scalar(s_tmp[:, 0:1], s_pcol[:], s_cid20[:, 0:1],
                                   None, ALU.add))
    em.v(lambda e: e.tensor_scalar(s_tmp[:, 1:2], s_pcol[:], float(NG), None, ALU.is_ge))
    em.v(lambda e: e.scalar_tensor_tensor(s_tmp[:, 2:3], s_tmp[:, 1:2], 1.0e6,
                                          s_tmp[:, 0:1], op0=ALU.mult, op1=ALU.add))
    em.v(lambda e: e.tensor_copy(s_segidx[:], s_tmp[:, 2:3]))
    v_seg = em.val("VSER")
    gzw = em.g_c(lambda e: e.memset(s_eL[:], 0.0), waits=[("VSER", v_seg)])
    g_seg = em.g_dma(lambda e: e.indirect_dma_start(
        out=s_eL[:], out_offset=None,
        in_=srca_d.ap().rearrange("(a b) c -> a (b c)", b=128),
        in_offset=IndirectOffsetOnAxis(s_segidx[:, 0:1], 0),
        bounds_check=regs["seg"], oob_is_err=False),
        waits=[("GSER", gzw)])
    t_seg = em.t(lambda e: e.transpose(p_a[4][0:128, 0:128], s_eL[:], s_ident[:]),
                 waits=[("GDMA", g_seg), ("TSER", p3_t)])
    em.v(lambda e: e.tensor_copy(s_segT[:, 0:32], p_a[4][0:128, 0:32]),
         waits=[("TSER", t_seg)])
    em.v(lambda e: e.tensor_scalar(s_tmp[:, 0:NG], s_segT[:, 0:NG], 0.0, None,
                                   ALU.is_equal))
    em.v(lambda e: e.scalar_tensor_tensor(s_tmp[:, 32:32 + NG], s_tmp[:, 0:NG], 1.0e7,
                                          s_segT[:, 0:NG], op0=ALU.mult, op1=ALU.add))
    em.v(lambda e: e.tensor_scalar(s_tmp[:, 32:32 + NG], s_tmp[:, 32:32 + NG], -1.0,
                                   None, ALU.add))
    em.v(lambda e: e.tensor_copy(s_srcidx[:, 0:NG], s_tmp[:, 32:32 + NG]))
    v_srcidx = em.val("VSER")

    v_buf = {}
    gz = {}

    def emit_p4_group(g):
        gz[g] = em.g_c(lambda e, g=g: e.memset(s_xg[g % 2][:], 0.0),
                       waits=([("VSER", v_buf[g - 2])] if g >= 2 else [("VSER", v_srcidx)]))
        g_gath = em.g_dma(lambda e, g=g: e.indirect_dma_start(
            out=s_xg[g % 2][:], out_offset=None,
            in_=x_full, in_offset=IndirectOffsetOnAxis(s_srcidx[:, g:g + 1], 0),
            bounds_check=regs["tok"], oob_is_err=False),
            waits=[("GSER", gz[g])])
        xgn = s_xgn[g % 3]
        em.v(lambda e, g=g: e.bn_stats(s_bn6[:, 0:6], s_xg[g % 2][:, 0:512]),
             waits=[("GDMA", g_gath)])
        em.v(lambda e, g=g: e.bn_stats(s_bn6[:, 6:12], s_xg[g % 2][:, 512:1024]))
        em.v(lambda e: e.bn_aggr(s_mvg[:], s_bn6[:]))
        em.v(lambda e: e.tensor_scalar(s_veps[:], s_mvg[:, 1:2], EPS, None, ALU.add))
        em.s(lambda e: e.activation(s_sqv[:], s_veps[:], AF.Sqrt),
             waits=[("VSER", em.val("VSER"))])
        em.v(lambda e: e.reciprocal(s_rstg[:], s_sqv[:]), waits=[("SSER", em.val("SSER"))])
        wl = []
        if g >= 3:
            wl.append(("SDMA", v_buf[("t", g - 3)]))
        em.v(lambda e, g=g, xgn=xgn: e.tensor_scalar(
            xgn[:], s_xg[g % 2][:], s_mvg[:, 0:1], s_rstg[:, 0:1], ALU.subtract, ALU.mult),
            waits=wl)
        v_apply_g = em.val("VSER")
        for c in range(NCK):
            em.sy(lambda e, xgn=xgn, c=c, g=g: e.dma_start(
                out=s_bufT[c][:, g * 128:(g + 1) * 128],
                in_=xgn[:, c * 128:(c + 1) * 128], transpose=True),
                waits=[("VSER", v_apply_g)] if c == 0 else [])
        v_buf[g] = em.val("VSER")
        v_buf[("t", g)] = em.val("SDMA")

    ctxN = s_qT
    last_exp = {0: 0, 1: 0}
    last_avmm = {}
    v_norm_unit = {}
    u = 0
    for p in range(NPAIR):
        hA, hB = 2 * p, 2 * p + 1
        for sq in range(4):
            setA = p_a[4 + 2 * (u % 2)]
            setB = p_a[5 + 2 * (u % 2)]
            exp_done = [0] * NT
            for i in range(NT + 1):
                if i < NT:
                    par = i % 2
                    wl = [("SSER", last_exp[par])]
                    if u == 0 and i == 0:
                        wl += [("VSER", p3_v), ("SSER", p3_s2)]
                    if i == 0 and u >= 1:
                        wl.append(("VSER", v_norm_unit[u - 1]))
                    if i == 0 and u >= 2:
                        wl.append(("VSER", v_norm_unit[u - 2]))
                    em.t(lambda e, p=p, i=i, sq=sq, par=par: e.matmul(
                        p_a[par][0:128, 0:512],
                        s_kT[p][0:64, i * 128:(i + 1) * 128],
                        s_qT[p][0:64, sq * 512:(sq + 1) * 512],
                        start=True, stop=True, skip_group_check=True,
                        tile_position=(0, 0)), waits=wl, inc=False)
                    tk = em.t(lambda e, p=p, i=i, sq=sq, par=par: e.matmul(
                        p_a[2 + par][0:128, 0:512],
                        s_kT[p][64:128, i * 128:(i + 1) * 128],
                        s_qT[p][64:128, sq * 512:(sq + 1) * 512],
                        start=True, stop=True, skip_group_check=True,
                        tile_position=(64, 0)), inc=True)
                    wl = [("TSER", tk)]
                    if i >= 2:
                        wl.append(("TSER", last_avmm[(u, i - 2)]))
                    em.s(lambda e, par=par: e.activation(
                        s_P[par][:], p_a[par][0:128, 0:512], AF.Exp, scale=SCALE),
                        waits=wl)
                    em.s(lambda e, par=par: e.activation(
                        s_P[2 + par][:], p_a[2 + par][0:128, 0:512], AF.Exp, scale=SCALE))
                    exp_done[i] = em.val("SSER")
                    last_exp[par] = exp_done[i]
                if i >= 1:
                    ii = i - 1
                    par = ii % 2
                    wl = [("SSER", exp_done[ii])]
                    em.t(lambda e, ii=ii, hA=hA, par=par, setA=setA: e.matmul(
                        setA[0:65, 0:512],
                        s_vtok[ii][:, hA * 65:hA * 65 + 65],
                        s_P[par][:],
                        start=(ii == 0), stop=(ii == NT - 1), skip_group_check=True),
                        waits=wl, inc=False)
                    last_avmm[(u, ii)] = em.t(
                        lambda e, ii=ii, hB=hB, par=par, setB=setB: e.matmul(
                            setB[0:65, 0:512],
                            s_vtok[ii][:, hB * 65:hB * 65 + 65],
                            s_P[2 + par][:],
                            start=(ii == 0), stop=(ii == NT - 1), skip_group_check=True),
                        inc=True)
            t_av = em.val("TSER")
            em.s(lambda e, setA=setA: e.activation(s_lntmp[:], setA[64:65, 0:512], AF.Ln),
                 waits=[("TSER", t_av)])
            em.s(lambda e: e.activation(s_recA[:], s_lntmp[:], AF.Exp, scale=-1.0))
            em.s(lambda e, setB=setB: e.activation(s_lntmp[:], setB[64:65, 0:512], AF.Ln))
            s_den = em.s(lambda e: e.activation(s_recB[:], s_lntmp[:], AF.Exp, scale=-1.0))
            wl = [("SSER", s_den), ("SSER", last_exp[0])]
            em.t(lambda e: e.matmul(
                p_a[0][0:128, 0:512], s_sello[:], s_recA[:],
                start=True, stop=False, skip_group_check=True), waits=wl, inc=False)
            t_bc = em.t(lambda e: e.matmul(
                p_a[0][0:128, 0:512], s_selhi[:], s_recB[:],
                start=False, stop=True, skip_group_check=True), inc=True)
            em.v(lambda e, setA=setA: e.tensor_copy(s_stage[0:64, :], setA[0:64, 0:512]),
                 waits=[("TSER", t_av)])
            em.v(lambda e, setB=setB: e.tensor_copy(s_stage[64:128, :], setB[0:64, 0:512]))
            v_norm_unit[u] = em.v(lambda e, p=p, sq=sq: e.tensor_tensor(
                ctxN[p][:, sq * 512:(sq + 1) * 512], s_stage[:], p_a[0][0:128, 0:512],
                op=ALU.mult), waits=[("TSER", t_bc)])
            if u < NG:
                emit_p4_group(u)
            u += 1
    p4_v = em.val("VSER")
    p4_st = em.val("SDMA")
    esR4.close()
    p5_v = em.val("VSER")
    p5_s = em.val("SSER")
    p5_t = em.val("TSER")
    esL1.close()

    s_hT = [sb(esR5, f"s_hT{f}", [128, 512], BF16, side="right") for f in range(32)]
    s_w1f = [sb(esR5, f"s_w1f{i}", [128, 2 * D], BF16, side="right") for i in range(2)]
    s_w2f = [sb(esR5, f"s_w2f{i}", [128, 2 * D], BF16, side="right") for i in range(2)]
    s_fout = [sb(esR5, f"s_fout{i}", [128, D], BF16, side="right") for i in range(4)]

    mm2_ev = 0
    t_mm2_prev = 0
    g_w1ld = {}
    s_gelu = {}
    st_contrib_blk = {}
    for blk in range(5):
        for fc in range(32):
            if fc % 2 == 0:
                wl = [("SDMA", st_wcast)]
                prev = (blk * 32 + fc) - 4
                if prev >= 0:
                    wl.append(("TSER", g_w1ld.get(("mm", prev + 1), 0)))
                g_w1ld[("ld", blk * 32 + fc)] = em.g_dma(
                    lambda e, fc=fc: e.dma_start(
                        out=s_w1f[(fc // 2) % 2][:].rearrange("p (a d) -> p a d", a=2),
                        in_=w1bf_d.ap()[fc:fc + 2, :, :].rearrange("a p d -> p a d")),
                    waits=wl)
            pb = p_a[6 + (fc % 2)]
            wl = [("GDMA", g_w1ld[("ld", blk * 32 + fc - (fc % 2))])]
            if blk == 0 and fc in (0, 1):
                wl += [("VSER", p4_v), ("SDMA", p4_st), ("VSER", p5_v), ("SSER", p5_s)]
            if blk * 32 + fc >= 2:
                wl.append(("SSER", s_gelu[blk * 32 + fc - 2]))
            if blk > 0 and fc in (0, 1):
                wl.append(("VSER", mm2_ev))
            for c in range(NCK):
                em.t(lambda e, c=c, fc=fc, blk=blk, pb=pb: e.matmul(
                    pb[0:128, 0:512],
                    s_w1f[(fc // 2) % 2][:, (fc % 2) * D + c * 128:(fc % 2) * D + (c + 1) * 128],
                    s_bufT[c][:, blk * 512:(blk + 1) * 512],
                    start=(c == 0), stop=(c == NCK - 1), skip_group_check=True),
                    waits=wl if c == 0 else [], inc=(c == NCK - 1))
            tk = em.val("TSER")
            g_w1ld[("mm", blk * 32 + fc)] = tk
            gwl = [("TSER", tk)]
            if blk == 0 and fc == 0:
                gwl.append(("VSER", v_b1T))
            s_gelu[blk * 32 + fc] = em.s(lambda e, fc=fc, pb=pb: e.activation(
                s_hT[fc][:], pb[0:128, 0:512], AF.Gelu, bias=s_b1T[:, fc:fc + 1]),
                waits=gwl)
        if dbg and blk == 2:
            em.sy(lambda e: e.dma_start(out=dbg_o["d_h0"], in_=s_hT[0][:]),
                  waits=[("SSER", s_gelu[blk * 32 + 0])])
            em.sy(lambda e: e.dma_start(out=dbg_o["d_w1f0"], in_=s_w1f[0][:]),
                  waits=[("SSER", s_gelu[blk * 32 + 0]), ("SSER", s_gelu[blk * 32 + 1])])
            em.sy(lambda e: e.dma_start(out=dbg_o["d_buf0"], in_=s_bufT[0][:, blk * 512:(blk + 1) * 512]),
                  waits=[("SSER", s_gelu[blk * 32 + 0])])
        s_gelu_all = em.val("SSER")
        for fc in range(32):
            if fc % 2 == 0:
                w2wl = [("SDMA", st_wcast)]
                if fc >= 4:
                    w2wl.append(("TSER", g_w1ld.get(("mm2", blk, fc - 3), 0)))
                elif blk > 0:
                    w2wl.append(("TSER", t_mm2_prev))
                g_w2ld = em.g_dma(lambda e, fc=fc: e.dma_start(
                    out=s_w2f[(fc // 2) % 2][:].rearrange("p (a d) -> p a d", a=2),
                    in_=w2bf_d.ap()[fc:fc + 2, :, :].rearrange("a p d -> p a d")),
                    waits=w2wl)
            for ct in range(4):
                for half in range(2):
                    pb = p_a[2 * ct + half]
                    wl = []
                    if fc == 0 and ct == 0 and half == 0:
                        wl = [("GDMA", g_w2ld), ("SSER", s_gelu_all)]
                        if blk > 0:
                            wl.append(("VSER", mm2_ev))
                        else:
                            wl.append(("VSER", p5_v))
                    elif ct == 0 and half == 0:
                        wl = [("GDMA", g_w2ld)]
                    em.t(lambda e, fc=fc, ct=ct, half=half, pb=pb: e.matmul(
                        pb[0:128, 0:512],
                        s_hT[fc][:, ct * 128:(ct + 1) * 128],
                        s_w2f[(fc // 2) % 2][:, (fc % 2) * D + half * 512:(fc % 2) * D + (half + 1) * 512],
                        start=(fc == 0), stop=False, skip_group_check=True),
                        waits=wl, inc=(ct == 3 and half == 1))
            g_w1ld[("mm2", blk, fc)] = em.val("TSER")
        for ct in range(4):
            for half in range(2):
                pb = p_a[2 * ct + half]
                em.t(lambda e, ct=ct, half=half, pb=pb: e.matmul(
                    pb[0:128, 0:512], s_onesb[0:1, 0:128],
                    s_b2[0:1, half * 512:(half + 1) * 512],
                    start=False, stop=True, skip_group_check=True),
                    inc=(ct == 3 and half == 1))
        t_mm2 = em.val("TSER")
        t_mm2_prev = t_mm2
        for ct in range(4):
            for half in range(2):
                wl = [("TSER", t_mm2)] if (ct == 0 and half == 0) else []
                if blk >= 1 and ct == 0 and half == 0:
                    wl.append(("SDMA", st_contrib_blk[blk - 1]))
                em.v(lambda e, ct=ct, half=half: e.tensor_copy(
                    s_fout[ct][:, half * 512:(half + 1) * 512],
                    p_a[2 * ct + half][0:128, 0:512]),
                    waits=wl)
        mm2_ev = em.val("VSER")
        if dbg and blk == 2:
            for ct in range(4):
                em.sy(lambda e, ct=ct: e.dma_start(
                    out=dbg_o["d_fout"][ct * 128:(ct + 1) * 128, :], in_=s_fout[ct][:]),
                    waits=[("VSER", mm2_ev)] if ct == 0 else [])
            em.sy(lambda e: e.dma_start(out=dbg_o["d_w2f"], in_=s_w2f[1][:]),
                  waits=[("VSER", mm2_ev)])
        for ct in range(4):
            g = blk * 4 + ct
            em.sy(lambda e, ct=ct, g=g: e.dma_start(
                out=contrib_d.ap()[g * 128:(g + 1) * 128, :], in_=s_fout[ct][:]),
                waits=[("VSER", mm2_ev)] if ct == 0 else [])
        st_contrib_blk[blk] = em.val("SDMA")
    st_contrib = em.val("SDMA")
    for blk in range(5):
        em.g_cc(lambda e, blk=blk: e.collective_compute(
            "AllGather", ALU.bypass, replica_groups=[list(range(B))],
            ins=[contrib_d.ap()[blk * 512:(blk + 1) * 512, :].opt()],
            outs=[outall_d.ap()[blk * 4096:(blk + 1) * 4096, :].opt()]),
            f"CC3_{blk}", waits=[("SDMA", st_contrib)])
    esR5.close()
    esR3.close()

    s_wob = sb(esR6, "s_wob", [128, NCK * D], BF16, side="right")
    s_og1 = [sb(esR6, f"s_og1{i}", [128, D], BF16, side="right") for i in range(2)]
    s_og2 = [sb(esR6, f"s_og2{i}", [128, D], BF16, side="right") for i in range(2)]
    s_yf = [sb(esR6, f"s_yf{i}", [128, D], side="right") for i in range(2)]
    s_xy = [sb(esR6, f"s_xy{i}", [128, D], side="right") for i in range(2)]
    s_out = [sb(esR6, f"s_out{i}", [128, D], side="right") for i in range(2)]

    g_wo = em.g_dma(lambda e: e.dma_start(
        out=s_wob[:].rearrange("p (c d) -> p c d", d=D),
        in_=wo.rearrange("(c p) d -> p c d", p=128)))

    st_tile = {}
    op_ev = {}
    cm = {}
    ccw_done = False
    for i in range(NT):
        pb0 = p_a[4 + 2 * (i % 2)]
        pb1 = p_a[5 + 2 * (i % 2)]
        wl = [("GDMA", g_wo)]
        if i >= 2:
            wl.append(("VSER", op_ev[i - 2]))
        else:
            wl.append(("VSER", mm2_ev))
            wl.append(("VSER", p5_v))
        for c in range(NCK):
            em.t(lambda e, i=i, c=c, pb0=pb0: e.matmul(
                pb0[0:128, 0:512], ctxN[c][:, i * 128:(i + 1) * 128],
                s_wob[:, c * D: c * D + 512],
                start=(c == 0), stop=(c == NCK - 1), skip_group_check=True),
                waits=wl if c == 0 else [], inc=False)
            em.t(lambda e, i=i, c=c, pb1=pb1: e.matmul(
                pb1[0:128, 0:512], ctxN[c][:, i * 128:(i + 1) * 128],
                s_wob[:, c * D + 512: c * D + 1024],
                start=(c == 0), stop=(c == NCK - 1), skip_group_check=True),
                inc=(c == NCK - 1))
        t_op = em.val("TSER")
        gz1 = em.g_c(lambda e, i=i: e.memset(s_og1[i % 2][:], 0.0),
                     waits=[("VSER", cm[i - 2])] if i >= 2 else [])
        gz2 = em.g_c(lambda e, i=i: e.memset(s_og2[i % 2][:], 0.0))
        if not ccw_done:
            for bb in range(5):
                em.wait("gpsimd", f"CC3_{bb}", 1)
            ccw_done = True
        em.g_dma(lambda e, i=i: e.indirect_dma_start(
            out=s_og1[i % 2][:], out_offset=None, in_=outall_d.ap(),
            in_offset=IndirectOffsetOnAxis(s_r1u[:, i:i + 1], 0),
            bounds_check=regs["slots"], oob_is_err=False),
            waits=[("GSER", gz2), ("VSER", v_slots)])
        gog = em.g_dma(lambda e, i=i: e.indirect_dma_start(
            out=s_og2[i % 2][:], out_offset=None, in_=outall_d.ap(),
            in_offset=IndirectOffsetOnAxis(s_r2u[:, i:i + 1], 0),
            bounds_check=regs["slots"], oob_is_err=False))
        xl = em.sy(lambda e, i=i: e.dma_start(
            out=s_xt[i % 2][:], in_=x_own[i * 128:(i + 1) * 128, :]),
            waits=[("VSER", cm[i - 2]) if i >= 2 else ("VSER", p1_v)])
        em.v(lambda e, i=i: e.tensor_scalar(
            s_yf[i % 2][:], s_og1[i % 2][:], s_s1[:, i:i + 1], None, ALU.mult),
            waits=[("GDMA", gog)])
        em.v(lambda e, i=i: e.scalar_tensor_tensor(
            s_yf[i % 2][:], s_og2[i % 2][:], s_s2[:, i:i + 1], s_yf[i % 2][:],
            op0=ALU.mult, op1=ALU.add))
        em.v(lambda e, i=i: e.tensor_tensor(
            s_xy[i % 2][:], s_xt[i % 2][:], s_yf[i % 2][:], op=ALU.add),
            waits=[("SDMA", xl)])
        wl = [("TSER", t_op)]
        if i >= 2:
            wl.append(("SDMA", st_tile[i - 2]))
        em.v(lambda e, i=i, pb0=pb0: e.tensor_tensor(
            s_out[i % 2][:, 0:512], pb0[0:128, 0:512], s_xy[i % 2][:, 0:512],
            op=ALU.add), waits=wl)
        op_ev[i] = em.v(lambda e, i=i, pb1=pb1: e.tensor_tensor(
            s_out[i % 2][:, 512:1024], pb1[0:128, 0:512], s_xy[i % 2][:, 512:1024],
            op=ALU.add))
        cm[i] = em.val("VSER")
        st_tile[i] = em.sy(lambda e, i=i: e.dma_start(
            out=y_out[i * 128:(i + 1) * 128, :], in_=s_out[i % 2][:]),
            waits=[("VSER", cm[i])])
    em.wait("sync", "SDMA", st_tile[NT - 1])
    em.wait("gpsimd", "GDMA", em.val("GDMA"))

    if dbg:
        snap = em.snap() if hasattr(em, "snap") else dict(em.cnt)
        dmap = [
            ("d_e1", s_e1[:]), ("d_e2", s_e2[:]),
            ("d_lp1", s_lp1[:]), ("d_lp2", s_lp2[:]), ("d_g1", s_g1[:]),
            ("d_g2", s_g2[:]), ("d_sl1", s_sl1[:]), ("d_sl2", s_sl2[:]),
            ("d_G", s_G[:]), ("d_offr", s_offr[:]),
            ("d_src", s_segT[:]),
            ("d_qT0", s_qT[0][:]),
            ("d_yf0", s_yf[0][:]), ("d_mv0", s_mv[0][:]),
            ("d_gp1", s_gp1[:]), ("d_gp2", s_gp2[:]),
            ("d_contrib", contrib_d.ap()), ("d_outall", outall_d.ap()[0:4096, :]),
            ("d_w1bf", w1bf_d.ap().rearrange("a b c -> (a b) c")),
            ("d_r1u", s_r1u[:]), ("d_r2u", s_r2u[:]),
            ("d_k1", s_k1[:]), ("d_k2", s_k2[:]),
        ]
        for name, src in dmap:
            em.sy(lambda e, name=name, src=src: e.dma_start(out=dbg_o[name], in_=src),
                  waits=[("VSER", snap.get("VSER", 0)), ("SSER", snap.get("SSER", 0))])
        em.wait("sync", "SDMA", em.val("SDMA"))

    with nc.Block() as block:
        for eng in ENGS:
            def mk(eng):
                def body(e, _ops=em.ops[eng]):
                    for f in _ops:
                        f(e)
                return body
            getattr(block, eng)(mk(eng))

    esR6.close()
    es.close()
    return nc


def _stage_inputs(inputs):
    x = np.ascontiguousarray(np.asarray(inputs["L0_attn_x"], dtype=np.float32))
    wq = np.ascontiguousarray(np.asarray(inputs["L0_attn_Wq"], dtype=np.float32))
    wk = np.ascontiguousarray(np.asarray(inputs["L0_attn_Wk"], dtype=np.float32))
    wv = np.ascontiguousarray(np.asarray(inputs["L0_attn_Wv"], dtype=np.float32))
    wo = np.ascontiguousarray(np.asarray(inputs["L0_attn_Wo"], dtype=np.float32))
    qw = np.ascontiguousarray(np.asarray(inputs["L0_attn_quadrant_weights"],
                                         dtype=np.float32).reshape(1, 4))
    wr = np.ascontiguousarray(np.asarray(inputs["L0_ffn_w_router"], dtype=np.float32))
    br = np.ascontiguousarray(np.asarray(inputs["L0_ffn_b_router"],
                                         dtype=np.float32).reshape(1, E))
    w1 = np.ascontiguousarray(np.asarray(inputs["L0_ffn_w1"], dtype=np.float32))
    b1 = np.ascontiguousarray(np.asarray(inputs["L0_ffn_b1"], dtype=np.float32))
    w2 = np.ascontiguousarray(np.asarray(inputs["L0_ffn_w2"], dtype=np.float32))
    b2 = np.ascontiguousarray(np.asarray(inputs["L0_ffn_b2"], dtype=np.float32))

    x_full = np.ascontiguousarray(x.reshape(NTOK, D))
    ident = np.eye(128, dtype=np.float32)
    ustrict = np.triu(np.ones((128, 128), np.float32), k=1)
    m16p = np.zeros((128, 128), np.float32)
    for tp in range(NT):
        for ee in range(8):
            for tt in range(NT):
                if tp < tt:
                    m16p[tp * 8 + ee, tt * 8 + ee] = 1.0
    mbdm = np.zeros((4, 128, 128), np.float32)
    for h in range(2):
        base = 64 * h
        for j in range(32):
            mbdm[0][base + j, base + j] = 1.0
            mbdm[1][base + j, base + 32 + j] = 1.0
            mbdm[2][base + 32 + j, base + j] = 1.0
            mbdm[3][base + 32 + j, base + 32 + j] = 1.0
    mbdm = mbdm.reshape(512, 128)
    selmask = np.zeros((2, 128), np.float32)
    selmask[0, 0:64] = 1.0
    selmask[1, 64:128] = 1.0

    in_maps = []
    for b in range(B):
        maskb = np.zeros((E, 1), np.float32)
        maskb[:b, 0] = 1.0
        in_maps.append(dict(
            x_own=np.ascontiguousarray(x[b]),
            x_full=x_full,
            wq=wq, wk=wk, wv=wv, wo=wo, qw=qw, wr=wr, br=br,
            w1e=np.ascontiguousarray(w1[b]),
            b1e=np.ascontiguousarray(b1[b].reshape(1, FF)),
            w2e=np.ascontiguousarray(w2[b]),
            b2e=np.ascontiguousarray(b2[b].reshape(1, D)),
            cid=np.array([[float(b)]], np.float32),
            maskb=maskb,
            ident=ident, ustrict=ustrict, m16p=m16p, mbdm=mbdm,
            selmask=selmask,
        ))
    return in_maps


def kernel(**inputs):
    in_maps = _stage_inputs(inputs)
    nc = build(dbg=False)
    res = run_bass_kernel_spmd(nc, in_maps, core_ids=list(range(B)))
    y = np.stack([np.asarray(r["y"], dtype=np.float32) for r in res.results], axis=0)
    return y


# version 12 (final)
# speedup vs baseline: 1.2587x; 1.0211x over previous
import numpy as np
import ml_dtypes
from contextlib import ExitStack
import concourse.bass as bass
import concourse.mybir as mybir
from concourse.bass import IndirectOffsetOnAxis
from concourse.bass_utils import run_bass_kernel_spmd

F32 = mybir.dt.float32
BF16 = mybir.dt.bfloat16
U32 = mybir.dt.uint32
AF = mybir.ActivationFunctionType
ALU = mybir.AluOpType
AX = mybir.AxisListType

B = 8
T = 2048
D = 1024
H = 16
DH = 64
NPAIR = 8
E = 8
FF = 4096
CAP = 2560
NT = 16
NCK = 8
NG = 20
NTOK = B * T
SCALE = float(1.0 / np.sqrt(32.0))
EPS = 1e-5

ENGS = ("sync", "gpsimd", "tensor", "scalar", "vector")


class Em:
    def __init__(self):
        self.ops = {k: [] for k in ENGS}
        self.cnt = {}
        self.sems = {}

    def val(self, sem):
        return self.cnt.get(sem, 0)

    def wait(self, eng, sem, v=None):
        if v is None:
            v = self.val(sem)
        if v <= 0:
            return
        self.ops[eng].append(lambda e, s=sem, vv=v: e.wait_ge(self.sems[s], vv))

    def op(self, eng, fn, inc=None, amt=1):
        if inc is not None:
            self.cnt[inc] = self.cnt.get(inc, 0) + amt
            self.ops[eng].append(
                lambda e, f=fn, s=inc, a=amt: f(e).then_inc(self.sems[s], a))
        else:
            self.ops[eng].append(lambda e, f=fn: f(e))

    def v(self, fn, waits=()):
        self.wait("vector", "VSER")
        for s, vv in waits:
            self.wait("vector", s, vv)
        self.op("vector", fn, inc="VSER")
        return self.val("VSER")

    def s(self, fn, waits=(), chain=True):
        if chain:
            self.wait("scalar", "SSER")
        for s, vv in waits:
            self.wait("scalar", s, vv)
        self.op("scalar", fn, inc="SSER")
        return self.val("SSER")

    def t(self, fn, waits=(), inc=True):
        for s, vv in waits:
            self.wait("tensor", s, vv)
        self.op("tensor", fn, inc="TSER" if inc else None)
        return self.val("TSER")

    def g_dma(self, fn, waits=()):
        for s, vv in waits:
            self.wait("gpsimd", s, vv)
        self.op("gpsimd", fn, inc="GDMA", amt=16)
        return self.val("GDMA")

    def g_c(self, fn, waits=()):
        self.wait("gpsimd", "GSER")
        for s, vv in waits:
            self.wait("gpsimd", s, vv)
        self.op("gpsimd", fn, inc="GSER")
        return self.val("GSER")

    def g_cc(self, fn, ccsem, waits=()):
        for s, vv in waits:
            self.wait("gpsimd", s, vv)
        self.cnt[ccsem] = self.cnt.get(ccsem, 0) + 1
        self.ops["gpsimd"].append(
            lambda e, f=fn, s=ccsem: f(e).then_inc(self.sems[s]))

    def sy(self, fn, waits=()):
        for s, vv in waits:
            self.wait("sync", s, vv)
        self.op("sync", fn, inc="SDMA", amt=16)
        return self.val("SDMA")


def build(dbg=False):
    nc = bass.Bass("TRN2", target_bir_lowering=False, debug=False)

    def din(name, shape, dt=F32):
        return nc.dram_tensor(name, shape, dt, kind="ExternalInput").ap()

    x_own = din("x_own", [T, D])
    x_full = din("x_full", [NTOK, D])
    wq = din("wq", [D, D])
    wk = din("wk", [D, D])
    wv = din("wv", [D, D])
    wo = din("wo", [D, D])
    qw = din("qw", [1, 4])
    wr = din("wr", [D, E])
    br = din("br", [1, E])
    w1e = din("w1e", [D, FF])
    b1e = din("b1e", [1, FF])
    w2e = din("w2e", [FF, D])
    b2e = din("b2e", [1, D])
    cid = din("cid", [1, 1])
    maskb = din("maskb", [E, 1])
    ident = din("ident", [128, 128])
    ustrict = din("ustrict", [128, 128])
    m16p = din("m16p", [128, 128])
    mbdm = din("mbdm", [512, 128])
    selmask = din("selmask", [2, 128])

    y_out = nc.dram_tensor("y", [T, D], F32, kind="ExternalOutput").ap()

    dbg_o = {}
    if dbg:
        for name, shape, dt in [
            ("d_logT", [E, T], F32), ("d_e1", [128, NT], F32), ("d_e2", [128, NT], F32),
            ("d_lp1", [128, NT], F32), ("d_lp2", [128, NT], F32),
            ("d_g1", [128, NT], F32), ("d_g2", [128, NT], F32),
            ("d_sl1", [128, NT], F32), ("d_sl2", [128, NT], F32),
            ("d_G", [E, E], F32), ("d_offr", [1, E], F32),
            ("d_src", [128, 32], F32),
            ("d_xnT0", [128, T], BF16), ("d_qT0", [128, T], BF16),
            ("d_kT0", [128, T], BF16), ("d_vt0", [128, H * (DH + 1)], BF16),
            ("d_bufT0", [128, CAP], BF16),
            ("d_yf0", [128, D], F32), ("d_mv0", [128, 2], F32),
            ("d_gp1", [128, NT], F32), ("d_gp2", [128, NT], F32),
            ("d_r1u", [128, NT], U32), ("d_r2u", [128, NT], U32),
            ("d_k1", [128, NT], F32), ("d_k2", [128, NT], F32),
            ("d_contrib", [CAP, D], F32), ("d_outall", [4096, D], F32),
            ("d_w1bf", [32 * 128, D], BF16),
            ("d_h0", [128, 512], BF16), ("d_w1f0", [128, D], BF16),
            ("d_buf0", [128, 512], BF16), ("d_fout", [4 * 128, D], F32),
            ("d_w2f", [128, D], BF16),
        ]:
            dbg_o[name] = nc.dram_tensor(name, shape, dt, kind="ExternalOutput").ap()

    g_in_d = nc.dram_tensor("g_in_d", [1, E], F32)
    g_out_d = nc.dram_tensor("g_out_d", [E, E], F32)
    crow_d = nc.dram_tensor("crow_d", [1, E], F32)
    crow2_d = nc.dram_tensor("crow2_d", [1, E], F32)
    offr_d = nc.dram_tensor("offr_d", [1, E], F32)
    srcp_d = nc.dram_tensor("srcp_d", [E * CAP, 1], F32)
    srca_d = nc.dram_tensor("srca_d", [E * CAP, 1], F32)
    contrib_d = nc.dram_tensor("contrib_d", [CAP, D], BF16)
    w1bf_d = nc.dram_tensor("w1bf_d", [32, 128, D], BF16)
    w2bf_d = nc.dram_tensor("w2bf_d", [32, 128, D], BF16)
    outall_d = nc.dram_tensor("outall_d", [E * CAP, D], BF16, addr_space="Shared")

    em = Em()
    es = ExitStack()
    esL1 = ExitStack()
    esL2 = ExitStack()
    esR0 = ExitStack()
    esR1 = ExitStack()
    esR2 = ExitStack()
    esR3 = ExitStack()
    esR4 = ExitStack()
    esR5 = ExitStack()
    esR6 = ExitStack()

    def sb(stk, name, shape, dt=F32, side="left"):
        return stk.enter_context(nc.sbuf_tensor(name, shape, dt, side=side))

    s_ident = sb(es, "s_ident", [128, 128])
    s_ustrict = sb(es, "s_ustrict", [128, 128])
    s_m16p = sb(es, "s_m16p", [128, 128])
    s_mbdm = sb(es, "s_mbdm", [128, 512], BF16)
    s_sello = sb(es, "s_sello", [1, 128])
    s_selhi = sb(es, "s_selhi", [1, 128])
    s_maskb = sb(es, "s_maskb", [E, 1])
    s_cid = sb(es, "s_cid", [1, 1])
    s_qw = sb(es, "s_qw", [1, 4])
    s_brT = sb(es, "s_brT", [E, 1])
    s_ones = sb(es, "s_ones", [128, 128])
    s_wrhi = sb(es, "s_wrhi", [128, NCK * E], BF16)
    s_wrlo = sb(es, "s_wrlo", [128, NCK * E], BF16)
    s_wrf = sb(es, "s_wrf", [128, NCK * E])
    s_b2 = sb(es, "s_b2", [1, D], BF16)
    s_onesb = sb(es, "s_onesb", [1, 128], BF16)
    s_b1r = sb(es, "s_b1r", [32, 128])
    s_b1T = sb(es, "s_b1T", [128, 32])
    s_mv = [sb(es, f"s_mv{i}", [128, 2]) for i in range(NT)]
    s_rstd = [sb(es, f"s_rstd{i}", [128, 1]) for i in range(NT)]
    s_qT = [sb(es, f"s_qT{c}", [128, T], BF16) for c in range(NCK)]
    s_Ltok = sb(es, "s_Ltok", [128, 128])
    s_max1 = sb(es, "s_max1", [128, NT])
    s_max2 = sb(es, "s_max2", [128, NT])
    s_oh1 = sb(es, "s_oh1", [128, 128])
    s_oh2 = sb(es, "s_oh2", [128, 128])
    s_msk = sb(es, "s_msk", [128, 128])
    s_iotae = sb(es, "s_iotae", [128, 128])
    s_tmp = sb(es, "s_tmp", [128, 128])
    s_cnt = sb(es, "s_cnt", [128, 128])
    s_pref = sb(es, "s_pref", [128, 128])
    s_totr = sb(es, "s_totr", [1, 128])
    s_totc = sb(es, "s_totc", [128, 1])
    s_tbc = sb(es, "s_tbc", [128, 1])
    s_tbr = sb(es, "s_tbr", [1, 128])
    s_lb = sb(es, "s_lb", [128, 128])
    s_lp1 = sb(es, "s_lp1", [128, NT])
    s_lp2 = sb(es, "s_lp2", [128, NT])
    s_e1 = sb(es, "s_e1", [128, NT])
    s_e2 = sb(es, "s_e2", [128, NT])
    s_eL = sb(es, "s_eL", [128, 128])
    s_ssum = sb(es, "s_ssum", [128, NT])
    s_rs = sb(es, "s_rs", [128, NT])
    s_em1 = sb(es, "s_em1", [128, NT])
    s_em2 = sb(es, "s_em2", [128, NT])
    s_g1 = sb(es, "s_g1", [128, NT])
    s_g2 = sb(es, "s_g2", [128, NT])
    s_tot8 = sb(es, "s_tot8", [1, E])
    s_G = sb(es, "s_G", [E, E])
    s_offr = sb(es, "s_offr", [1, E])
    s_ie8 = sb(es, "s_ie8", [1, E])
    s_ie512 = sb(es, "s_ie512", [1, E])
    s_crow2 = sb(es, "s_crow2", [1, E])
    s_cbcr2 = sb(es, "s_cbcr2", [1, 128])
    s_cbc2 = sb(es, "s_cbc2", [128, 128])
    s_blk = sb(es, "s_blk", [128, NT])
    s_crow = sb(es, "s_crow", [1, E])
    s_cbcr = sb(es, "s_cbcr", [1, 128])
    s_offbcr = sb(es, "s_offbcr", [1, 128])
    s_cbc = sb(es, "s_cbc", [128, 128])
    s_offbc = sb(es, "s_offbc", [128, 128])
    s_t16a = sb(es, "s_t16a", [128, NT])
    s_t16b = sb(es, "s_t16b", [128, NT])
    s_gp1 = sb(es, "s_gp1", [128, NT])
    s_gp2 = sb(es, "s_gp2", [128, NT])
    s_k1 = sb(es, "s_k1", [128, NT])
    s_k2 = sb(es, "s_k2", [128, NT])
    s_sl1 = sb(es, "s_sl1", [128, NT])
    s_sl2 = sb(es, "s_sl2", [128, NT])
    s_sl1u = sb(es, "s_sl1u", [128, NT], U32)
    s_sl2u = sb(es, "s_sl2u", [128, NT], U32)
    s_r1u = sb(es, "s_r1u", [128, NT], U32)
    s_r2u = sb(es, "s_r2u", [128, NT], U32)
    s_ptok = sb(es, "s_ptok", [128, NT])
    s_cid128 = sb(es, "s_cid128", [128, 1])
    s_cidtok = sb(es, "s_cidtok", [128, 1])
    s_val1 = sb(es, "s_val1", [128, NT])
    s_zero = sb(es, "s_zero", [128, 160])
    s_ws = sb(es, "s_ws", [128, NT])
    s_rn = sb(es, "s_rn", [128, NT])
    s_s1 = sb(es, "s_s1", [128, NT])
    s_s2 = sb(es, "s_s2", [128, NT])
    s_w4 = sb(es, "s_w4", [1, 4])
    s_w4a = sb(es, "s_w4a", [1, 4])
    s_w128 = sb(es, "s_w128", [128, 4])
    s_mbdA = sb(es, "s_mbdA", [128, 128])
    s_mbdB = sb(es, "s_mbdB", [128, 128])
    s_mbdT = sb(es, "s_mbdT", [128, 128], BF16)
    s_sc1 = sb(es, "s_sc1", [1, 1])
    s_sc2 = sb(es, "s_sc2", [1, 1])
    s_recA = sb(es, "s_recA", [1, 512])
    s_recB = sb(es, "s_recB", [1, 512])
    s_lntmp = sb(es, "s_lntmp", [1, 512])
    s_stage = sb(es, "s_stage", [128, 512], BF16)
    s_segidx = sb(es, "s_segidx", [128, 1], U32)
    s_segT = sb(es, "s_segT", [128, 32])
    s_srcidx = sb(es, "s_srcidx", [128, 32], U32)
    s_pcol = sb(es, "s_pcol", [128, 1])
    s_cid20 = sb(es, "s_cid20", [128, 1])
    s_xt = [sb(es, f"s_xt{i}", [128, D]) for i in range(2)]
    s_bn6 = sb(es, "s_bn6", [128, 12])
    s_veps = sb(es, "s_veps", [128, 1])
    s_sqv = sb(es, "s_sqv", [128, 1])
    s_mvg = sb(es, "s_mvg", [128, 2])
    s_rstg = sb(es, "s_rstg", [128, 1])
    s_P = [sb(es, f"s_P{i}", [128, 512], BF16) for i in range(4)]
    s_lgs = [sb(es, f"s_lgs{i}", [E, 128]) for i in range(2)]
    s_wc = [sb(es, f"s_wc{i}", [128, D], BF16) for i in range(2)]
    s_lo = [sb(es, f"s_lo{i}", [128, 128], BF16) for i in range(2)]

    s_kT = [sb(esL1, f"s_kT{c}", [128, T], BF16) for c in range(NCK)]
    s_vtok = [sb(esL1, f"s_vtok{i}", [128, H * (DH + 1)], BF16) for i in range(NT)]
    s_xnT = [sb(esL2, f"s_xnT{c}", [128, T], BF16) for c in range(NCK)]
    s_xn = [sb(esL2, f"s_xn{i}", [128, D]) for i in range(2)]

    s_qraw = sb(esR0, "s_qraw", [128, T], BF16, side="right")
    s_wqs = [sb(esR1, f"s_wqs{i}", [128, D], BF16, side="right") for i in range(2)]
    s_wks = [sb(esR1, f"s_wks{i}", [128, D], BF16, side="right") for i in range(2)]
    s_wvb = None
    s_bufT = None

    p_a = [es.enter_context(nc.psum_tensor(f"p_a{i}", [128, 512], F32))
           for i in range(8)]

    for name in ("SDMA", "GDMA", "GSER", "TSER", "SSER", "VSER",
                 "CC1", "CC2", "CC3_0", "CC3_1", "CC3_2", "CC3_3", "CC3_4"):
        em.sems[name] = es.enter_context(nc.semaphore(name))

    em.sy(lambda e: e.dma_start(out=s_ident[:], in_=ident))
    em.sy(lambda e: e.dma_start(out=s_ustrict[:], in_=ustrict))
    em.sy(lambda e: e.dma_start(out=s_m16p[:], in_=m16p))
    em.sy(lambda e: e.dma_start(out=s_sello[:], in_=selmask[0:1, :]))
    em.sy(lambda e: e.dma_start(out=s_selhi[:], in_=selmask[1:2, :]))
    em.sy(lambda e: e.dma_start(out=s_maskb[:], in_=maskb))
    em.sy(lambda e: e.dma_start(out=s_cid[:], in_=cid))
    em.sy(lambda e: e.dma_start(out=s_qw[:], in_=qw))
    em.sy(lambda e: e.dma_start(out=s_brT[:], in_=br.rearrange("a b -> b a")))
    em.sy(lambda e: e.dma_start(out=s_wrf[:].rearrange("p (c e) -> p c e", e=E), in_=wr.rearrange("(c p) e -> p c e", p=128)))
    em.sy(lambda e: e.dma_start(out=s_b1r[:], in_=b1e.rearrange("a (r c) -> (a r) c", c=128)))
    ld_consts = em.val("SDMA")

    em.g_dma(lambda e: e.dma_start(out=s_wrhi[:].rearrange("p (c e) -> p c e", e=E),
                                   in_=wr.rearrange("(c p) e -> p c e", p=128)))
    em.g_dma(lambda e: e.dma_start(out=s_mbdm[:].rearrange("p (k c) -> p k c", c=128),
                                   in_=mbdm.rearrange("(k p) c -> p k c", p=128)))
    em.g_dma(lambda e: e.dma_start(out=s_b2[:], in_=b2e))
    ld_wbf = em.val("GDMA")

    regs = {}
    def _mkregs(e):
        regs["slots"] = e.to_reg(E * CAP - 1)
        regs["tok"] = e.to_reg(NTOK - 1)
        regs["seg"] = e.to_reg(E * NG - 1)
    em.op("gpsimd", _mkregs, inc=None)
    em.g_c(lambda e: e.memset(s_ones[:], 1.0))
    em.g_c(lambda e: e.memset(s_onesb[:], 1.0))
    em.g_c(lambda e: e.iota(s_iotae[:], pattern=[[0, 16], [1, 8]], base=0,
                            channel_multiplier=0, allow_small_or_imprecise_dtypes=True))
    em.g_c(lambda e: e.iota(s_ptok[:], pattern=[[128, NT]], base=0,
                            channel_multiplier=1, allow_small_or_imprecise_dtypes=True))
    em.g_c(lambda e: e.iota(s_pcol[:], pattern=[[0, 1]], base=0,
                            channel_multiplier=1, allow_small_or_imprecise_dtypes=True))
    em.g_c(lambda e: e.iota(s_ie8[:], pattern=[[CAP, E]], base=0,
                            channel_multiplier=0, allow_small_or_imprecise_dtypes=True))
    em.g_c(lambda e: e.iota(s_ie512[:], pattern=[[512, E]], base=0,
                            channel_multiplier=0, allow_small_or_imprecise_dtypes=True))
    em.g_c(lambda e: e.memset(s_zero[:], 0.0))
    g_consts = em.val("GSER")
    em.g_dma(lambda e: e.dma_start(
        out=srcp_d.ap().rearrange("(p f) a -> p (f a)", p=128), in_=s_zero[:]),
        waits=[("GSER", g_consts)])
    zr_done = em.val("GDMA")

    em.v(lambda e: e.tensor_tensor(s_wrlo[:], s_wrf[:], s_wrhi[:], op=ALU.subtract),
         waits=[("GDMA", ld_wbf), ("SDMA", ld_consts)])
    em.t(lambda e: e.transpose(p_a[0][0:128, 0:32], s_b1r[:], s_ident[0:32, 0:32]),
         waits=[("SDMA", ld_consts)])
    v_b1T = em.v(lambda e: e.tensor_copy(s_b1T[:], p_a[0][0:128, 0:32]),
                 waits=[("TSER", em.val("TSER"))])
    t10 = em.t(lambda e: e.matmul(p_a[6][0:128, 0:1], s_ones[0:1, :], s_cid[:],
                                  start=True, stop=True, skip_group_check=True),
               waits=[("SDMA", ld_consts), ("GSER", g_consts)])
    em.v(lambda e: e.tensor_copy(s_cid128[:], p_a[6][0:128, 0:1]), waits=[("TSER", t10)])
    em.v(lambda e: e.tensor_scalar(s_cidtok[:], s_cid128[:], float(T), 1.0,
                                   ALU.mult, ALU.add))
    v_cid = em.val("VSER")

    v_apply = [0] * NT
    t_tile = [0] * NT
    ev_s = {}
    ev_lo = {}
    t_rmm = {}
    xload = [0] * NT
    for i in range(NT):
        wl = []
        if i >= 2:
            wl.append(("VSER", v_apply[i - 2]))
        xload[i] = em.sy(lambda e, i=i: e.dma_start(
            out=s_xt[i % 2][:], in_=x_own[i * 128:(i + 1) * 128, :]), waits=wl)
        xt = s_xt[i % 2]
        xn = s_xn[i % 2]
        em.v(lambda e, xt=xt: e.bn_stats(s_bn6[:, 0:6], xt[:, 0:512]),
             waits=[("SDMA", xload[i])])
        em.v(lambda e, xt=xt: e.bn_stats(s_bn6[:, 6:12], xt[:, 512:1024]))
        em.v(lambda e, i=i: e.bn_aggr(s_mv[i][:], s_bn6[:]))
        em.v(lambda e, i=i: e.tensor_scalar(s_veps[:], s_mv[i][:, 1:2], EPS, None, ALU.add))
        em.s(lambda e: e.activation(s_sqv[:], s_veps[:], AF.Sqrt),
             waits=[("VSER", em.val("VSER"))])
        em.v(lambda e, i=i: e.reciprocal(s_rstd[i][:], s_sqv[:]),
             waits=[("SSER", em.val("SSER"))])
        wl = [("TSER", t_tile[i - 2])] if i >= 2 else []
        v_apply[i] = em.v(lambda e, xt=xt, xn=xn, i=i: e.tensor_scalar(
            xn[:], xt[:], s_mv[i][:, 0:1], s_rstd[i][:], ALU.subtract, ALU.mult),
            waits=wl)
        for c in range(NCK):
            k = i * NCK + c
            pb = p_a[k % 2][0:128, ((k // 2) % 2) * 128:((k // 2) % 2) * 128 + 128]
            wl = [("VSER", v_apply[i])]
            if k >= 4:
                wl.append(("VSER", ev_lo[k - 4]))
            tk = em.t(lambda e, xn=xn, c=c, pb=pb: e.transpose(
                pb, xn[:, c * 128:(c + 1) * 128], s_ident[:]), waits=wl)
            ev_s[k] = em.v(lambda e, c=c, i=i, pb=pb: e.tensor_copy(
                s_xnT[c][:, i * 128:(i + 1) * 128], pb),
                waits=[("TSER", tk)])
            wl = []
            if k >= 2:
                wl.append(("TSER", t_rmm[k - 2]))
            ev_lo[k] = em.v(lambda e, k=k, pb=pb, c=c, i=i: e.tensor_tensor(
                s_lo[k % 2][:], pb,
                s_xnT[c][:, i * 128:(i + 1) * 128], op=ALU.subtract), waits=wl)
            pdst = p_a[4 + i // 4][0:8, (i % 4) * 128:(i % 4) * 128 + 128]
            l_hi = s_wrhi[:, c * E:(c + 1) * E]
            l_lo = s_wrlo[:, c * E:(c + 1) * E]
            rhs_hi = s_xnT[c][:, i * 128:(i + 1) * 128]
            wl = [("VSER", ev_lo[k])]
            if k == 0:
                wl.append(("GDMA", ld_wbf))
            em.t(lambda e, pdst=pdst, l=l_hi, r=rhs_hi, c=c: e.matmul(
                pdst, l, r, start=(c == 0), stop=False, skip_group_check=True),
                waits=wl, inc=False)
            em.t(lambda e, pdst=pdst, l=l_hi, k=k: e.matmul(
                pdst, l, s_lo[k % 2][:], start=False, stop=False,
                skip_group_check=True), inc=False)
            t_rmm[k] = em.t(lambda e, pdst=pdst, l=l_lo, r=rhs_hi, c=c: e.matmul(
                pdst, l, r, start=False, stop=(c == NCK - 1), skip_group_check=True),
                inc=True)
        t_tile[i] = em.val("TSER")
    p1_v = em.val("VSER")
    p1_s = em.val("SSER")
    p1_t = em.val("TSER")

    tr_ev = {}
    for i in range(NT):
        em.v(lambda e, i=i: e.tensor_scalar(
            s_lgs[i % 2][0:8, 0:128],
            p_a[4 + i // 4][0:8, (i % 4) * 128:(i % 4) * 128 + 128],
            s_brT[:, 0:1], None, ALU.add),
            waits=[("TSER", t_tile[i]), ("SDMA", ld_consts)]
            + ([("TSER", tr_ev[("t", i - 2)])] if i >= 2 else []))
        v_lg = em.val("VSER")
        pb = p_a[2 + (i % 2)]
        wl = [("VSER", v_lg)]
        if i >= 2:
            wl.append(("VSER", tr_ev[i - 2]))
        if i < 2:
            wl.append(("SSER", p1_s))
        tk = em.t(lambda e, i=i, pb=pb: e.transpose(
            pb[0:128, 0:8], s_lgs[i % 2][0:8, 0:128], s_ident[0:8, 0:8]), waits=wl)
        tr_ev[("t", i)] = tk
        tr_ev[i] = em.v(lambda e, i=i, pb=pb: e.tensor_copy(
            s_Ltok[:, i * 8:(i + 1) * 8], pb[0:128, 0:8]), waits=[("TSER", tk)])

    L3 = s_Ltok[:].rearrange("p (t e) -> p t e", e=8)
    oh1_3 = s_oh1[:].rearrange("p (t e) -> p t e", e=8)
    oh2_3 = s_oh2[:].rearrange("p (t e) -> p t e", e=8)
    msk3 = s_msk[:].rearrange("p (t e) -> p t e", e=8)
    tmp3 = s_tmp[:].rearrange("p (t e) -> p t e", e=8)
    io3 = s_iotae[:].rearrange("p (t e) -> p t e", e=8)
    lb3 = s_lb[:].rearrange("p (t e) -> p t e", e=8)
    cbc3 = s_cbc[:].rearrange("p (t e) -> p t e", e=8)
    off3 = s_offbc[:].rearrange("p (t e) -> p t e", e=8)

    em.v(lambda e: e.reduce_max(s_max1[:], L3, axis=AX.X))
    em.v(lambda e: e.tensor_tensor(
        oh1_3, L3, s_max1[:].unsqueeze(2).broadcast_to((128, NT, 8)), op=ALU.is_equal))
    em.v(lambda e: e.tensor_tensor(tmp3, oh1_3, io3, op=ALU.mult),
         waits=[("GSER", g_consts)])
    em.v(lambda e: e.reduce_sum(s_e1[:], tmp3, axis=AX.X))
    em.v(lambda e: e.scalar_tensor_tensor(
        s_msk[:], s_oh1[:], -1.0e9, s_Ltok[:], op0=ALU.mult, op1=ALU.add))
    em.v(lambda e: e.reduce_max(s_max2[:], msk3, axis=AX.X))
    em.v(lambda e: e.tensor_tensor(
        oh2_3, msk3, s_max2[:].unsqueeze(2).broadcast_to((128, NT, 8)), op=ALU.is_equal))
    em.v(lambda e: e.tensor_tensor(tmp3, oh2_3, io3, op=ALU.mult))
    em.v(lambda e: e.reduce_sum(s_e2[:], tmp3, axis=AX.X))
    v_top2 = em.val("VSER")
    em.s(lambda e: e.activation(s_eL[:], s_Ltok[:], AF.Exp), waits=[("VSER", v_top2)])
    em.s(lambda e: e.activation(s_em1[:], s_max1[:], AF.Exp))
    em.s(lambda e: e.activation(s_em2[:], s_max2[:], AF.Exp))
    s_gate = em.val("SSER")
    em.v(lambda e: e.reduce_sum(s_ssum[:], s_eL[:].rearrange("p (t e) -> p t e", e=8),
                                axis=AX.X), waits=[("SSER", s_gate)])
    em.v(lambda e: e.reciprocal(s_rs[:], s_ssum[:]))
    em.v(lambda e: e.tensor_tensor(s_g1[:], s_em1[:], s_rs[:], op=ALU.mult))
    em.v(lambda e: e.tensor_tensor(s_g2[:], s_em2[:], s_rs[:], op=ALU.mult))
    v_cnt = em.v(lambda e: e.tensor_tensor(s_cnt[:], s_oh1[:], s_oh2[:], op=ALU.add))
    t1 = em.t(lambda e: e.matmul(p_a[6][0:128, 0:128], s_ustrict[:], s_cnt[:],
                                 start=True, stop=True, skip_group_check=True),
              waits=[("VSER", v_cnt)])
    t2 = em.t(lambda e: e.matmul(p_a[7][0:1, 0:128], s_ones[:, 0:1], s_cnt[:],
                                 start=True, stop=True, skip_group_check=True))
    em.v(lambda e: e.tensor_copy(s_pref[:], p_a[6][0:128, 0:128]), waits=[("TSER", t2)])
    v_totr = em.v(lambda e: e.tensor_copy(s_totr[:], p_a[7][0:1, 0:128]))
    t3 = em.t(lambda e: e.transpose(p_a[6][0:128, 0:1], s_totr[:], s_ident[0:1, 0:1]),
              waits=[("VSER", v_totr)])
    v_totc = em.v(lambda e: e.tensor_copy(s_totc[:], p_a[6][0:128, 0:1]),
                  waits=[("TSER", t3)])
    t4 = em.t(lambda e: e.matmul(p_a[7][0:128, 0:1], s_m16p[:], s_totc[:],
                                 start=True, stop=True, skip_group_check=True),
              waits=[("VSER", v_totc)])
    v_tbc = em.v(lambda e: e.tensor_copy(s_tbc[:], p_a[7][0:128, 0:1]),
                 waits=[("TSER", t4)])
    t5 = em.t(lambda e: e.transpose(p_a[6][0:1, 0:128], s_tbc[:], s_ident[:]),
              waits=[("VSER", v_tbc)])
    v_tbr = em.v(lambda e: e.tensor_copy(s_tbr[:], p_a[6][0:1, 0:128]),
                 waits=[("TSER", t5)])
    t6 = em.t(lambda e: e.matmul(p_a[7][0:128, 0:128], s_ones[0:1, :], s_tbr[:],
                                 start=True, stop=True, skip_group_check=True),
              waits=[("VSER", v_tbr)])
    em.v(lambda e: e.tensor_tensor(s_lb[:], p_a[7][0:128, 0:128], s_pref[:],
                                   op=ALU.add), waits=[("TSER", t6)])
    em.v(lambda e: e.tensor_tensor(tmp3, lb3, oh1_3, op=ALU.mult))
    em.v(lambda e: e.reduce_sum(s_lp1[:], tmp3, axis=AX.X))
    em.v(lambda e: e.tensor_tensor(tmp3, lb3, oh2_3, op=ALU.mult))
    em.v(lambda e: e.reduce_sum(s_lp2[:], tmp3, axis=AX.X))
    v_tots = em.v(lambda e: e.tensor_tensor(
        s_tot8[:], s_tbr[0:1, 120:128], s_totr[0:1, 120:128], op=ALU.add))

    em.g_dma(lambda e: e.dma_start(out=g_in_d.ap(), in_=s_tot8[:]),
             waits=[("VSER", v_tots)])
    em.g_cc(lambda e: e.collective_compute(
        "AllGather", ALU.bypass, replica_groups=[list(range(B))],
        ins=[g_in_d.ap().opt()], outs=[g_out_d.ap().opt()]), "CC1",
        waits=[("GDMA", em.val("GDMA"))])
    em.wait("gpsimd", "CC1", 1)
    g_G = em.g_dma(lambda e: e.dma_start(out=s_G[:], in_=g_out_d.ap()))
    t7 = em.t(lambda e: e.matmul(p_a[6][0:1, 0:8], s_maskb[:], s_G[:],
                                 start=True, stop=True, skip_group_check=True),
              waits=[("GDMA", g_G), ("SDMA", ld_consts)])
    v_offr = em.v(lambda e: e.tensor_copy(s_offr[:], p_a[6][0:1, 0:8]),
                  waits=[("TSER", t7)])
    v_crow = em.v(lambda e: e.tensor_tensor(s_crow[:], s_offr[:], s_ie8[:],
                                            op=ALU.add), waits=[("GSER", g_consts)])
    em.v(lambda e: e.tensor_tensor(s_crow2[:], s_offr[:], s_ie512[:], op=ALU.add))
    g_w1 = em.g_dma(lambda e: e.dma_start(out=crow_d.ap(), in_=s_crow[:]),
                    waits=[("VSER", em.val("VSER"))])
    g_w1b = em.g_dma(lambda e: e.dma_start(out=crow2_d.ap(), in_=s_crow2[:]))
    g_w2 = em.g_dma(lambda e: e.dma_start(out=offr_d.ap(), in_=s_offr[:]))
    em.g_dma(lambda e: e.dma_start(
        out=s_cbcr[:].rearrange("p (a b) -> p a b", b=8),
        in_=crow_d.ap().unsqueeze(1).broadcast_to((1, NT, 8))),
        waits=[("GDMA", g_w2)])
    em.g_dma(lambda e: e.dma_start(
        out=s_cbcr2[:].rearrange("p (a b) -> p a b", b=8),
        in_=crow2_d.ap().unsqueeze(1).broadcast_to((1, NT, 8))))
    g_bcr = em.g_dma(lambda e: e.dma_start(
        out=s_offbcr[:].rearrange("p (a b) -> p a b", b=8),
        in_=offr_d.ap().unsqueeze(1).broadcast_to((1, NT, 8))))
    t8 = em.t(lambda e: e.matmul(p_a[6][0:128, 0:128], s_ones[0:1, :], s_cbcr[:],
                                 start=True, stop=True, skip_group_check=True),
              waits=[("GDMA", g_bcr)])
    em.v(lambda e: e.tensor_copy(s_cbc[:], p_a[6][0:128, 0:128]), waits=[("TSER", t8)])
    t9 = em.t(lambda e: e.matmul(p_a[7][0:128, 0:128], s_ones[0:1, :], s_offbcr[:],
                                 start=True, stop=True, skip_group_check=True))
    v9 = em.v(lambda e: e.tensor_copy(s_offbc[:], p_a[7][0:128, 0:128]), waits=[("TSER", t9)])
    t9b = em.t(lambda e: e.matmul(p_a[7][0:128, 0:128], s_ones[0:1, :], s_cbcr2[:],
                                  start=True, stop=True, skip_group_check=True),
               waits=[("VSER", v9)])
    em.v(lambda e: e.tensor_copy(s_cbc2[:], p_a[7][0:128, 0:128]), waits=[("TSER", t9b)])

    cbc2_3 = s_cbc2[:].rearrange("p (t e) -> p t e", e=8)
    for (ohx, gpx, kx, slx, slux, rux, lpx) in (
            (oh1_3, s_gp1, s_k1, s_sl1, s_sl1u, s_r1u, s_lp1),
            (oh2_3, s_gp2, s_k2, s_sl2, s_sl2u, s_r2u, s_lp2)):
        em.v(lambda e, ohx=ohx: e.tensor_tensor(tmp3, ohx, off3, op=ALU.mult))
        em.v(lambda e: e.reduce_sum(s_t16a[:], tmp3, axis=AX.X))
        em.v(lambda e, gpx=gpx, lpx=lpx: e.tensor_tensor(gpx[:], s_t16a[:], lpx[:], op=ALU.add))
        em.v(lambda e, kx=kx, gpx=gpx: e.tensor_scalar(kx[:], gpx[:], float(CAP), None, ALU.is_lt))
        em.v(lambda e, ohx=ohx: e.tensor_tensor(tmp3, ohx, cbc3, op=ALU.mult))
        em.v(lambda e: e.reduce_sum(s_t16a[:], tmp3, axis=AX.X))
        em.v(lambda e, lpx=lpx: e.tensor_tensor(s_t16b[:], s_t16a[:], lpx[:], op=ALU.add))
        em.v(lambda e, kx=kx: e.tensor_scalar(s_t16a[:], kx[:], -1.0e7, 1.0e7, ALU.mult, ALU.add))
        em.v(lambda e, slx=slx: e.tensor_tensor(slx[:], s_t16b[:], s_t16a[:], op=ALU.add))
        em.v(lambda e, slux=slux, slx=slx: e.tensor_copy(slux[:], slx[:]))
        em.v(lambda e, gpx=gpx: e.tensor_scalar(s_blk[:], gpx[:], 512.0, None, ALU.is_ge))
        for kk in (1024.0, 1536.0, 2048.0):
            em.v(lambda e, gpx=gpx, kk=kk: e.scalar_tensor_tensor(
                s_blk[:], gpx[:], kk, s_blk[:], op0=ALU.is_ge, op1=ALU.add))
        em.v(lambda e, ohx=ohx: e.tensor_tensor(tmp3, ohx, cbc2_3, op=ALU.mult))
        em.v(lambda e: e.reduce_sum(s_t16a[:], tmp3, axis=AX.X))
        em.v(lambda e, lpx=lpx: e.tensor_tensor(s_t16b[:], s_t16a[:], lpx[:], op=ALU.add))
        em.v(lambda e: e.scalar_tensor_tensor(
            s_t16b[:], s_blk[:], 3584.0, s_t16b[:], op0=ALU.mult, op1=ALU.add))
        em.v(lambda e, kx=kx: e.tensor_scalar(s_t16a[:], kx[:], -1.0e7, 1.0e7, ALU.mult, ALU.add))
        em.v(lambda e: e.tensor_tensor(s_blk[:], s_t16b[:], s_t16a[:], op=ALU.add))
        em.v(lambda e, rux=rux: e.tensor_copy(rux[:], s_blk[:]))
    em.v(lambda e: e.tensor_tensor(s_t16a[:], s_g1[:], s_k1[:], op=ALU.mult))
    em.v(lambda e: e.tensor_tensor(s_t16b[:], s_g2[:], s_k2[:], op=ALU.mult))
    em.v(lambda e: e.tensor_tensor(s_ws[:], s_t16a[:], s_t16b[:], op=ALU.add))
    em.v(lambda e: e.tensor_scalar(s_rn[:], s_ws[:], 1.0e-6, None, ALU.max))
    em.v(lambda e: e.reciprocal(s_rn[:], s_rn[:]))
    em.v(lambda e: e.tensor_tensor(s_s1[:], s_t16a[:], s_rn[:], op=ALU.mult))
    em.v(lambda e: e.tensor_tensor(s_s2[:], s_t16b[:], s_rn[:], op=ALU.mult))
    v_vals = em.v(lambda e: e.tensor_scalar(s_val1[:], s_ptok[:], s_cidtok[:, 0:1],
                                            None, ALU.add), waits=[("GSER", g_consts),
                                                                   ("VSER", v_cid)])
    v_slots = em.val("VSER")
    for j in range(NT):
        em.g_dma(lambda e, j=j: e.indirect_dma_start(
            out=srcp_d.ap(), out_offset=IndirectOffsetOnAxis(s_sl1u[:, j:j + 1], 0),
            in_=s_val1[:, j:j + 1], in_offset=None,
            bounds_check=regs["slots"], oob_is_err=False),
            waits=[("VSER", v_slots), ("GDMA", zr_done)] if j == 0 else [])
    for j in range(NT):
        em.g_dma(lambda e, j=j: e.indirect_dma_start(
            out=srcp_d.ap(), out_offset=IndirectOffsetOnAxis(s_sl2u[:, j:j + 1], 0),
            in_=s_val1[:, j:j + 1], in_offset=None,
            bounds_check=regs["slots"], oob_is_err=False))
    g_scat = em.val("GDMA")
    em.g_cc(lambda e: e.collective_compute(
        "AllReduce", ALU.add, replica_groups=[list(range(B))],
        ins=[srcp_d.ap().rearrange("a b -> b a").opt()],
        outs=[srca_d.ap().rearrange("a b -> b a").opt()]), "CC2",
        waits=[("GDMA", g_scat)])


    em.v(lambda e: e.reduce_max(s_sc1[:], s_qw[:], axis=AX.X), waits=[("SDMA", ld_consts)])
    em.v(lambda e: e.tensor_scalar(s_w4a[:], s_qw[:], s_sc1[:, 0:1], None, ALU.subtract))
    em.s(lambda e: e.activation(s_w4[:], s_w4a[:], AF.Exp), waits=[("VSER", em.val("VSER"))])
    em.v(lambda e: e.reduce_sum(s_sc2[:], s_w4[:], axis=AX.X), waits=[("SSER", em.val("SSER"))])
    em.v(lambda e: e.reciprocal(s_sc2[:], s_sc2[:]))
    em.v(lambda e: e.tensor_scalar(s_w4[:], s_w4[:], s_sc2[:, 0:1], None, ALU.mult))
    v_w4 = em.val("VSER")
    t11 = em.t(lambda e: e.matmul(p_a[6][0:128, 0:4], s_ones[0:1, :], s_w4[:],
                                  start=True, stop=True, skip_group_check=True),
               waits=[("VSER", v_w4)])
    em.v(lambda e: e.tensor_copy(s_w128[:], p_a[6][0:128, 0:4]), waits=[("TSER", t11)])
    em.v(lambda e: e.tensor_scalar(s_mbdA[:], s_mbdm[:, 0:128], s_w128[:, 0:1],
                                   None, ALU.mult))
    em.v(lambda e: e.scalar_tensor_tensor(s_mbdB[:], s_mbdm[:, 128:256], s_w128[:, 1:2],
                                          s_mbdA[:], op0=ALU.mult, op1=ALU.add))
    em.v(lambda e: e.scalar_tensor_tensor(s_mbdA[:], s_mbdm[:, 256:384], s_w128[:, 2:3],
                                          s_mbdB[:], op0=ALU.mult, op1=ALU.add))
    em.v(lambda e: e.scalar_tensor_tensor(s_mbdB[:], s_mbdm[:, 384:512], s_w128[:, 3:4],
                                          s_mbdA[:], op0=ALU.mult, op1=ALU.add))
    v_mbd = em.v(lambda e: e.tensor_copy(s_mbdT[:], s_mbdB[:]))

    ev = {}
    gidx = 0
    t_kj = {}
    for j in range(NCK):
        wl = [("TSER", t_kj[j - 2])] if j >= 2 else []
        g_wk = em.g_dma(lambda e, j=j: e.dma_start(
            out=s_wks[j % 2][:].rearrange("p (c f) -> p c f", f=128),
            in_=wk.rearrange("(c p) d -> p c d", p=128)[:, :, j * 128:(j + 1) * 128]),
            waits=wl)
        for sblk in range(4):
            pb = p_a[gidx % 4]
            wl = [("GDMA", g_wk)]
            if gidx >= 4:
                wl.append(("SSER", ev[gidx - 4]))
            if gidx == 0:
                wl += [("GDMA", ld_wbf), ("VSER", p1_v), ("TSER", p1_t)]
            for c in range(NCK):
                em.t(lambda e, j=j, c=c, sblk=sblk, pb=pb: e.matmul(
                    pb[0:128, 0:512],
                    s_wks[j % 2][:, c * 128:(c + 1) * 128],
                    s_xnT[c][:, sblk * 512:(sblk + 1) * 512],
                    start=(c == 0), stop=(c == NCK - 1), skip_group_check=True),
                    waits=wl if c == 0 else [], inc=(c == NCK - 1))
            tk = em.val("TSER")
            ev[gidx] = em.s(lambda e, j=j, sblk=sblk, pb=pb: e.copy(
                s_kT[j][:, sblk * 512:(sblk + 1) * 512], pb[0:128, 0:512]),
                waits=[("TSER", tk)])
            gidx += 1
        t_kj[j] = em.val("TSER")
    t_qj = {}
    for j in range(NCK):
        wl = [("TSER", t_qj[j - 2])] if j >= 2 else []
        g_wq = em.g_dma(lambda e, j=j: e.dma_start(
            out=s_wqs[j % 2][:].rearrange("p (c f) -> p c f", f=128),
            in_=wq.rearrange("(c p) d -> p c d", p=128)[:, :, j * 128:(j + 1) * 128]),
            waits=wl)
        for sblk in range(4):
            pb = p_a[gidx % 4]
            wl = [("SSER", ev[gidx - 4]), ("GDMA", g_wq)]
            for c in range(NCK):
                em.t(lambda e, j=j, c=c, sblk=sblk, pb=pb: e.matmul(
                    pb[0:128, 0:512],
                    s_wqs[j % 2][:, c * 128:(c + 1) * 128],
                    s_xnT[c][:, sblk * 512:(sblk + 1) * 512],
                    start=(c == 0), stop=(c == NCK - 1), skip_group_check=True),
                    waits=wl if c == 0 else [], inc=(c == NCK - 1))
            tk = em.val("TSER")
            ev[gidx] = em.s(lambda e, sblk=sblk, pb=pb: e.copy(
                s_qraw[:, sblk * 512:(sblk + 1) * 512], pb[0:128, 0:512]),
                waits=[("TSER", tk)])
            gidx += 1
        vq = ev[gidx - 1]
        for sblk in range(4):
            pb = p_a[gidx % 4]
            wl = [("SSER", ev[gidx - 4]), ("SSER", vq)]
            if j == 0 and sblk == 0:
                wl.append(("VSER", v_mbd))
            em.t(lambda e, sblk=sblk, pb=pb: e.matmul(
                pb[0:128, 0:512], s_mbdT[:], s_qraw[:, sblk * 512:(sblk + 1) * 512],
                start=True, stop=True, skip_group_check=True), waits=wl)
            tk = em.val("TSER")
            ev[gidx] = em.s(lambda e, j=j, sblk=sblk, pb=pb: e.copy(
                s_qT[j][:, sblk * 512:(sblk + 1) * 512], pb[0:128, 0:512]),
                waits=[("TSER", tk)])
            gidx += 1
        t_qj[j] = em.val("TSER")
    t_q_done = em.val("TSER")
    esR1.close()
    s_wvb = sb(esR2, "s_wvb", [128, NCK * D], BF16, side="right")
    g_wv = em.g_dma(lambda e: e.dma_start(
        out=s_wvb[:].rearrange("p (c d) -> p c d", d=D),
        in_=wv.rearrange("(c p) d -> p c d", p=128)),
        waits=[("TSER", t_q_done)])
    vt_ms = [0] * NT
    for i in range(NT):
        vt_ms[i] = em.v(lambda e, i=i: e.memset(s_vtok[i][:], 1.0))
    for i in range(NT):
        for half in range(2):
            pb = p_a[gidx % 4]
            wl = [("SSER", ev[gidx - 4]), ("VSER", vt_ms[i])]
            if i == 0 and half == 0:
                wl.append(("GDMA", g_wv))
            for c in range(NCK):
                em.t(lambda e, i=i, c=c, half=half, pb=pb: e.matmul(
                    pb[0:128, 0:512],
                    s_xnT[c][:, i * 128:(i + 1) * 128],
                    s_wvb[:, c * D + half * 512: c * D + (half + 1) * 512],
                    start=(c == 0), stop=(c == NCK - 1), skip_group_check=True),
                    waits=wl if c == 0 else [], inc=(c == NCK - 1))
            tk = em.val("TSER")
            dst3 = s_vtok[i][:].rearrange("p (h x) -> p h x", x=DH + 1)[:, half * 8:(half + 1) * 8, 0:DH]
            src3 = pb[0:128, 0:512].rearrange("p (h x) -> p h x", x=DH)
            ev[gidx] = em.s(lambda e, dst3=dst3, src3=src3: e.copy(dst3, src3),
                            waits=[("TSER", tk)])
            gidx += 1
    p3_v = em.val("VSER")
    p3_s2 = em.val("SSER")
    p3_t = em.val("TSER")
    esR2.close()
    esR0.close()
    esL2.close()
    st_wc = {}
    for fc in range(32):
        wl = [("SDMA", st_wc[fc - 2])] if fc >= 2 else []
        gld = em.g_dma(lambda e, fc=fc: e.dma_start(
            out=s_wc[fc % 2][:].rearrange("p (c f) -> p c f", f=128),
            in_=w1e.rearrange("(c p) f -> p c f", p=128)[:, :, fc * 128:(fc + 1) * 128]),
            waits=wl)
        st_wc[fc] = em.sy(lambda e, fc=fc: e.dma_start(
            out=w1bf_d.ap()[fc, :, :], in_=s_wc[fc % 2][:]), waits=[("GDMA", gld)])
    for fc in range(32):
        wl = [("SDMA", st_wc[(32, fc - 2)])] if fc >= 2 else [("SDMA", st_wc[31])]
        gld = em.g_dma(lambda e, fc=fc: e.dma_start(
            out=s_wc[fc % 2][:], in_=w2e[fc * 128:(fc + 1) * 128, :]),
            waits=wl)
        st_wc[(32, fc)] = em.sy(lambda e, fc=fc: e.dma_start(
            out=w2bf_d.ap()[fc, :, :], in_=s_wc[fc % 2][:]), waits=[("GDMA", gld)])
    st_wcast = em.val("SDMA")

    s_bufT = [sb(esR3, f"s_bufT{c}", [128, CAP], BF16, side="right") for c in range(NCK)]
    s_xg = [sb(esR4, f"s_xg{i}", [128, D], side="right") for i in range(2)]
    s_xgn = [sb(esR4, f"s_xgn{i}", [128, D], BF16, side="right") for i in range(3)]

    em.wait("gpsimd", "CC2", 1)
    em.v(lambda e: e.tensor_scalar(s_cid20[:], s_cid128[:], float(NG), None, ALU.mult))
    em.v(lambda e: e.tensor_scalar(s_tmp[:, 0:1], s_pcol[:], s_cid20[:, 0:1],
                                   None, ALU.add))
    em.v(lambda e: e.tensor_scalar(s_tmp[:, 1:2], s_pcol[:], float(NG), None, ALU.is_ge))
    em.v(lambda e: e.scalar_tensor_tensor(s_tmp[:, 2:3], s_tmp[:, 1:2], 1.0e6,
                                          s_tmp[:, 0:1], op0=ALU.mult, op1=ALU.add))
    em.v(lambda e: e.tensor_copy(s_segidx[:], s_tmp[:, 2:3]))
    v_seg = em.val("VSER")
    gzw = em.g_c(lambda e: e.memset(s_eL[:], 0.0), waits=[("VSER", v_seg)])
    g_seg = em.g_dma(lambda e: e.indirect_dma_start(
        out=s_eL[:], out_offset=None,
        in_=srca_d.ap().rearrange("(a b) c -> a (b c)", b=128),
        in_offset=IndirectOffsetOnAxis(s_segidx[:, 0:1], 0),
        bounds_check=regs["seg"], oob_is_err=False),
        waits=[("GSER", gzw)])
    t_seg = em.t(lambda e: e.transpose(p_a[4][0:128, 0:128], s_eL[:], s_ident[:]),
                 waits=[("GDMA", g_seg), ("TSER", p3_t)])
    em.v(lambda e: e.tensor_copy(s_segT[:, 0:32], p_a[4][0:128, 0:32]),
         waits=[("TSER", t_seg)])
    em.v(lambda e: e.tensor_scalar(s_tmp[:, 0:NG], s_segT[:, 0:NG], 0.0, None,
                                   ALU.is_equal))
    em.v(lambda e: e.scalar_tensor_tensor(s_tmp[:, 32:32 + NG], s_tmp[:, 0:NG], 1.0e7,
                                          s_segT[:, 0:NG], op0=ALU.mult, op1=ALU.add))
    em.v(lambda e: e.tensor_scalar(s_tmp[:, 32:32 + NG], s_tmp[:, 32:32 + NG], -1.0,
                                   None, ALU.add))
    em.v(lambda e: e.tensor_copy(s_srcidx[:, 0:NG], s_tmp[:, 32:32 + NG]))
    v_srcidx = em.val("VSER")

    v_buf = {}
    gz = {}

    def emit_p4_group(g):
        gz[g] = em.g_c(lambda e, g=g: e.memset(s_xg[g % 2][:], 0.0),
                       waits=([("VSER", v_buf[g - 2])] if g >= 2 else [("VSER", v_srcidx)]))
        g_gath = em.g_dma(lambda e, g=g: e.indirect_dma_start(
            out=s_xg[g % 2][:], out_offset=None,
            in_=x_full, in_offset=IndirectOffsetOnAxis(s_srcidx[:, g:g + 1], 0),
            bounds_check=regs["tok"], oob_is_err=False),
            waits=[("GSER", gz[g])])
        xgn = s_xgn[g % 3]
        em.v(lambda e, g=g: e.bn_stats(s_bn6[:, 0:6], s_xg[g % 2][:, 0:512]),
             waits=[("GDMA", g_gath)])
        em.v(lambda e, g=g: e.bn_stats(s_bn6[:, 6:12], s_xg[g % 2][:, 512:1024]))
        em.v(lambda e: e.bn_aggr(s_mvg[:], s_bn6[:]))
        em.v(lambda e: e.tensor_scalar(s_veps[:], s_mvg[:, 1:2], EPS, None, ALU.add))
        em.s(lambda e: e.activation(s_sqv[:], s_veps[:], AF.Sqrt),
             waits=[("VSER", em.val("VSER"))])
        em.v(lambda e: e.reciprocal(s_rstg[:], s_sqv[:]), waits=[("SSER", em.val("SSER"))])
        wl = []
        if g >= 3:
            wl.append(("SDMA", v_buf[("t", g - 3)]))
        em.v(lambda e, g=g, xgn=xgn: e.tensor_scalar(
            xgn[:], s_xg[g % 2][:], s_mvg[:, 0:1], s_rstg[:, 0:1], ALU.subtract, ALU.mult),
            waits=wl)
        v_apply_g = em.val("VSER")
        for c in range(NCK):
            em.sy(lambda e, xgn=xgn, c=c, g=g: e.dma_start(
                out=s_bufT[c][:, g * 128:(g + 1) * 128],
                in_=xgn[:, c * 128:(c + 1) * 128], transpose=True),
                waits=[("VSER", v_apply_g)] if c == 0 else [])
        v_buf[g] = em.val("VSER")
        v_buf[("t", g)] = em.val("SDMA")

    ctxN = s_qT
    last_exp = {0: 0, 1: 0}
    last_avmm = {}
    v_norm_unit = {}
    u = 0
    for p in range(NPAIR):
        hA, hB = 2 * p, 2 * p + 1
        for sq in range(4):
            setA = p_a[4 + 2 * (u % 2)]
            setB = p_a[5 + 2 * (u % 2)]
            exp_done = [0] * NT
            for i in range(NT + 1):
                if i < NT:
                    par = i % 2
                    wl = [("SSER", last_exp[par])]
                    if u == 0 and i == 0:
                        wl += [("VSER", p3_v), ("SSER", p3_s2)]
                    if i == 0 and u >= 1:
                        wl.append(("VSER", v_norm_unit[u - 1]))
                    if i == 0 and u >= 2:
                        wl.append(("VSER", v_norm_unit[u - 2]))
                    em.t(lambda e, p=p, i=i, sq=sq, par=par: e.matmul(
                        p_a[par][0:128, 0:512],
                        s_kT[p][0:64, i * 128:(i + 1) * 128],
                        s_qT[p][0:64, sq * 512:(sq + 1) * 512],
                        start=True, stop=True, skip_group_check=True,
                        tile_position=(0, 0)), waits=wl, inc=False)
                    tk = em.t(lambda e, p=p, i=i, sq=sq, par=par: e.matmul(
                        p_a[2 + par][0:128, 0:512],
                        s_kT[p][64:128, i * 128:(i + 1) * 128],
                        s_qT[p][64:128, sq * 512:(sq + 1) * 512],
                        start=True, stop=True, skip_group_check=True,
                        tile_position=(64, 0)), inc=True)
                    wl = [("TSER", tk)]
                    if i >= 2:
                        wl.append(("TSER", last_avmm[(u, i - 2)]))
                    em.s(lambda e, par=par: e.activation(
                        s_P[par][:], p_a[par][0:128, 0:512], AF.Exp, scale=SCALE),
                        waits=wl, chain=False)
                    em.s(lambda e, par=par: e.activation(
                        s_P[2 + par][:], p_a[2 + par][0:128, 0:512], AF.Exp, scale=SCALE),
                        chain=False)
                    exp_done[i] = em.val("SSER")
                    last_exp[par] = exp_done[i]
                if i >= 1:
                    ii = i - 1
                    par = ii % 2
                    wl = [("SSER", exp_done[ii])]
                    em.t(lambda e, ii=ii, hA=hA, par=par, setA=setA: e.matmul(
                        setA[0:65, 0:512],
                        s_vtok[ii][:, hA * 65:hA * 65 + 65],
                        s_P[par][:],
                        start=(ii == 0), stop=(ii == NT - 1), skip_group_check=True),
                        waits=wl, inc=False)
                    last_avmm[(u, ii)] = em.t(
                        lambda e, ii=ii, hB=hB, par=par, setB=setB: e.matmul(
                            setB[0:65, 0:512],
                            s_vtok[ii][:, hB * 65:hB * 65 + 65],
                            s_P[2 + par][:],
                            start=(ii == 0), stop=(ii == NT - 1), skip_group_check=True),
                        inc=True)
            t_av = em.val("TSER")
            em.v(lambda e, setA=setA: e.reciprocal(s_recA[:], setA[64:65, 0:512]),
                 waits=[("TSER", t_av)])
            v_den = em.v(lambda e, setB=setB: e.reciprocal(s_recB[:], setB[64:65, 0:512]))
            wl = [("VSER", v_den), ("SSER", last_exp[0])]
            em.t(lambda e: e.matmul(
                p_a[0][0:128, 0:512], s_sello[:], s_recA[:],
                start=True, stop=False, skip_group_check=True), waits=wl, inc=False)
            t_bc = em.t(lambda e: e.matmul(
                p_a[0][0:128, 0:512], s_selhi[:], s_recB[:],
                start=False, stop=True, skip_group_check=True), inc=True)
            em.v(lambda e, setA=setA: e.tensor_copy(s_stage[0:64, :], setA[0:64, 0:512]),
                 waits=[("TSER", t_av)])
            em.v(lambda e, setB=setB: e.tensor_copy(s_stage[64:128, :], setB[0:64, 0:512]))
            v_norm_unit[u] = em.v(lambda e, p=p, sq=sq: e.tensor_tensor(
                ctxN[p][:, sq * 512:(sq + 1) * 512], s_stage[:], p_a[0][0:128, 0:512],
                op=ALU.mult), waits=[("TSER", t_bc)])
            if u < NG:
                emit_p4_group(u)
            u += 1
    p4_v = em.val("VSER")
    p4_st = em.val("SDMA")
    esR4.close()
    p5_v = em.val("VSER")
    p5_s = em.val("SSER")
    p5_t = em.val("TSER")
    esL1.close()

    s_hT = [sb(esR5, f"s_hT{f}", [128, 512], BF16, side="right") for f in range(32)]
    s_w1f = [sb(esR5, f"s_w1f{i}", [128, 2 * D], BF16, side="right") for i in range(2)]
    s_w2f = [sb(esR5, f"s_w2f{i}", [128, 2 * D], BF16, side="right") for i in range(2)]
    s_fout = [sb(esR5, f"s_fout{i}", [128, D], BF16, side="right") for i in range(4)]

    mm2_ev = 0
    t_mm2_prev = 0
    g_w1ld = {}
    s_gelu = {}
    st_contrib_blk = {}
    for blk in range(5):
        for fc in range(32):
            if fc % 2 == 0:
                wl = [("SDMA", st_wcast)]
                prev = (blk * 32 + fc) - 4
                if prev >= 0:
                    wl.append(("TSER", g_w1ld.get(("mm", prev + 1), 0)))
                g_w1ld[("ld", blk * 32 + fc)] = em.g_dma(
                    lambda e, fc=fc: e.dma_start(
                        out=s_w1f[(fc // 2) % 2][:].rearrange("p (a d) -> p a d", a=2),
                        in_=w1bf_d.ap()[fc:fc + 2, :, :].rearrange("a p d -> p a d")),
                    waits=wl)
            pb = p_a[6 + (fc % 2)]
            wl = [("GDMA", g_w1ld[("ld", blk * 32 + fc - (fc % 2))])]
            if blk == 0 and fc in (0, 1):
                wl += [("VSER", p4_v), ("SDMA", p4_st), ("VSER", p5_v), ("SSER", p5_s)]
            if blk * 32 + fc >= 2:
                wl.append(("SSER", s_gelu[blk * 32 + fc - 2]))
            if blk > 0 and fc in (0, 1):
                wl.append(("VSER", mm2_ev))
            for c in range(NCK):
                em.t(lambda e, c=c, fc=fc, blk=blk, pb=pb: e.matmul(
                    pb[0:128, 0:512],
                    s_w1f[(fc // 2) % 2][:, (fc % 2) * D + c * 128:(fc % 2) * D + (c + 1) * 128],
                    s_bufT[c][:, blk * 512:(blk + 1) * 512],
                    start=(c == 0), stop=(c == NCK - 1), skip_group_check=True),
                    waits=wl if c == 0 else [], inc=(c == NCK - 1))
            tk = em.val("TSER")
            g_w1ld[("mm", blk * 32 + fc)] = tk
            gwl = [("TSER", tk)]
            if blk == 0 and fc == 0:
                gwl.append(("VSER", v_b1T))
            s_gelu[blk * 32 + fc] = em.s(lambda e, fc=fc, pb=pb: e.activation(
                s_hT[fc][:], pb[0:128, 0:512], AF.Gelu, bias=s_b1T[:, fc:fc + 1]),
                waits=gwl, chain=False)
        if dbg and blk == 2:
            em.sy(lambda e: e.dma_start(out=dbg_o["d_h0"], in_=s_hT[0][:]),
                  waits=[("SSER", s_gelu[blk * 32 + 0])])
            em.sy(lambda e: e.dma_start(out=dbg_o["d_w1f0"], in_=s_w1f[0][:]),
                  waits=[("SSER", s_gelu[blk * 32 + 0]), ("SSER", s_gelu[blk * 32 + 1])])
            em.sy(lambda e: e.dma_start(out=dbg_o["d_buf0"], in_=s_bufT[0][:, blk * 512:(blk + 1) * 512]),
                  waits=[("SSER", s_gelu[blk * 32 + 0])])
        s_gelu_all = em.val("SSER")
        for fc in range(32):
            if fc % 2 == 0:
                w2wl = [("SDMA", st_wcast)]
                if fc >= 4:
                    w2wl.append(("TSER", g_w1ld.get(("mm2", blk, fc - 3), 0)))
                elif blk > 0:
                    w2wl.append(("TSER", t_mm2_prev))
                g_w2ld = em.g_dma(lambda e, fc=fc: e.dma_start(
                    out=s_w2f[(fc // 2) % 2][:].rearrange("p (a d) -> p a d", a=2),
                    in_=w2bf_d.ap()[fc:fc + 2, :, :].rearrange("a p d -> p a d")),
                    waits=w2wl)
            for ct in range(4):
                for half in range(2):
                    pb = p_a[2 * ct + half]
                    wl = []
                    if fc == 0 and ct == 0 and half == 0:
                        wl = [("GDMA", g_w2ld), ("SSER", s_gelu_all)]
                        if blk > 0:
                            wl.append(("VSER", mm2_ev))
                        else:
                            wl.append(("VSER", p5_v))
                    elif ct == 0 and half == 0:
                        wl = [("GDMA", g_w2ld)]
                    em.t(lambda e, fc=fc, ct=ct, half=half, pb=pb: e.matmul(
                        pb[0:128, 0:512],
                        s_hT[fc][:, ct * 128:(ct + 1) * 128],
                        s_w2f[(fc // 2) % 2][:, (fc % 2) * D + half * 512:(fc % 2) * D + (half + 1) * 512],
                        start=(fc == 0), stop=False, skip_group_check=True),
                        waits=wl, inc=(ct == 3 and half == 1))
            g_w1ld[("mm2", blk, fc)] = em.val("TSER")
        for ct in range(4):
            for half in range(2):
                pb = p_a[2 * ct + half]
                em.t(lambda e, ct=ct, half=half, pb=pb: e.matmul(
                    pb[0:128, 0:512], s_onesb[0:1, 0:128],
                    s_b2[0:1, half * 512:(half + 1) * 512],
                    start=False, stop=True, skip_group_check=True),
                    inc=(ct == 3 and half == 1))
        t_mm2 = em.val("TSER")
        t_mm2_prev = t_mm2
        for ct in range(4):
            for half in range(2):
                wl = [("TSER", t_mm2)] if (ct == 0 and half == 0) else []
                if blk >= 1 and ct == 0 and half == 0:
                    wl.append(("SDMA", st_contrib_blk[blk - 1]))
                em.v(lambda e, ct=ct, half=half: e.tensor_copy(
                    s_fout[ct][:, half * 512:(half + 1) * 512],
                    p_a[2 * ct + half][0:128, 0:512]),
                    waits=wl)
        mm2_ev = em.val("VSER")
        if dbg and blk == 2:
            for ct in range(4):
                em.sy(lambda e, ct=ct: e.dma_start(
                    out=dbg_o["d_fout"][ct * 128:(ct + 1) * 128, :], in_=s_fout[ct][:]),
                    waits=[("VSER", mm2_ev)] if ct == 0 else [])
            em.sy(lambda e: e.dma_start(out=dbg_o["d_w2f"], in_=s_w2f[1][:]),
                  waits=[("VSER", mm2_ev)])
        for ct in range(4):
            g = blk * 4 + ct
            em.sy(lambda e, ct=ct, g=g: e.dma_start(
                out=contrib_d.ap()[g * 128:(g + 1) * 128, :], in_=s_fout[ct][:]),
                waits=[("VSER", mm2_ev)] if ct == 0 else [])
        st_contrib_blk[blk] = em.val("SDMA")
    st_contrib = em.val("SDMA")
    for blk in range(5):
        em.g_cc(lambda e, blk=blk: e.collective_compute(
            "AllGather", ALU.bypass, replica_groups=[list(range(B))],
            ins=[contrib_d.ap()[blk * 512:(blk + 1) * 512, :].opt()],
            outs=[outall_d.ap()[blk * 4096:(blk + 1) * 4096, :].opt()]),
            f"CC3_{blk}", waits=[("SDMA", st_contrib)])
    esR5.close()
    esR3.close()

    s_wob = sb(esR6, "s_wob", [128, NCK * D], BF16, side="right")
    s_og1 = [sb(esR6, f"s_og1{i}", [128, D], BF16, side="right") for i in range(2)]
    s_og2 = [sb(esR6, f"s_og2{i}", [128, D], BF16, side="right") for i in range(2)]
    s_yf = [sb(esR6, f"s_yf{i}", [128, D], side="right") for i in range(2)]
    s_xy = [sb(esR6, f"s_xy{i}", [128, D], side="right") for i in range(2)]
    s_out = [sb(esR6, f"s_out{i}", [128, D], side="right") for i in range(2)]

    g_wo = em.g_dma(lambda e: e.dma_start(
        out=s_wob[:].rearrange("p (c d) -> p c d", d=D),
        in_=wo.rearrange("(c p) d -> p c d", p=128)))

    st_tile = {}
    op_ev = {}
    cm = {}
    ccw_done = False
    for i in range(NT):
        pb0 = p_a[4 + 2 * (i % 2)]
        pb1 = p_a[5 + 2 * (i % 2)]
        wl = [("GDMA", g_wo)]
        if i >= 2:
            wl.append(("VSER", op_ev[i - 2]))
        else:
            wl.append(("VSER", mm2_ev))
            wl.append(("VSER", p5_v))
        for c in range(NCK):
            em.t(lambda e, i=i, c=c, pb0=pb0: e.matmul(
                pb0[0:128, 0:512], ctxN[c][:, i * 128:(i + 1) * 128],
                s_wob[:, c * D: c * D + 512],
                start=(c == 0), stop=(c == NCK - 1), skip_group_check=True),
                waits=wl if c == 0 else [], inc=False)
            em.t(lambda e, i=i, c=c, pb1=pb1: e.matmul(
                pb1[0:128, 0:512], ctxN[c][:, i * 128:(i + 1) * 128],
                s_wob[:, c * D + 512: c * D + 1024],
                start=(c == 0), stop=(c == NCK - 1), skip_group_check=True),
                inc=(c == NCK - 1))
        t_op = em.val("TSER")
        gz1 = em.g_c(lambda e, i=i: e.memset(s_og1[i % 2][:], 0.0),
                     waits=[("VSER", cm[i - 2])] if i >= 2 else [])
        gz2 = em.g_c(lambda e, i=i: e.memset(s_og2[i % 2][:], 0.0))
        if not ccw_done:
            for bb in range(5):
                em.wait("gpsimd", f"CC3_{bb}", 1)
            ccw_done = True
        em.g_dma(lambda e, i=i: e.indirect_dma_start(
            out=s_og1[i % 2][:], out_offset=None, in_=outall_d.ap(),
            in_offset=IndirectOffsetOnAxis(s_r1u[:, i:i + 1], 0),
            bounds_check=regs["slots"], oob_is_err=False),
            waits=[("GSER", gz2), ("VSER", v_slots)])
        gog = em.g_dma(lambda e, i=i: e.indirect_dma_start(
            out=s_og2[i % 2][:], out_offset=None, in_=outall_d.ap(),
            in_offset=IndirectOffsetOnAxis(s_r2u[:, i:i + 1], 0),
            bounds_check=regs["slots"], oob_is_err=False))
        xl = em.sy(lambda e, i=i: e.dma_start(
            out=s_xt[i % 2][:], in_=x_own[i * 128:(i + 1) * 128, :]),
            waits=[("VSER", cm[i - 2]) if i >= 2 else ("VSER", p1_v)])
        em.v(lambda e, i=i: e.tensor_scalar(
            s_yf[i % 2][:], s_og1[i % 2][:], s_s1[:, i:i + 1], None, ALU.mult),
            waits=[("GDMA", gog)])
        em.v(lambda e, i=i: e.scalar_tensor_tensor(
            s_yf[i % 2][:], s_og2[i % 2][:], s_s2[:, i:i + 1], s_yf[i % 2][:],
            op0=ALU.mult, op1=ALU.add))
        em.v(lambda e, i=i: e.tensor_tensor(
            s_xy[i % 2][:], s_xt[i % 2][:], s_yf[i % 2][:], op=ALU.add),
            waits=[("SDMA", xl)])
        wl = [("TSER", t_op)]
        if i >= 2:
            wl.append(("SDMA", st_tile[i - 2]))
        em.v(lambda e, i=i, pb0=pb0: e.tensor_tensor(
            s_out[i % 2][:, 0:512], pb0[0:128, 0:512], s_xy[i % 2][:, 0:512],
            op=ALU.add), waits=wl)
        op_ev[i] = em.v(lambda e, i=i, pb1=pb1: e.tensor_tensor(
            s_out[i % 2][:, 512:1024], pb1[0:128, 0:512], s_xy[i % 2][:, 512:1024],
            op=ALU.add))
        cm[i] = em.val("VSER")
        st_tile[i] = em.sy(lambda e, i=i: e.dma_start(
            out=y_out[i * 128:(i + 1) * 128, :], in_=s_out[i % 2][:]),
            waits=[("VSER", cm[i])])
    em.wait("sync", "SDMA", st_tile[NT - 1])
    em.wait("gpsimd", "GDMA", em.val("GDMA"))

    if dbg:
        snap = em.snap() if hasattr(em, "snap") else dict(em.cnt)
        dmap = [
            ("d_e1", s_e1[:]), ("d_e2", s_e2[:]),
            ("d_lp1", s_lp1[:]), ("d_lp2", s_lp2[:]), ("d_g1", s_g1[:]),
            ("d_g2", s_g2[:]), ("d_sl1", s_sl1[:]), ("d_sl2", s_sl2[:]),
            ("d_G", s_G[:]), ("d_offr", s_offr[:]),
            ("d_src", s_segT[:]),
            ("d_qT0", s_qT[0][:]),
            ("d_yf0", s_yf[0][:]), ("d_mv0", s_mv[0][:]),
            ("d_gp1", s_gp1[:]), ("d_gp2", s_gp2[:]),
            ("d_contrib", contrib_d.ap()), ("d_outall", outall_d.ap()[0:4096, :]),
            ("d_w1bf", w1bf_d.ap().rearrange("a b c -> (a b) c")),
            ("d_r1u", s_r1u[:]), ("d_r2u", s_r2u[:]),
            ("d_k1", s_k1[:]), ("d_k2", s_k2[:]),
        ]
        for name, src in dmap:
            em.sy(lambda e, name=name, src=src: e.dma_start(out=dbg_o[name], in_=src),
                  waits=[("VSER", snap.get("VSER", 0)), ("SSER", snap.get("SSER", 0))])
        em.wait("sync", "SDMA", em.val("SDMA"))

    with nc.Block() as block:
        for eng in ENGS:
            def mk(eng):
                def body(e, _ops=em.ops[eng]):
                    for f in _ops:
                        f(e)
                return body
            getattr(block, eng)(mk(eng))

    esR6.close()
    es.close()
    return nc


def _stage_inputs(inputs):
    x = np.ascontiguousarray(np.asarray(inputs["L0_attn_x"], dtype=np.float32))
    wq = np.ascontiguousarray(np.asarray(inputs["L0_attn_Wq"], dtype=np.float32))
    wk = np.ascontiguousarray(np.asarray(inputs["L0_attn_Wk"], dtype=np.float32))
    wv = np.ascontiguousarray(np.asarray(inputs["L0_attn_Wv"], dtype=np.float32))
    wo = np.ascontiguousarray(np.asarray(inputs["L0_attn_Wo"], dtype=np.float32))
    qw = np.ascontiguousarray(np.asarray(inputs["L0_attn_quadrant_weights"],
                                         dtype=np.float32).reshape(1, 4))
    wr = np.ascontiguousarray(np.asarray(inputs["L0_ffn_w_router"], dtype=np.float32))
    br = np.ascontiguousarray(np.asarray(inputs["L0_ffn_b_router"],
                                         dtype=np.float32).reshape(1, E))
    w1 = np.ascontiguousarray(np.asarray(inputs["L0_ffn_w1"], dtype=np.float32))
    b1 = np.ascontiguousarray(np.asarray(inputs["L0_ffn_b1"], dtype=np.float32))
    w2 = np.ascontiguousarray(np.asarray(inputs["L0_ffn_w2"], dtype=np.float32))
    b2 = np.ascontiguousarray(np.asarray(inputs["L0_ffn_b2"], dtype=np.float32))

    x_full = np.ascontiguousarray(x.reshape(NTOK, D))
    ident = np.eye(128, dtype=np.float32)
    ustrict = np.triu(np.ones((128, 128), np.float32), k=1)
    m16p = np.zeros((128, 128), np.float32)
    for tp in range(NT):
        for ee in range(8):
            for tt in range(NT):
                if tp < tt:
                    m16p[tp * 8 + ee, tt * 8 + ee] = 1.0
    mbdm = np.zeros((4, 128, 128), np.float32)
    for h in range(2):
        base = 64 * h
        for j in range(32):
            mbdm[0][base + j, base + j] = 1.0
            mbdm[1][base + j, base + 32 + j] = 1.0
            mbdm[2][base + 32 + j, base + j] = 1.0
            mbdm[3][base + 32 + j, base + 32 + j] = 1.0
    mbdm = mbdm.reshape(512, 128)
    selmask = np.zeros((2, 128), np.float32)
    selmask[0, 0:64] = 1.0
    selmask[1, 64:128] = 1.0

    in_maps = []
    for b in range(B):
        maskb = np.zeros((E, 1), np.float32)
        maskb[:b, 0] = 1.0
        in_maps.append(dict(
            x_own=np.ascontiguousarray(x[b]),
            x_full=x_full,
            wq=wq, wk=wk, wv=wv, wo=wo, qw=qw, wr=wr, br=br,
            w1e=np.ascontiguousarray(w1[b]),
            b1e=np.ascontiguousarray(b1[b].reshape(1, FF)),
            w2e=np.ascontiguousarray(w2[b]),
            b2e=np.ascontiguousarray(b2[b].reshape(1, D)),
            cid=np.array([[float(b)]], np.float32),
            maskb=maskb,
            ident=ident, ustrict=ustrict, m16p=m16p, mbdm=mbdm,
            selmask=selmask,
        ))
    return in_maps


def kernel(**inputs):
    in_maps = _stage_inputs(inputs)
    nc = build(dbg=False)
    res = run_bass_kernel_spmd(nc, in_maps, core_ids=list(range(B)))
    y = np.stack([np.asarray(r["y"], dtype=np.float32) for r in res.results], axis=0)
    return y


# version 13
# speedup vs baseline: 1.2675x; 1.0070x over previous
import numpy as np
import ml_dtypes
from contextlib import ExitStack
import concourse.bass as bass
import concourse.mybir as mybir
from concourse.bass import IndirectOffsetOnAxis
from concourse.bass_utils import run_bass_kernel_spmd

F32 = mybir.dt.float32
BF16 = mybir.dt.bfloat16
U32 = mybir.dt.uint32
AF = mybir.ActivationFunctionType
ALU = mybir.AluOpType
AX = mybir.AxisListType

B = 8
T = 2048
D = 1024
H = 16
DH = 64
NPAIR = 8
E = 8
FF = 4096
CAP = 2560
NT = 16
NCK = 8
NG = 20
NTOK = B * T
SCALE = float(1.0 / np.sqrt(32.0))
EPS = 1e-5

ENGS = ("sync", "gpsimd", "tensor", "scalar", "vector")


class Em:
    def __init__(self):
        self.ops = {k: [] for k in ENGS}
        self.cnt = {}
        self.sems = {}

    def val(self, sem):
        return self.cnt.get(sem, 0)

    def wait(self, eng, sem, v=None):
        if v is None:
            v = self.val(sem)
        if v <= 0:
            return
        self.ops[eng].append(lambda e, s=sem, vv=v: e.wait_ge(self.sems[s], vv))

    def op(self, eng, fn, inc=None, amt=1):
        if inc is not None:
            self.cnt[inc] = self.cnt.get(inc, 0) + amt
            self.ops[eng].append(
                lambda e, f=fn, s=inc, a=amt: f(e).then_inc(self.sems[s], a))
        else:
            self.ops[eng].append(lambda e, f=fn: f(e))

    def v(self, fn, waits=()):
        self.wait("vector", "VSER")
        for s, vv in waits:
            self.wait("vector", s, vv)
        self.op("vector", fn, inc="VSER")
        return self.val("VSER")

    def s(self, fn, waits=(), chain=True):
        if chain:
            self.wait("scalar", "SSER")
        for s, vv in waits:
            self.wait("scalar", s, vv)
        self.op("scalar", fn, inc="SSER")
        return self.val("SSER")

    def t(self, fn, waits=(), inc=True):
        for s, vv in waits:
            self.wait("tensor", s, vv)
        self.op("tensor", fn, inc="TSER" if inc else None)
        return self.val("TSER")

    def g_dma(self, fn, waits=()):
        for s, vv in waits:
            self.wait("gpsimd", s, vv)
        self.op("gpsimd", fn, inc="GDMA", amt=16)
        return self.val("GDMA")

    def g_c(self, fn, waits=()):
        self.wait("gpsimd", "GSER")
        for s, vv in waits:
            self.wait("gpsimd", s, vv)
        self.op("gpsimd", fn, inc="GSER")
        return self.val("GSER")

    def g_cc(self, fn, ccsem, waits=()):
        for s, vv in waits:
            self.wait("gpsimd", s, vv)
        self.cnt[ccsem] = self.cnt.get(ccsem, 0) + 1
        self.ops["gpsimd"].append(
            lambda e, f=fn, s=ccsem: f(e).then_inc(self.sems[s]))

    def sy(self, fn, waits=()):
        for s, vv in waits:
            self.wait("sync", s, vv)
        self.op("sync", fn, inc="SDMA", amt=16)
        return self.val("SDMA")


def build(dbg=False):
    nc = bass.Bass("TRN2", target_bir_lowering=False, debug=False)

    def din(name, shape, dt=F32):
        return nc.dram_tensor(name, shape, dt, kind="ExternalInput").ap()

    x_own = din("x_own", [T, D])
    x_full = din("x_full", [NTOK, D])
    wq = din("wq", [D, D])
    wk = din("wk", [D, D])
    wv = din("wv", [D, D])
    wo = din("wo", [D, D])
    qw = din("qw", [1, 4])
    wr = din("wr", [D, E])
    br = din("br", [1, E])
    w1e = din("w1e", [D, FF])
    b1e = din("b1e", [1, FF])
    w2e = din("w2e", [FF, D])
    b2e = din("b2e", [1, D])
    cid = din("cid", [1, 1])
    maskb = din("maskb", [E, 1])
    ident = din("ident", [128, 128])
    ustrict = din("ustrict", [128, 128])
    m16p = din("m16p", [128, 128])
    mbdm = din("mbdm", [512, 128])
    selmask = din("selmask", [2, 128])

    y_out = nc.dram_tensor("y", [T, D], F32, kind="ExternalOutput").ap()

    dbg_o = {}
    if dbg:
        for name, shape, dt in [
            ("d_logT", [E, T], F32), ("d_e1", [128, NT], F32), ("d_e2", [128, NT], F32),
            ("d_lp1", [128, NT], F32), ("d_lp2", [128, NT], F32),
            ("d_g1", [128, NT], F32), ("d_g2", [128, NT], F32),
            ("d_sl1", [128, NT], F32), ("d_sl2", [128, NT], F32),
            ("d_G", [E, E], F32), ("d_offr", [1, E], F32),
            ("d_src", [128, 32], F32),
            ("d_xnT0", [128, T], BF16), ("d_qT0", [128, T], BF16),
            ("d_kT0", [128, T], BF16), ("d_vt0", [128, H * (DH + 1)], BF16),
            ("d_bufT0", [128, CAP], BF16),
            ("d_yf0", [128, D], F32), ("d_mv0", [128, 2], F32),
            ("d_gp1", [128, NT], F32), ("d_gp2", [128, NT], F32),
            ("d_r1u", [128, NT], U32), ("d_r2u", [128, NT], U32),
            ("d_k1", [128, NT], F32), ("d_k2", [128, NT], F32),
            ("d_contrib", [CAP, D], F32), ("d_outall", [4096, D], F32),
            ("d_w1bf", [32 * 128, D], BF16),
            ("d_h0", [128, 512], BF16), ("d_w1f0", [128, D], BF16),
            ("d_buf0", [128, 512], BF16), ("d_fout", [4 * 128, D], F32),
            ("d_w2f", [128, D], BF16),
        ]:
            dbg_o[name] = nc.dram_tensor(name, shape, dt, kind="ExternalOutput").ap()

    g_in_d = nc.dram_tensor("g_in_d", [1, E], F32)
    g_out_d = nc.dram_tensor("g_out_d", [E, E], F32)
    crow_d = nc.dram_tensor("crow_d", [1, E], F32)
    crow2_d = nc.dram_tensor("crow2_d", [1, E], F32)
    offr_d = nc.dram_tensor("offr_d", [1, E], F32)
    srcp_d = nc.dram_tensor("srcp_d", [E * CAP, 1], F32)
    srca_d = nc.dram_tensor("srca_d", [E * CAP, 1], F32)
    contrib_d = nc.dram_tensor("contrib_d", [CAP, D], BF16)
    w1bf_d = nc.dram_tensor("w1bf_d", [32, 128, D], BF16)
    w2bf_d = nc.dram_tensor("w2bf_d", [32, 128, D], BF16)
    outall_d = nc.dram_tensor("outall_d", [E * CAP, D], BF16, addr_space="Shared")

    em = Em()
    es = ExitStack()
    esL1 = ExitStack()
    esL2 = ExitStack()
    esR0 = ExitStack()
    esR1 = ExitStack()
    esR2 = ExitStack()
    esR3 = ExitStack()
    esR4 = ExitStack()
    esR5 = ExitStack()
    esR6 = ExitStack()

    def sb(stk, name, shape, dt=F32, side="left"):
        return stk.enter_context(nc.sbuf_tensor(name, shape, dt, side=side))

    s_ident = sb(es, "s_ident", [128, 128])
    s_ustrict = sb(es, "s_ustrict", [128, 128])
    s_m16p = sb(es, "s_m16p", [128, 128])
    s_mbdm = sb(es, "s_mbdm", [128, 512], BF16)
    s_sello = sb(es, "s_sello", [1, 128])
    s_selhi = sb(es, "s_selhi", [1, 128])
    s_maskb = sb(es, "s_maskb", [E, 1])
    s_cid = sb(es, "s_cid", [1, 1])
    s_qw = sb(es, "s_qw", [1, 4])
    s_brT = sb(es, "s_brT", [E, 1])
    s_ones = sb(es, "s_ones", [128, 128])
    s_wrhi = sb(es, "s_wrhi", [128, NCK * E], BF16)
    s_wrlo = sb(es, "s_wrlo", [128, NCK * E], BF16)
    s_wrf = sb(es, "s_wrf", [128, NCK * E])
    s_b2 = sb(es, "s_b2", [1, D], BF16)
    s_onesb = sb(es, "s_onesb", [1, 128], BF16)
    s_b1r = sb(es, "s_b1r", [32, 128])
    s_b1T = sb(es, "s_b1T", [128, 32])
    s_mv = [sb(es, f"s_mv{i}", [128, 2]) for i in range(NT)]
    s_rstd = [sb(es, f"s_rstd{i}", [128, 1]) for i in range(NT)]
    s_qT = [sb(es, f"s_qT{c}", [128, T], BF16) for c in range(NCK)]
    s_Ltok = sb(es, "s_Ltok", [128, 128])
    s_max1 = sb(es, "s_max1", [128, NT])
    s_max2 = sb(es, "s_max2", [128, NT])
    s_oh1 = sb(es, "s_oh1", [128, 128])
    s_oh2 = sb(es, "s_oh2", [128, 128])
    s_msk = sb(es, "s_msk", [128, 128])
    s_iotae = sb(es, "s_iotae", [128, 128])
    s_tmp = sb(es, "s_tmp", [128, 128])
    s_cnt = sb(es, "s_cnt", [128, 128])
    s_pref = sb(es, "s_pref", [128, 128])
    s_totr = sb(es, "s_totr", [1, 128])
    s_totc = sb(es, "s_totc", [128, 1])
    s_tbc = sb(es, "s_tbc", [128, 1])
    s_tbr = sb(es, "s_tbr", [1, 128])
    s_lb = sb(es, "s_lb", [128, 128])
    s_lp1 = sb(es, "s_lp1", [128, NT])
    s_lp2 = sb(es, "s_lp2", [128, NT])
    s_e1 = sb(es, "s_e1", [128, NT])
    s_e2 = sb(es, "s_e2", [128, NT])
    s_eL = sb(es, "s_eL", [128, 128])
    s_ssum = sb(es, "s_ssum", [128, NT])
    s_rs = sb(es, "s_rs", [128, NT])
    s_em1 = sb(es, "s_em1", [128, NT])
    s_em2 = sb(es, "s_em2", [128, NT])
    s_g1 = sb(es, "s_g1", [128, NT])
    s_g2 = sb(es, "s_g2", [128, NT])
    s_tot8 = sb(es, "s_tot8", [1, E])
    s_G = sb(es, "s_G", [E, E])
    s_offr = sb(es, "s_offr", [1, E])
    s_ie8 = sb(es, "s_ie8", [1, E])
    s_ie512 = sb(es, "s_ie512", [1, E])
    s_crow2 = sb(es, "s_crow2", [1, E])
    s_cbcr2 = sb(es, "s_cbcr2", [1, 128])
    s_cbc2 = sb(es, "s_cbc2", [128, 128])
    s_blk = sb(es, "s_blk", [128, NT])
    s_crow = sb(es, "s_crow", [1, E])
    s_cbcr = sb(es, "s_cbcr", [1, 128])
    s_offbcr = sb(es, "s_offbcr", [1, 128])
    s_cbc = sb(es, "s_cbc", [128, 128])
    s_offbc = sb(es, "s_offbc", [128, 128])
    s_t16a = sb(es, "s_t16a", [128, NT])
    s_t16b = sb(es, "s_t16b", [128, NT])
    s_gp1 = sb(es, "s_gp1", [128, NT])
    s_gp2 = sb(es, "s_gp2", [128, NT])
    s_k1 = sb(es, "s_k1", [128, NT])
    s_k2 = sb(es, "s_k2", [128, NT])
    s_sl1 = sb(es, "s_sl1", [128, NT])
    s_sl2 = sb(es, "s_sl2", [128, NT])
    s_sl1u = sb(es, "s_sl1u", [128, NT], U32)
    s_sl2u = sb(es, "s_sl2u", [128, NT], U32)
    s_r1u = sb(es, "s_r1u", [128, NT], U32)
    s_r2u = sb(es, "s_r2u", [128, NT], U32)
    s_ptok = sb(es, "s_ptok", [128, NT])
    s_cid128 = sb(es, "s_cid128", [128, 1])
    s_cidtok = sb(es, "s_cidtok", [128, 1])
    s_val1 = sb(es, "s_val1", [128, NT])
    s_zero = sb(es, "s_zero", [128, 160])
    s_ws = sb(es, "s_ws", [128, NT])
    s_rn = sb(es, "s_rn", [128, NT])
    s_s1 = sb(es, "s_s1", [128, NT])
    s_s2 = sb(es, "s_s2", [128, NT])
    s_w4 = sb(es, "s_w4", [1, 4])
    s_w4a = sb(es, "s_w4a", [1, 4])
    s_w128 = sb(es, "s_w128", [128, 4])
    s_mbdA = sb(es, "s_mbdA", [128, 128])
    s_mbdB = sb(es, "s_mbdB", [128, 128])
    s_mbdT = sb(es, "s_mbdT", [128, 128], BF16)
    s_sc1 = sb(es, "s_sc1", [1, 1])
    s_sc2 = sb(es, "s_sc2", [1, 1])
    s_recA = sb(es, "s_recA", [1, 512])
    s_recB = sb(es, "s_recB", [1, 512])
    s_lntmp = sb(es, "s_lntmp", [1, 512])
    s_stage = sb(es, "s_stage", [128, 512], BF16)
    s_segidx = sb(es, "s_segidx", [128, 1], U32)
    s_segT = sb(es, "s_segT", [128, 32])
    s_srcidx = sb(es, "s_srcidx", [128, 32], U32)
    s_pcol = sb(es, "s_pcol", [128, 1])
    s_cid20 = sb(es, "s_cid20", [128, 1])
    s_xt = [sb(es, f"s_xt{i}", [128, D]) for i in range(2)]
    s_bn6 = sb(es, "s_bn6", [128, 12])
    s_veps = sb(es, "s_veps", [128, 1])
    s_sqv = sb(es, "s_sqv", [128, 1])
    s_mvg = sb(es, "s_mvg", [128, 2])
    s_rstg = sb(es, "s_rstg", [128, 1])
    s_P = [sb(es, f"s_P{i}", [128, 512], BF16) for i in range(4)]
    s_lgs = [sb(es, f"s_lgs{i}", [E, 128]) for i in range(2)]
    s_wc = [sb(es, f"s_wc{i}", [128, D], BF16) for i in range(2)]
    s_lo = [sb(es, f"s_lo{i}", [128, 128], BF16) for i in range(2)]

    s_kT = [sb(esL1, f"s_kT{c}", [128, T], BF16) for c in range(NCK)]
    s_vtok = [sb(esL1, f"s_vtok{i}", [128, H * (DH + 1)], BF16) for i in range(NT)]
    s_xnT = [sb(esL2, f"s_xnT{c}", [128, T], BF16) for c in range(NCK)]
    s_xn = [sb(esL2, f"s_xn{i}", [128, D]) for i in range(2)]

    s_qraw = sb(esR0, "s_qraw", [128, T], BF16, side="right")
    s_wqs = [sb(esR1, f"s_wqs{i}", [128, D], BF16, side="right") for i in range(2)]
    s_wks = [sb(esR1, f"s_wks{i}", [128, D], BF16, side="right") for i in range(2)]
    s_wvb = None
    s_bufT = None

    p_a = [es.enter_context(nc.psum_tensor(f"p_a{i}", [128, 512], F32))
           for i in range(8)]

    for name in ("SDMA", "GDMA", "GSER", "TSER", "SSER", "VSER",
                 "CC1", "CC2", "CC3_0", "CC3_1", "CC3_2", "CC3_3", "CC3_4"):
        em.sems[name] = es.enter_context(nc.semaphore(name))

    em.sy(lambda e: e.dma_start(out=s_ident[:], in_=ident))
    em.sy(lambda e: e.dma_start(out=s_ustrict[:], in_=ustrict))
    em.sy(lambda e: e.dma_start(out=s_m16p[:], in_=m16p))
    em.sy(lambda e: e.dma_start(out=s_sello[:], in_=selmask[0:1, :]))
    em.sy(lambda e: e.dma_start(out=s_selhi[:], in_=selmask[1:2, :]))
    em.sy(lambda e: e.dma_start(out=s_maskb[:], in_=maskb))
    em.sy(lambda e: e.dma_start(out=s_cid[:], in_=cid))
    em.sy(lambda e: e.dma_start(out=s_qw[:], in_=qw))
    em.sy(lambda e: e.dma_start(out=s_brT[:], in_=br.rearrange("a b -> b a")))
    em.sy(lambda e: e.dma_start(out=s_wrf[:].rearrange("p (c e) -> p c e", e=E), in_=wr.rearrange("(c p) e -> p c e", p=128)))
    em.sy(lambda e: e.dma_start(out=s_b1r[:], in_=b1e.rearrange("a (r c) -> (a r) c", c=128)))
    ld_consts = em.val("SDMA")

    em.g_dma(lambda e: e.dma_start(out=s_wrhi[:].rearrange("p (c e) -> p c e", e=E),
                                   in_=wr.rearrange("(c p) e -> p c e", p=128)))
    em.g_dma(lambda e: e.dma_start(out=s_mbdm[:].rearrange("p (k c) -> p k c", c=128),
                                   in_=mbdm.rearrange("(k p) c -> p k c", p=128)))
    em.g_dma(lambda e: e.dma_start(out=s_b2[:], in_=b2e))
    ld_wbf = em.val("GDMA")

    regs = {}
    def _mkregs(e):
        regs["slots"] = e.to_reg(E * CAP - 1)
        regs["tok"] = e.to_reg(NTOK - 1)
        regs["seg"] = e.to_reg(E * NG - 1)
    em.op("gpsimd", _mkregs, inc=None)
    em.g_c(lambda e: e.memset(s_ones[:], 1.0))
    em.g_c(lambda e: e.memset(s_onesb[:], 1.0))
    em.g_c(lambda e: e.iota(s_iotae[:], pattern=[[0, 16], [1, 8]], base=0,
                            channel_multiplier=0, allow_small_or_imprecise_dtypes=True))
    em.g_c(lambda e: e.iota(s_ptok[:], pattern=[[128, NT]], base=0,
                            channel_multiplier=1, allow_small_or_imprecise_dtypes=True))
    em.g_c(lambda e: e.iota(s_pcol[:], pattern=[[0, 1]], base=0,
                            channel_multiplier=1, allow_small_or_imprecise_dtypes=True))
    em.g_c(lambda e: e.iota(s_ie8[:], pattern=[[CAP, E]], base=0,
                            channel_multiplier=0, allow_small_or_imprecise_dtypes=True))
    em.g_c(lambda e: e.iota(s_ie512[:], pattern=[[512, E]], base=0,
                            channel_multiplier=0, allow_small_or_imprecise_dtypes=True))
    em.g_c(lambda e: e.memset(s_zero[:], 0.0))
    g_consts = em.val("GSER")
    em.g_dma(lambda e: e.dma_start(
        out=srcp_d.ap().rearrange("(p f) a -> p (f a)", p=128), in_=s_zero[:]),
        waits=[("GSER", g_consts)])
    zr_done = em.val("GDMA")

    em.v(lambda e: e.tensor_tensor(s_wrlo[:], s_wrf[:], s_wrhi[:], op=ALU.subtract),
         waits=[("GDMA", ld_wbf), ("SDMA", ld_consts)])
    em.t(lambda e: e.transpose(p_a[0][0:128, 0:32], s_b1r[:], s_ident[0:32, 0:32]),
         waits=[("SDMA", ld_consts)])
    v_b1T = em.v(lambda e: e.tensor_copy(s_b1T[:], p_a[0][0:128, 0:32]),
                 waits=[("TSER", em.val("TSER"))])
    t10 = em.t(lambda e: e.matmul(p_a[6][0:128, 0:1], s_ones[0:1, :], s_cid[:],
                                  start=True, stop=True, skip_group_check=True),
               waits=[("SDMA", ld_consts), ("GSER", g_consts)])
    em.v(lambda e: e.tensor_copy(s_cid128[:], p_a[6][0:128, 0:1]), waits=[("TSER", t10)])
    em.v(lambda e: e.tensor_scalar(s_cidtok[:], s_cid128[:], float(T), 1.0,
                                   ALU.mult, ALU.add))
    v_cid = em.val("VSER")

    v_apply = [0] * NT
    t_tile = [0] * NT
    ev_s = {}
    ev_lo = {}
    t_rmm = {}
    xload = [0] * NT
    for i in range(NT):
        wl = []
        if i >= 2:
            wl.append(("VSER", v_apply[i - 2]))
        xload[i] = em.sy(lambda e, i=i: e.dma_start(
            out=s_xt[i % 2][:], in_=x_own[i * 128:(i + 1) * 128, :]), waits=wl)
        xt = s_xt[i % 2]
        xn = s_xn[i % 2]
        em.v(lambda e, xt=xt: e.bn_stats(s_bn6[:, 0:6], xt[:, 0:512]),
             waits=[("SDMA", xload[i])])
        em.v(lambda e, xt=xt: e.bn_stats(s_bn6[:, 6:12], xt[:, 512:1024]))
        em.v(lambda e, i=i: e.bn_aggr(s_mv[i][:], s_bn6[:]))
        em.v(lambda e, i=i: e.tensor_scalar(s_veps[:], s_mv[i][:, 1:2], EPS, None, ALU.add))
        em.s(lambda e: e.activation(s_sqv[:], s_veps[:], AF.Sqrt),
             waits=[("VSER", em.val("VSER"))])
        em.v(lambda e, i=i: e.reciprocal(s_rstd[i][:], s_sqv[:]),
             waits=[("SSER", em.val("SSER"))])
        wl = [("TSER", t_tile[i - 2])] if i >= 2 else []
        v_apply[i] = em.v(lambda e, xt=xt, xn=xn, i=i: e.tensor_scalar(
            xn[:], xt[:], s_mv[i][:, 0:1], s_rstd[i][:], ALU.subtract, ALU.mult),
            waits=wl)
        for c in range(NCK):
            k = i * NCK + c
            pb = p_a[k % 2][0:128, ((k // 2) % 2) * 128:((k // 2) % 2) * 128 + 128]
            wl = [("VSER", v_apply[i])]
            if k >= 4:
                wl.append(("VSER", ev_lo[k - 4]))
            tk = em.t(lambda e, xn=xn, c=c, pb=pb: e.transpose(
                pb, xn[:, c * 128:(c + 1) * 128], s_ident[:]), waits=wl)
            ev_s[k] = em.v(lambda e, c=c, i=i, pb=pb: e.tensor_copy(
                s_xnT[c][:, i * 128:(i + 1) * 128], pb),
                waits=[("TSER", tk)])
            wl = []
            if k >= 2:
                wl.append(("TSER", t_rmm[k - 2]))
            ev_lo[k] = em.v(lambda e, k=k, pb=pb, c=c, i=i: e.tensor_tensor(
                s_lo[k % 2][:], pb,
                s_xnT[c][:, i * 128:(i + 1) * 128], op=ALU.subtract), waits=wl)
            pdst = p_a[4 + i // 4][0:8, (i % 4) * 128:(i % 4) * 128 + 128]
            l_hi = s_wrhi[:, c * E:(c + 1) * E]
            l_lo = s_wrlo[:, c * E:(c + 1) * E]
            rhs_hi = s_xnT[c][:, i * 128:(i + 1) * 128]
            wl = [("VSER", ev_lo[k])]
            if k == 0:
                wl.append(("GDMA", ld_wbf))
            em.t(lambda e, pdst=pdst, l=l_hi, r=rhs_hi, c=c: e.matmul(
                pdst, l, r, start=(c == 0), stop=False, skip_group_check=True),
                waits=wl, inc=False)
            em.t(lambda e, pdst=pdst, l=l_hi, k=k: e.matmul(
                pdst, l, s_lo[k % 2][:], start=False, stop=False,
                skip_group_check=True), inc=False)
            t_rmm[k] = em.t(lambda e, pdst=pdst, l=l_lo, r=rhs_hi, c=c: e.matmul(
                pdst, l, r, start=False, stop=(c == NCK - 1), skip_group_check=True),
                inc=True)
        t_tile[i] = em.val("TSER")
    p1_v = em.val("VSER")
    p1_s = em.val("SSER")
    p1_t = em.val("TSER")

    tr_ev = {}
    for i in range(NT):
        em.v(lambda e, i=i: e.tensor_scalar(
            s_lgs[i % 2][0:8, 0:128],
            p_a[4 + i // 4][0:8, (i % 4) * 128:(i % 4) * 128 + 128],
            s_brT[:, 0:1], None, ALU.add),
            waits=[("TSER", t_tile[i]), ("SDMA", ld_consts)]
            + ([("TSER", tr_ev[("t", i - 2)])] if i >= 2 else []))
        v_lg = em.val("VSER")
        pb = p_a[2 + (i % 2)]
        wl = [("VSER", v_lg)]
        if i >= 2:
            wl.append(("VSER", tr_ev[i - 2]))
        if i < 2:
            wl.append(("SSER", p1_s))
        tk = em.t(lambda e, i=i, pb=pb: e.transpose(
            pb[0:128, 0:8], s_lgs[i % 2][0:8, 0:128], s_ident[0:8, 0:8]), waits=wl)
        tr_ev[("t", i)] = tk
        tr_ev[i] = em.v(lambda e, i=i, pb=pb: e.tensor_copy(
            s_Ltok[:, i * 8:(i + 1) * 8], pb[0:128, 0:8]), waits=[("TSER", tk)])

    L3 = s_Ltok[:].rearrange("p (t e) -> p t e", e=8)
    oh1_3 = s_oh1[:].rearrange("p (t e) -> p t e", e=8)
    oh2_3 = s_oh2[:].rearrange("p (t e) -> p t e", e=8)
    msk3 = s_msk[:].rearrange("p (t e) -> p t e", e=8)
    tmp3 = s_tmp[:].rearrange("p (t e) -> p t e", e=8)
    io3 = s_iotae[:].rearrange("p (t e) -> p t e", e=8)
    lb3 = s_lb[:].rearrange("p (t e) -> p t e", e=8)
    cbc3 = s_cbc[:].rearrange("p (t e) -> p t e", e=8)
    off3 = s_offbc[:].rearrange("p (t e) -> p t e", e=8)

    em.v(lambda e: e.reduce_max(s_max1[:], L3, axis=AX.X))
    em.v(lambda e: e.tensor_tensor(
        oh1_3, L3, s_max1[:].unsqueeze(2).broadcast_to((128, NT, 8)), op=ALU.is_equal))
    em.v(lambda e: e.tensor_tensor(tmp3, oh1_3, io3, op=ALU.mult),
         waits=[("GSER", g_consts)])
    em.v(lambda e: e.reduce_sum(s_e1[:], tmp3, axis=AX.X))
    em.v(lambda e: e.scalar_tensor_tensor(
        s_msk[:], s_oh1[:], -1.0e9, s_Ltok[:], op0=ALU.mult, op1=ALU.add))
    em.v(lambda e: e.reduce_max(s_max2[:], msk3, axis=AX.X))
    em.v(lambda e: e.tensor_tensor(
        oh2_3, msk3, s_max2[:].unsqueeze(2).broadcast_to((128, NT, 8)), op=ALU.is_equal))
    em.v(lambda e: e.tensor_tensor(tmp3, oh2_3, io3, op=ALU.mult))
    em.v(lambda e: e.reduce_sum(s_e2[:], tmp3, axis=AX.X))
    v_top2 = em.val("VSER")
    em.s(lambda e: e.activation(s_eL[:], s_Ltok[:], AF.Exp), waits=[("VSER", v_top2)])
    em.s(lambda e: e.activation(s_em1[:], s_max1[:], AF.Exp))
    em.s(lambda e: e.activation(s_em2[:], s_max2[:], AF.Exp))
    s_gate = em.val("SSER")
    em.v(lambda e: e.reduce_sum(s_ssum[:], s_eL[:].rearrange("p (t e) -> p t e", e=8),
                                axis=AX.X), waits=[("SSER", s_gate)])
    em.v(lambda e: e.reciprocal(s_rs[:], s_ssum[:]))
    em.v(lambda e: e.tensor_tensor(s_g1[:], s_em1[:], s_rs[:], op=ALU.mult))
    em.v(lambda e: e.tensor_tensor(s_g2[:], s_em2[:], s_rs[:], op=ALU.mult))
    v_cnt = em.v(lambda e: e.tensor_tensor(s_cnt[:], s_oh1[:], s_oh2[:], op=ALU.add))
    t1 = em.t(lambda e: e.matmul(p_a[6][0:128, 0:128], s_ustrict[:], s_cnt[:],
                                 start=True, stop=True, skip_group_check=True),
              waits=[("VSER", v_cnt)])
    t2 = em.t(lambda e: e.matmul(p_a[7][0:1, 0:128], s_ones[:, 0:1], s_cnt[:],
                                 start=True, stop=True, skip_group_check=True))
    em.v(lambda e: e.tensor_copy(s_pref[:], p_a[6][0:128, 0:128]), waits=[("TSER", t2)])
    v_totr = em.v(lambda e: e.tensor_copy(s_totr[:], p_a[7][0:1, 0:128]))
    t3 = em.t(lambda e: e.transpose(p_a[6][0:128, 0:1], s_totr[:], s_ident[0:1, 0:1]),
              waits=[("VSER", v_totr)])
    v_totc = em.v(lambda e: e.tensor_copy(s_totc[:], p_a[6][0:128, 0:1]),
                  waits=[("TSER", t3)])
    t4 = em.t(lambda e: e.matmul(p_a[7][0:128, 0:1], s_m16p[:], s_totc[:],
                                 start=True, stop=True, skip_group_check=True),
              waits=[("VSER", v_totc)])
    v_tbc = em.v(lambda e: e.tensor_copy(s_tbc[:], p_a[7][0:128, 0:1]),
                 waits=[("TSER", t4)])
    t5 = em.t(lambda e: e.transpose(p_a[6][0:1, 0:128], s_tbc[:], s_ident[:]),
              waits=[("VSER", v_tbc)])
    v_tbr = em.v(lambda e: e.tensor_copy(s_tbr[:], p_a[6][0:1, 0:128]),
                 waits=[("TSER", t5)])
    t6 = em.t(lambda e: e.matmul(p_a[7][0:128, 0:128], s_ones[0:1, :], s_tbr[:],
                                 start=True, stop=True, skip_group_check=True),
              waits=[("VSER", v_tbr)])
    em.v(lambda e: e.tensor_tensor(s_lb[:], p_a[7][0:128, 0:128], s_pref[:],
                                   op=ALU.add), waits=[("TSER", t6)])
    em.v(lambda e: e.tensor_tensor(tmp3, lb3, oh1_3, op=ALU.mult))
    em.v(lambda e: e.reduce_sum(s_lp1[:], tmp3, axis=AX.X))
    em.v(lambda e: e.tensor_tensor(tmp3, lb3, oh2_3, op=ALU.mult))
    em.v(lambda e: e.reduce_sum(s_lp2[:], tmp3, axis=AX.X))
    v_tots = em.v(lambda e: e.tensor_tensor(
        s_tot8[:], s_tbr[0:1, 120:128], s_totr[0:1, 120:128], op=ALU.add))

    em.g_dma(lambda e: e.dma_start(out=g_in_d.ap(), in_=s_tot8[:]),
             waits=[("VSER", v_tots)])
    em.g_cc(lambda e: e.collective_compute(
        "AllGather", ALU.bypass, replica_groups=[list(range(B))],
        ins=[g_in_d.ap().opt()], outs=[g_out_d.ap().opt()]), "CC1",
        waits=[("GDMA", em.val("GDMA"))])
    em.wait("gpsimd", "CC1", 1)
    g_G = em.g_dma(lambda e: e.dma_start(out=s_G[:], in_=g_out_d.ap()))
    t7 = em.t(lambda e: e.matmul(p_a[6][0:1, 0:8], s_maskb[:], s_G[:],
                                 start=True, stop=True, skip_group_check=True),
              waits=[("GDMA", g_G), ("SDMA", ld_consts)])
    v_offr = em.v(lambda e: e.tensor_copy(s_offr[:], p_a[6][0:1, 0:8]),
                  waits=[("TSER", t7)])
    v_crow = em.v(lambda e: e.tensor_tensor(s_crow[:], s_offr[:], s_ie8[:],
                                            op=ALU.add), waits=[("GSER", g_consts)])
    em.v(lambda e: e.tensor_tensor(s_crow2[:], s_offr[:], s_ie512[:], op=ALU.add))
    g_w1 = em.g_dma(lambda e: e.dma_start(out=crow_d.ap(), in_=s_crow[:]),
                    waits=[("VSER", em.val("VSER"))])
    g_w1b = em.g_dma(lambda e: e.dma_start(out=crow2_d.ap(), in_=s_crow2[:]))
    g_w2 = em.g_dma(lambda e: e.dma_start(out=offr_d.ap(), in_=s_offr[:]))
    em.g_dma(lambda e: e.dma_start(
        out=s_cbcr[:].rearrange("p (a b) -> p a b", b=8),
        in_=crow_d.ap().unsqueeze(1).broadcast_to((1, NT, 8))),
        waits=[("GDMA", g_w2)])
    em.g_dma(lambda e: e.dma_start(
        out=s_cbcr2[:].rearrange("p (a b) -> p a b", b=8),
        in_=crow2_d.ap().unsqueeze(1).broadcast_to((1, NT, 8))))
    g_bcr = em.g_dma(lambda e: e.dma_start(
        out=s_offbcr[:].rearrange("p (a b) -> p a b", b=8),
        in_=offr_d.ap().unsqueeze(1).broadcast_to((1, NT, 8))))
    t8 = em.t(lambda e: e.matmul(p_a[6][0:128, 0:128], s_ones[0:1, :], s_cbcr[:],
                                 start=True, stop=True, skip_group_check=True),
              waits=[("GDMA", g_bcr)])
    em.v(lambda e: e.tensor_copy(s_cbc[:], p_a[6][0:128, 0:128]), waits=[("TSER", t8)])
    t9 = em.t(lambda e: e.matmul(p_a[7][0:128, 0:128], s_ones[0:1, :], s_offbcr[:],
                                 start=True, stop=True, skip_group_check=True))
    v9 = em.v(lambda e: e.tensor_copy(s_offbc[:], p_a[7][0:128, 0:128]), waits=[("TSER", t9)])
    t9b = em.t(lambda e: e.matmul(p_a[7][0:128, 0:128], s_ones[0:1, :], s_cbcr2[:],
                                  start=True, stop=True, skip_group_check=True),
               waits=[("VSER", v9)])
    em.v(lambda e: e.tensor_copy(s_cbc2[:], p_a[7][0:128, 0:128]), waits=[("TSER", t9b)])

    cbc2_3 = s_cbc2[:].rearrange("p (t e) -> p t e", e=8)
    for (ohx, gpx, kx, slx, slux, rux, lpx) in (
            (oh1_3, s_gp1, s_k1, s_sl1, s_sl1u, s_r1u, s_lp1),
            (oh2_3, s_gp2, s_k2, s_sl2, s_sl2u, s_r2u, s_lp2)):
        em.v(lambda e, ohx=ohx: e.tensor_tensor(tmp3, ohx, off3, op=ALU.mult))
        em.v(lambda e: e.reduce_sum(s_t16a[:], tmp3, axis=AX.X))
        em.v(lambda e, gpx=gpx, lpx=lpx: e.tensor_tensor(gpx[:], s_t16a[:], lpx[:], op=ALU.add))
        em.v(lambda e, kx=kx, gpx=gpx: e.tensor_scalar(kx[:], gpx[:], float(CAP), None, ALU.is_lt))
        em.v(lambda e, ohx=ohx: e.tensor_tensor(tmp3, ohx, cbc3, op=ALU.mult))
        em.v(lambda e: e.reduce_sum(s_t16a[:], tmp3, axis=AX.X))
        em.v(lambda e, lpx=lpx: e.tensor_tensor(s_t16b[:], s_t16a[:], lpx[:], op=ALU.add))
        em.v(lambda e, kx=kx: e.tensor_scalar(s_t16a[:], kx[:], -1.0e7, 1.0e7, ALU.mult, ALU.add))
        em.v(lambda e, slx=slx: e.tensor_tensor(slx[:], s_t16b[:], s_t16a[:], op=ALU.add))
        em.v(lambda e, slux=slux, slx=slx: e.tensor_copy(slux[:], slx[:]))
        em.v(lambda e, gpx=gpx: e.tensor_scalar(s_blk[:], gpx[:], 512.0, None, ALU.is_ge))
        for kk in (1024.0, 1536.0, 2048.0):
            em.v(lambda e, gpx=gpx, kk=kk: e.scalar_tensor_tensor(
                s_blk[:], gpx[:], kk, s_blk[:], op0=ALU.is_ge, op1=ALU.add))
        em.v(lambda e, ohx=ohx: e.tensor_tensor(tmp3, ohx, cbc2_3, op=ALU.mult))
        em.v(lambda e: e.reduce_sum(s_t16a[:], tmp3, axis=AX.X))
        em.v(lambda e, lpx=lpx: e.tensor_tensor(s_t16b[:], s_t16a[:], lpx[:], op=ALU.add))
        em.v(lambda e: e.scalar_tensor_tensor(
            s_t16b[:], s_blk[:], 3584.0, s_t16b[:], op0=ALU.mult, op1=ALU.add))
        em.v(lambda e, kx=kx: e.tensor_scalar(s_t16a[:], kx[:], -1.0e7, 1.0e7, ALU.mult, ALU.add))
        em.v(lambda e: e.tensor_tensor(s_blk[:], s_t16b[:], s_t16a[:], op=ALU.add))
        em.v(lambda e, rux=rux: e.tensor_copy(rux[:], s_blk[:]))
    em.v(lambda e: e.tensor_tensor(s_t16a[:], s_g1[:], s_k1[:], op=ALU.mult))
    em.v(lambda e: e.tensor_tensor(s_t16b[:], s_g2[:], s_k2[:], op=ALU.mult))
    em.v(lambda e: e.tensor_tensor(s_ws[:], s_t16a[:], s_t16b[:], op=ALU.add))
    em.v(lambda e: e.tensor_scalar(s_rn[:], s_ws[:], 1.0e-6, None, ALU.max))
    em.v(lambda e: e.reciprocal(s_rn[:], s_rn[:]))
    em.v(lambda e: e.tensor_tensor(s_s1[:], s_t16a[:], s_rn[:], op=ALU.mult))
    em.v(lambda e: e.tensor_tensor(s_s2[:], s_t16b[:], s_rn[:], op=ALU.mult))
    v_vals = em.v(lambda e: e.tensor_scalar(s_val1[:], s_ptok[:], s_cidtok[:, 0:1],
                                            None, ALU.add), waits=[("GSER", g_consts),
                                                                   ("VSER", v_cid)])
    v_slots = em.val("VSER")
    for j in range(NT):
        em.g_dma(lambda e, j=j: e.indirect_dma_start(
            out=srcp_d.ap(), out_offset=IndirectOffsetOnAxis(s_sl1u[:, j:j + 1], 0),
            in_=s_val1[:, j:j + 1], in_offset=None,
            bounds_check=regs["slots"], oob_is_err=False),
            waits=[("VSER", v_slots), ("GDMA", zr_done)] if j == 0 else [])
    for j in range(NT):
        em.g_dma(lambda e, j=j: e.indirect_dma_start(
            out=srcp_d.ap(), out_offset=IndirectOffsetOnAxis(s_sl2u[:, j:j + 1], 0),
            in_=s_val1[:, j:j + 1], in_offset=None,
            bounds_check=regs["slots"], oob_is_err=False))
    g_scat = em.val("GDMA")
    em.g_cc(lambda e: e.collective_compute(
        "AllReduce", ALU.add, replica_groups=[list(range(B))],
        ins=[srcp_d.ap().rearrange("a b -> b a").opt()],
        outs=[srca_d.ap().rearrange("a b -> b a").opt()]), "CC2",
        waits=[("GDMA", g_scat)])


    em.v(lambda e: e.reduce_max(s_sc1[:], s_qw[:], axis=AX.X), waits=[("SDMA", ld_consts)])
    em.v(lambda e: e.tensor_scalar(s_w4a[:], s_qw[:], s_sc1[:, 0:1], None, ALU.subtract))
    em.s(lambda e: e.activation(s_w4[:], s_w4a[:], AF.Exp), waits=[("VSER", em.val("VSER"))])
    em.v(lambda e: e.reduce_sum(s_sc2[:], s_w4[:], axis=AX.X), waits=[("SSER", em.val("SSER"))])
    em.v(lambda e: e.reciprocal(s_sc2[:], s_sc2[:]))
    em.v(lambda e: e.tensor_scalar(s_w4[:], s_w4[:], s_sc2[:, 0:1], None, ALU.mult))
    v_w4 = em.val("VSER")
    t11 = em.t(lambda e: e.matmul(p_a[6][0:128, 0:4], s_ones[0:1, :], s_w4[:],
                                  start=True, stop=True, skip_group_check=True),
               waits=[("VSER", v_w4)])
    em.v(lambda e: e.tensor_copy(s_w128[:], p_a[6][0:128, 0:4]), waits=[("TSER", t11)])
    em.v(lambda e: e.tensor_scalar(s_mbdA[:], s_mbdm[:, 0:128], s_w128[:, 0:1],
                                   None, ALU.mult))
    em.v(lambda e: e.scalar_tensor_tensor(s_mbdB[:], s_mbdm[:, 128:256], s_w128[:, 1:2],
                                          s_mbdA[:], op0=ALU.mult, op1=ALU.add))
    em.v(lambda e: e.scalar_tensor_tensor(s_mbdA[:], s_mbdm[:, 256:384], s_w128[:, 2:3],
                                          s_mbdB[:], op0=ALU.mult, op1=ALU.add))
    em.v(lambda e: e.scalar_tensor_tensor(s_mbdB[:], s_mbdm[:, 384:512], s_w128[:, 3:4],
                                          s_mbdA[:], op0=ALU.mult, op1=ALU.add))
    v_mbd = em.v(lambda e: e.tensor_copy(s_mbdT[:], s_mbdB[:]))

    ev = {}
    gidx = 0
    t_kj = {}
    for j in range(NCK):
        wl = [("TSER", t_kj[j - 2])] if j >= 2 else []
        g_wk = em.g_dma(lambda e, j=j: e.dma_start(
            out=s_wks[j % 2][:].rearrange("p (c f) -> p c f", f=128),
            in_=wk.rearrange("(c p) d -> p c d", p=128)[:, :, j * 128:(j + 1) * 128]),
            waits=wl)
        for sblk in range(4):
            pb = p_a[gidx % 4]
            wl = [("GDMA", g_wk)]
            if gidx >= 4:
                wl.append(("SSER", ev[gidx - 4]))
            if gidx == 0:
                wl += [("GDMA", ld_wbf), ("VSER", p1_v), ("TSER", p1_t)]
            for c in range(NCK):
                em.t(lambda e, j=j, c=c, sblk=sblk, pb=pb: e.matmul(
                    pb[0:128, 0:512],
                    s_wks[j % 2][:, c * 128:(c + 1) * 128],
                    s_xnT[c][:, sblk * 512:(sblk + 1) * 512],
                    start=(c == 0), stop=(c == NCK - 1), skip_group_check=True),
                    waits=wl if c == 0 else [], inc=(c == NCK - 1))
            tk = em.val("TSER")
            ev[gidx] = em.s(lambda e, j=j, sblk=sblk, pb=pb: e.copy(
                s_kT[j][:, sblk * 512:(sblk + 1) * 512], pb[0:128, 0:512]),
                waits=[("TSER", tk)], chain=False)
            gidx += 1
        t_kj[j] = em.val("TSER")
    t_qj = {}
    for j in range(NCK):
        wl = [("TSER", t_qj[j - 2])] if j >= 2 else []
        g_wq = em.g_dma(lambda e, j=j: e.dma_start(
            out=s_wqs[j % 2][:].rearrange("p (c f) -> p c f", f=128),
            in_=wq.rearrange("(c p) d -> p c d", p=128)[:, :, j * 128:(j + 1) * 128]),
            waits=wl)
        for sblk in range(4):
            pb = p_a[gidx % 4]
            wl = [("SSER", ev[gidx - 4]), ("GDMA", g_wq)]
            for c in range(NCK):
                em.t(lambda e, j=j, c=c, sblk=sblk, pb=pb: e.matmul(
                    pb[0:128, 0:512],
                    s_wqs[j % 2][:, c * 128:(c + 1) * 128],
                    s_xnT[c][:, sblk * 512:(sblk + 1) * 512],
                    start=(c == 0), stop=(c == NCK - 1), skip_group_check=True),
                    waits=wl if c == 0 else [], inc=(c == NCK - 1))
            tk = em.val("TSER")
            ev[gidx] = em.s(lambda e, sblk=sblk, pb=pb: e.copy(
                s_qraw[:, sblk * 512:(sblk + 1) * 512], pb[0:128, 0:512]),
                waits=[("TSER", tk)], chain=False)
            gidx += 1
        vq = ev[gidx - 1]
        for sblk in range(4):
            pb = p_a[gidx % 4]
            wl = [("SSER", ev[gidx - 4]), ("SSER", vq)]
            if j == 0 and sblk == 0:
                wl.append(("VSER", v_mbd))
            em.t(lambda e, sblk=sblk, pb=pb: e.matmul(
                pb[0:128, 0:512], s_mbdT[:], s_qraw[:, sblk * 512:(sblk + 1) * 512],
                start=True, stop=True, skip_group_check=True), waits=wl)
            tk = em.val("TSER")
            ev[gidx] = em.s(lambda e, j=j, sblk=sblk, pb=pb: e.copy(
                s_qT[j][:, sblk * 512:(sblk + 1) * 512], pb[0:128, 0:512]),
                waits=[("TSER", tk)], chain=False)
            gidx += 1
        t_qj[j] = em.val("TSER")
    t_q_done = em.val("TSER")
    esR1.close()
    s_wvb = sb(esR2, "s_wvb", [128, NCK * D], BF16, side="right")
    g_wv = em.g_dma(lambda e: e.dma_start(
        out=s_wvb[:].rearrange("p (c d) -> p c d", d=D),
        in_=wv.rearrange("(c p) d -> p c d", p=128)),
        waits=[("TSER", t_q_done)])
    vt_ms = [0] * NT
    for i in range(NT):
        vt_ms[i] = em.v(lambda e, i=i: e.memset(s_vtok[i][:], 1.0))
    for i in range(NT):
        for half in range(2):
            pb = p_a[gidx % 4]
            wl = [("SSER", ev[gidx - 4]), ("VSER", vt_ms[i])]
            if i == 0 and half == 0:
                wl.append(("GDMA", g_wv))
            for c in range(NCK):
                em.t(lambda e, i=i, c=c, half=half, pb=pb: e.matmul(
                    pb[0:128, 0:512],
                    s_xnT[c][:, i * 128:(i + 1) * 128],
                    s_wvb[:, c * D + half * 512: c * D + (half + 1) * 512],
                    start=(c == 0), stop=(c == NCK - 1), skip_group_check=True),
                    waits=wl if c == 0 else [], inc=(c == NCK - 1))
            tk = em.val("TSER")
            dst3 = s_vtok[i][:].rearrange("p (h x) -> p h x", x=DH + 1)[:, half * 8:(half + 1) * 8, 0:DH]
            src3 = pb[0:128, 0:512].rearrange("p (h x) -> p h x", x=DH)
            ev[gidx] = em.s(lambda e, dst3=dst3, src3=src3: e.copy(dst3, src3),
                            waits=[("TSER", tk)], chain=False)
            gidx += 1
    p3_v = em.val("VSER")
    p3_s2 = em.val("SSER")
    p3_t = em.val("TSER")
    esR2.close()
    esR0.close()
    esL2.close()
    st_wc = {}
    for fc in range(32):
        wl = [("SDMA", st_wc[fc - 2])] if fc >= 2 else []
        gld = em.g_dma(lambda e, fc=fc: e.dma_start(
            out=s_wc[fc % 2][:].rearrange("p (c f) -> p c f", f=128),
            in_=w1e.rearrange("(c p) f -> p c f", p=128)[:, :, fc * 128:(fc + 1) * 128]),
            waits=wl)
        st_wc[fc] = em.sy(lambda e, fc=fc: e.dma_start(
            out=w1bf_d.ap()[fc, :, :], in_=s_wc[fc % 2][:]), waits=[("GDMA", gld)])
    for fc in range(32):
        wl = [("SDMA", st_wc[(32, fc - 2)])] if fc >= 2 else [("SDMA", st_wc[31])]
        gld = em.g_dma(lambda e, fc=fc: e.dma_start(
            out=s_wc[fc % 2][:], in_=w2e[fc * 128:(fc + 1) * 128, :]),
            waits=wl)
        st_wc[(32, fc)] = em.sy(lambda e, fc=fc: e.dma_start(
            out=w2bf_d.ap()[fc, :, :], in_=s_wc[fc % 2][:]), waits=[("GDMA", gld)])
    st_wcast = em.val("SDMA")

    s_bufT = [sb(esR3, f"s_bufT{c}", [128, CAP], BF16, side="right") for c in range(NCK)]
    s_xg = [sb(esR4, f"s_xg{i}", [128, D], side="right") for i in range(2)]
    s_xgn = [sb(esR4, f"s_xgn{i}", [128, D], BF16, side="right") for i in range(3)]

    em.wait("gpsimd", "CC2", 1)
    em.v(lambda e: e.tensor_scalar(s_cid20[:], s_cid128[:], float(NG), None, ALU.mult))
    em.v(lambda e: e.tensor_scalar(s_tmp[:, 0:1], s_pcol[:], s_cid20[:, 0:1],
                                   None, ALU.add))
    em.v(lambda e: e.tensor_scalar(s_tmp[:, 1:2], s_pcol[:], float(NG), None, ALU.is_ge))
    em.v(lambda e: e.scalar_tensor_tensor(s_tmp[:, 2:3], s_tmp[:, 1:2], 1.0e6,
                                          s_tmp[:, 0:1], op0=ALU.mult, op1=ALU.add))
    em.v(lambda e: e.tensor_copy(s_segidx[:], s_tmp[:, 2:3]))
    v_seg = em.val("VSER")
    gzw = em.g_c(lambda e: e.memset(s_eL[:], 0.0), waits=[("VSER", v_seg)])
    g_seg = em.g_dma(lambda e: e.indirect_dma_start(
        out=s_eL[:], out_offset=None,
        in_=srca_d.ap().rearrange("(a b) c -> a (b c)", b=128),
        in_offset=IndirectOffsetOnAxis(s_segidx[:, 0:1], 0),
        bounds_check=regs["seg"], oob_is_err=False),
        waits=[("GSER", gzw)])
    t_seg = em.t(lambda e: e.transpose(p_a[4][0:128, 0:128], s_eL[:], s_ident[:]),
                 waits=[("GDMA", g_seg), ("TSER", p3_t)])
    em.v(lambda e: e.tensor_copy(s_segT[:, 0:32], p_a[4][0:128, 0:32]),
         waits=[("TSER", t_seg)])
    em.v(lambda e: e.tensor_scalar(s_tmp[:, 0:NG], s_segT[:, 0:NG], 0.0, None,
                                   ALU.is_equal))
    em.v(lambda e: e.scalar_tensor_tensor(s_tmp[:, 32:32 + NG], s_tmp[:, 0:NG], 1.0e7,
                                          s_segT[:, 0:NG], op0=ALU.mult, op1=ALU.add))
    em.v(lambda e: e.tensor_scalar(s_tmp[:, 32:32 + NG], s_tmp[:, 32:32 + NG], -1.0,
                                   None, ALU.add))
    em.v(lambda e: e.tensor_copy(s_srcidx[:, 0:NG], s_tmp[:, 32:32 + NG]))
    v_srcidx = em.val("VSER")

    v_buf = {}
    gz = {}

    def emit_p4_group(g):
        gz[g] = em.g_c(lambda e, g=g: e.memset(s_xg[g % 2][:], 0.0),
                       waits=([("VSER", v_buf[g - 2])] if g >= 2 else [("VSER", v_srcidx)]))
        g_gath = em.g_dma(lambda e, g=g: e.indirect_dma_start(
            out=s_xg[g % 2][:], out_offset=None,
            in_=x_full, in_offset=IndirectOffsetOnAxis(s_srcidx[:, g:g + 1], 0),
            bounds_check=regs["tok"], oob_is_err=False),
            waits=[("GSER", gz[g])])
        xgn = s_xgn[g % 3]
        em.v(lambda e, g=g: e.bn_stats(s_bn6[:, 0:6], s_xg[g % 2][:, 0:512]),
             waits=[("GDMA", g_gath)])
        em.v(lambda e, g=g: e.bn_stats(s_bn6[:, 6:12], s_xg[g % 2][:, 512:1024]))
        em.v(lambda e: e.bn_aggr(s_mvg[:], s_bn6[:]))
        em.v(lambda e: e.tensor_scalar(s_veps[:], s_mvg[:, 1:2], EPS, None, ALU.add))
        em.s(lambda e: e.activation(s_sqv[:], s_veps[:], AF.Sqrt),
             waits=[("VSER", em.val("VSER"))])
        em.v(lambda e: e.reciprocal(s_rstg[:], s_sqv[:]), waits=[("SSER", em.val("SSER"))])
        wl = []
        if g >= 3:
            wl.append(("SDMA", v_buf[("t", g - 3)]))
        em.v(lambda e, g=g, xgn=xgn: e.tensor_scalar(
            xgn[:], s_xg[g % 2][:], s_mvg[:, 0:1], s_rstg[:, 0:1], ALU.subtract, ALU.mult),
            waits=wl)
        v_apply_g = em.val("VSER")
        for c in range(NCK):
            em.sy(lambda e, xgn=xgn, c=c, g=g: e.dma_start(
                out=s_bufT[c][:, g * 128:(g + 1) * 128],
                in_=xgn[:, c * 128:(c + 1) * 128], transpose=True),
                waits=[("VSER", v_apply_g)] if c == 0 else [])
        v_buf[g] = em.val("VSER")
        v_buf[("t", g)] = em.val("SDMA")

    ctxN = s_qT
    last_exp = {0: 0, 1: 0}
    last_avmm = {}
    v_norm_unit = {}
    u = 0
    for p in range(NPAIR):
        hA, hB = 2 * p, 2 * p + 1
        for sq in range(4):
            setA = p_a[4 + 2 * (u % 2)]
            setB = p_a[5 + 2 * (u % 2)]
            exp_done = [0] * NT
            for i in range(NT + 1):
                if i < NT:
                    par = i % 2
                    wl = [("SSER", last_exp[par])]
                    if u == 0 and i == 0:
                        wl += [("VSER", p3_v), ("SSER", p3_s2)]
                    if i == 0 and u >= 1:
                        wl.append(("VSER", v_norm_unit[u - 1]))
                    if i == 0 and u >= 2:
                        wl.append(("VSER", v_norm_unit[u - 2]))
                    em.t(lambda e, p=p, i=i, sq=sq, par=par: e.matmul(
                        p_a[par][0:128, 0:512],
                        s_kT[p][0:64, i * 128:(i + 1) * 128],
                        s_qT[p][0:64, sq * 512:(sq + 1) * 512],
                        start=True, stop=True, skip_group_check=True,
                        tile_position=(0, 0)), waits=wl, inc=False)
                    tk = em.t(lambda e, p=p, i=i, sq=sq, par=par: e.matmul(
                        p_a[2 + par][0:128, 0:512],
                        s_kT[p][64:128, i * 128:(i + 1) * 128],
                        s_qT[p][64:128, sq * 512:(sq + 1) * 512],
                        start=True, stop=True, skip_group_check=True,
                        tile_position=(64, 0)), inc=True)
                    wl = [("TSER", tk)]
                    if i >= 2:
                        wl.append(("TSER", last_avmm[(u, i - 2)]))
                    em.s(lambda e, par=par: e.activation(
                        s_P[par][:], p_a[par][0:128, 0:512], AF.Exp, scale=SCALE),
                        waits=wl, chain=False)
                    em.s(lambda e, par=par: e.activation(
                        s_P[2 + par][:], p_a[2 + par][0:128, 0:512], AF.Exp, scale=SCALE),
                        chain=False)
                    exp_done[i] = em.val("SSER")
                    last_exp[par] = exp_done[i]
                if i >= 1:
                    ii = i - 1
                    par = ii % 2
                    wl = [("SSER", exp_done[ii])]
                    em.t(lambda e, ii=ii, hA=hA, par=par, setA=setA: e.matmul(
                        setA[0:65, 0:512],
                        s_vtok[ii][:, hA * 65:hA * 65 + 65],
                        s_P[par][:],
                        start=(ii == 0), stop=(ii == NT - 1), skip_group_check=True),
                        waits=wl, inc=False)
                    last_avmm[(u, ii)] = em.t(
                        lambda e, ii=ii, hB=hB, par=par, setB=setB: e.matmul(
                            setB[0:65, 0:512],
                            s_vtok[ii][:, hB * 65:hB * 65 + 65],
                            s_P[2 + par][:],
                            start=(ii == 0), stop=(ii == NT - 1), skip_group_check=True),
                        inc=True)
            t_av = em.val("TSER")
            em.v(lambda e, setA=setA: e.reciprocal(s_recA[:], setA[64:65, 0:512]),
                 waits=[("TSER", t_av)])
            v_den = em.v(lambda e, setB=setB: e.reciprocal(s_recB[:], setB[64:65, 0:512]))
            wl = [("VSER", v_den), ("SSER", last_exp[0])]
            em.t(lambda e: e.matmul(
                p_a[0][0:128, 0:512], s_sello[:], s_recA[:],
                start=True, stop=False, skip_group_check=True), waits=wl, inc=False)
            t_bc = em.t(lambda e: e.matmul(
                p_a[0][0:128, 0:512], s_selhi[:], s_recB[:],
                start=False, stop=True, skip_group_check=True), inc=True)
            em.v(lambda e, setA=setA: e.tensor_copy(s_stage[0:64, :], setA[0:64, 0:512]),
                 waits=[("TSER", t_av)])
            em.v(lambda e, setB=setB: e.tensor_copy(s_stage[64:128, :], setB[0:64, 0:512]))
            v_norm_unit[u] = em.v(lambda e, p=p, sq=sq: e.tensor_tensor(
                ctxN[p][:, sq * 512:(sq + 1) * 512], s_stage[:], p_a[0][0:128, 0:512],
                op=ALU.mult), waits=[("TSER", t_bc)])
            if u < NG:
                emit_p4_group(u)
            u += 1
    p4_v = em.val("VSER")
    p4_st = em.val("SDMA")
    esR4.close()
    p5_v = em.val("VSER")
    p5_s = em.val("SSER")
    p5_t = em.val("TSER")
    esL1.close()

    s_hT = [sb(esR5, f"s_hT{f}", [128, 512], BF16, side="right") for f in range(32)]
    s_w1f = [sb(esR5, f"s_w1f{i}", [128, 2 * D], BF16, side="right") for i in range(2)]
    s_w2f = [sb(esR5, f"s_w2f{i}", [128, 2 * D], BF16, side="right") for i in range(2)]
    s_fout = [sb(esR5, f"s_fout{i}", [128, D], BF16, side="right") for i in range(4)]

    mm2_ev = 0
    t_mm2_prev = 0
    g_w1ld = {}
    s_gelu = {}
    st_contrib_blk = {}
    for blk in range(5):
        for fc in range(32):
            if fc % 2 == 0:
                wl = [("SDMA", st_wcast)]
                prev = (blk * 32 + fc) - 4
                if prev >= 0:
                    wl.append(("TSER", g_w1ld.get(("mm", prev + 1), 0)))
                g_w1ld[("ld", blk * 32 + fc)] = em.g_dma(
                    lambda e, fc=fc: e.dma_start(
                        out=s_w1f[(fc // 2) % 2][:].rearrange("p (a d) -> p a d", a=2),
                        in_=w1bf_d.ap()[fc:fc + 2, :, :].rearrange("a p d -> p a d")),
                    waits=wl)
            pb = p_a[6 + (fc % 2)]
            wl = [("GDMA", g_w1ld[("ld", blk * 32 + fc - (fc % 2))])]
            if blk == 0 and fc in (0, 1):
                wl += [("VSER", p4_v), ("SDMA", p4_st), ("VSER", p5_v), ("SSER", p5_s)]
            if blk * 32 + fc >= 2:
                wl.append(("SSER", s_gelu[blk * 32 + fc - 2]))
            if blk > 0 and fc in (0, 1):
                wl.append(("VSER", mm2_ev))
            for c in range(NCK):
                em.t(lambda e, c=c, fc=fc, blk=blk, pb=pb: e.matmul(
                    pb[0:128, 0:512],
                    s_w1f[(fc // 2) % 2][:, (fc % 2) * D + c * 128:(fc % 2) * D + (c + 1) * 128],
                    s_bufT[c][:, blk * 512:(blk + 1) * 512],
                    start=(c == 0), stop=(c == NCK - 1), skip_group_check=True),
                    waits=wl if c == 0 else [], inc=(c == NCK - 1))
            tk = em.val("TSER")
            g_w1ld[("mm", blk * 32 + fc)] = tk
            gwl = [("TSER", tk)]
            if blk == 0 and fc == 0:
                gwl.append(("VSER", v_b1T))
            s_gelu[blk * 32 + fc] = em.s(lambda e, fc=fc, pb=pb: e.activation(
                s_hT[fc][:], pb[0:128, 0:512], AF.Gelu, bias=s_b1T[:, fc:fc + 1]),
                waits=gwl, chain=False)
        if dbg and blk == 2:
            em.sy(lambda e: e.dma_start(out=dbg_o["d_h0"], in_=s_hT[0][:]),
                  waits=[("SSER", s_gelu[blk * 32 + 0])])
            em.sy(lambda e: e.dma_start(out=dbg_o["d_w1f0"], in_=s_w1f[0][:]),
                  waits=[("SSER", s_gelu[blk * 32 + 0]), ("SSER", s_gelu[blk * 32 + 1])])
            em.sy(lambda e: e.dma_start(out=dbg_o["d_buf0"], in_=s_bufT[0][:, blk * 512:(blk + 1) * 512]),
                  waits=[("SSER", s_gelu[blk * 32 + 0])])
        s_gelu_all = em.val("SSER")
        for fc in range(32):
            if fc % 2 == 0:
                w2wl = [("SDMA", st_wcast)]
                if fc >= 4:
                    w2wl.append(("TSER", g_w1ld.get(("mm2", blk, fc - 3), 0)))
                elif blk > 0:
                    w2wl.append(("TSER", t_mm2_prev))
                g_w2ld = em.g_dma(lambda e, fc=fc: e.dma_start(
                    out=s_w2f[(fc // 2) % 2][:].rearrange("p (a d) -> p a d", a=2),
                    in_=w2bf_d.ap()[fc:fc + 2, :, :].rearrange("a p d -> p a d")),
                    waits=w2wl)
            for ct in range(4):
                for half in range(2):
                    pb = p_a[2 * ct + half]
                    wl = []
                    if fc == 0 and ct == 0 and half == 0:
                        wl = [("GDMA", g_w2ld), ("SSER", s_gelu_all)]
                        if blk > 0:
                            wl.append(("VSER", mm2_ev))
                        else:
                            wl.append(("VSER", p5_v))
                    elif ct == 0 and half == 0:
                        wl = [("GDMA", g_w2ld)]
                    em.t(lambda e, fc=fc, ct=ct, half=half, pb=pb: e.matmul(
                        pb[0:128, 0:512],
                        s_hT[fc][:, ct * 128:(ct + 1) * 128],
                        s_w2f[(fc // 2) % 2][:, (fc % 2) * D + half * 512:(fc % 2) * D + (half + 1) * 512],
                        start=(fc == 0), stop=False, skip_group_check=True),
                        waits=wl, inc=(ct == 3 and half == 1))
            g_w1ld[("mm2", blk, fc)] = em.val("TSER")
        for ct in range(4):
            for half in range(2):
                pb = p_a[2 * ct + half]
                em.t(lambda e, ct=ct, half=half, pb=pb: e.matmul(
                    pb[0:128, 0:512], s_onesb[0:1, 0:128],
                    s_b2[0:1, half * 512:(half + 1) * 512],
                    start=False, stop=True, skip_group_check=True),
                    inc=(ct == 3 and half == 1))
        t_mm2 = em.val("TSER")
        t_mm2_prev = t_mm2
        for ct in range(4):
            for half in range(2):
                wl = [("TSER", t_mm2)] if (ct == 0 and half == 0) else []
                if blk >= 1 and ct == 0 and half == 0:
                    wl.append(("SDMA", st_contrib_blk[blk - 1]))
                em.v(lambda e, ct=ct, half=half: e.tensor_copy(
                    s_fout[ct][:, half * 512:(half + 1) * 512],
                    p_a[2 * ct + half][0:128, 0:512]),
                    waits=wl)
        mm2_ev = em.val("VSER")
        if dbg and blk == 2:
            for ct in range(4):
                em.sy(lambda e, ct=ct: e.dma_start(
                    out=dbg_o["d_fout"][ct * 128:(ct + 1) * 128, :], in_=s_fout[ct][:]),
                    waits=[("VSER", mm2_ev)] if ct == 0 else [])
            em.sy(lambda e: e.dma_start(out=dbg_o["d_w2f"], in_=s_w2f[1][:]),
                  waits=[("VSER", mm2_ev)])
        for ct in range(4):
            g = blk * 4 + ct
            em.sy(lambda e, ct=ct, g=g: e.dma_start(
                out=contrib_d.ap()[g * 128:(g + 1) * 128, :], in_=s_fout[ct][:]),
                waits=[("VSER", mm2_ev)] if ct == 0 else [])
        st_contrib_blk[blk] = em.val("SDMA")
    st_contrib = em.val("SDMA")
    for blk in range(5):
        em.g_cc(lambda e, blk=blk: e.collective_compute(
            "AllGather", ALU.bypass, replica_groups=[list(range(B))],
            ins=[contrib_d.ap()[blk * 512:(blk + 1) * 512, :].opt()],
            outs=[outall_d.ap()[blk * 4096:(blk + 1) * 4096, :].opt()]),
            f"CC3_{blk}", waits=[("SDMA", st_contrib)])
    esR5.close()
    esR3.close()

    s_wob = sb(esR6, "s_wob", [128, NCK * D], BF16, side="right")
    s_og1 = [sb(esR6, f"s_og1{i}", [128, D], BF16, side="right") for i in range(2)]
    s_og2 = [sb(esR6, f"s_og2{i}", [128, D], BF16, side="right") for i in range(2)]
    s_yf = [sb(esR6, f"s_yf{i}", [128, D], side="right") for i in range(2)]
    s_xy = [sb(esR6, f"s_xy{i}", [128, D], side="right") for i in range(2)]
    s_out = [sb(esR6, f"s_out{i}", [128, D], side="right") for i in range(2)]

    g_wo = em.g_dma(lambda e: e.dma_start(
        out=s_wob[:].rearrange("p (c d) -> p c d", d=D),
        in_=wo.rearrange("(c p) d -> p c d", p=128)))

    st_tile = {}
    op_ev = {}
    cm = {}
    ccw_done = False
    for i in range(NT):
        pb0 = p_a[4 + 2 * (i % 2)]
        pb1 = p_a[5 + 2 * (i % 2)]
        wl = [("GDMA", g_wo)]
        if i >= 2:
            wl.append(("VSER", op_ev[i - 2]))
        else:
            wl.append(("VSER", mm2_ev))
            wl.append(("VSER", p5_v))
        for c in range(NCK):
            em.t(lambda e, i=i, c=c, pb0=pb0: e.matmul(
                pb0[0:128, 0:512], ctxN[c][:, i * 128:(i + 1) * 128],
                s_wob[:, c * D: c * D + 512],
                start=(c == 0), stop=(c == NCK - 1), skip_group_check=True),
                waits=wl if c == 0 else [], inc=False)
            em.t(lambda e, i=i, c=c, pb1=pb1: e.matmul(
                pb1[0:128, 0:512], ctxN[c][:, i * 128:(i + 1) * 128],
                s_wob[:, c * D + 512: c * D + 1024],
                start=(c == 0), stop=(c == NCK - 1), skip_group_check=True),
                inc=(c == NCK - 1))
        t_op = em.val("TSER")
        gz1 = em.g_c(lambda e, i=i: e.memset(s_og1[i % 2][:], 0.0),
                     waits=[("VSER", cm[i - 2])] if i >= 2 else [])
        gz2 = em.g_c(lambda e, i=i: e.memset(s_og2[i % 2][:], 0.0))
        if not ccw_done:
            for bb in range(5):
                em.wait("gpsimd", f"CC3_{bb}", 1)
            ccw_done = True
        em.g_dma(lambda e, i=i: e.indirect_dma_start(
            out=s_og1[i % 2][:], out_offset=None, in_=outall_d.ap(),
            in_offset=IndirectOffsetOnAxis(s_r1u[:, i:i + 1], 0),
            bounds_check=regs["slots"], oob_is_err=False),
            waits=[("GSER", gz2), ("VSER", v_slots)])
        gog = em.g_dma(lambda e, i=i: e.indirect_dma_start(
            out=s_og2[i % 2][:], out_offset=None, in_=outall_d.ap(),
            in_offset=IndirectOffsetOnAxis(s_r2u[:, i:i + 1], 0),
            bounds_check=regs["slots"], oob_is_err=False))
        xl = em.sy(lambda e, i=i: e.dma_start(
            out=s_xt[i % 2][:], in_=x_own[i * 128:(i + 1) * 128, :]),
            waits=[("VSER", cm[i - 2]) if i >= 2 else ("VSER", p1_v)])
        em.v(lambda e, i=i: e.tensor_scalar(
            s_yf[i % 2][:], s_og1[i % 2][:], s_s1[:, i:i + 1], None, ALU.mult),
            waits=[("GDMA", gog)])
        em.v(lambda e, i=i: e.scalar_tensor_tensor(
            s_yf[i % 2][:], s_og2[i % 2][:], s_s2[:, i:i + 1], s_yf[i % 2][:],
            op0=ALU.mult, op1=ALU.add))
        em.v(lambda e, i=i: e.tensor_tensor(
            s_xy[i % 2][:], s_xt[i % 2][:], s_yf[i % 2][:], op=ALU.add),
            waits=[("SDMA", xl)])
        wl = [("TSER", t_op)]
        if i >= 2:
            wl.append(("SDMA", st_tile[i - 2]))
        em.v(lambda e, i=i, pb0=pb0: e.tensor_tensor(
            s_out[i % 2][:, 0:512], pb0[0:128, 0:512], s_xy[i % 2][:, 0:512],
            op=ALU.add), waits=wl)
        op_ev[i] = em.v(lambda e, i=i, pb1=pb1: e.tensor_tensor(
            s_out[i % 2][:, 512:1024], pb1[0:128, 0:512], s_xy[i % 2][:, 512:1024],
            op=ALU.add))
        cm[i] = em.val("VSER")
        st_tile[i] = em.sy(lambda e, i=i: e.dma_start(
            out=y_out[i * 128:(i + 1) * 128, :], in_=s_out[i % 2][:]),
            waits=[("VSER", cm[i])])
    em.wait("sync", "SDMA", st_tile[NT - 1])
    em.wait("gpsimd", "GDMA", em.val("GDMA"))

    if dbg:
        snap = em.snap() if hasattr(em, "snap") else dict(em.cnt)
        dmap = [
            ("d_e1", s_e1[:]), ("d_e2", s_e2[:]),
            ("d_lp1", s_lp1[:]), ("d_lp2", s_lp2[:]), ("d_g1", s_g1[:]),
            ("d_g2", s_g2[:]), ("d_sl1", s_sl1[:]), ("d_sl2", s_sl2[:]),
            ("d_G", s_G[:]), ("d_offr", s_offr[:]),
            ("d_src", s_segT[:]),
            ("d_qT0", s_qT[0][:]),
            ("d_yf0", s_yf[0][:]), ("d_mv0", s_mv[0][:]),
            ("d_gp1", s_gp1[:]), ("d_gp2", s_gp2[:]),
            ("d_contrib", contrib_d.ap()), ("d_outall", outall_d.ap()[0:4096, :]),
            ("d_w1bf", w1bf_d.ap().rearrange("a b c -> (a b) c")),
            ("d_r1u", s_r1u[:]), ("d_r2u", s_r2u[:]),
            ("d_k1", s_k1[:]), ("d_k2", s_k2[:]),
        ]
        for name, src in dmap:
            em.sy(lambda e, name=name, src=src: e.dma_start(out=dbg_o[name], in_=src),
                  waits=[("VSER", snap.get("VSER", 0)), ("SSER", snap.get("SSER", 0))])
        em.wait("sync", "SDMA", em.val("SDMA"))

    with nc.Block() as block:
        for eng in ENGS:
            def mk(eng):
                def body(e, _ops=em.ops[eng]):
                    for f in _ops:
                        f(e)
                return body
            getattr(block, eng)(mk(eng))

    esR6.close()
    es.close()
    return nc


def _stage_inputs(inputs):
    x = np.ascontiguousarray(np.asarray(inputs["L0_attn_x"], dtype=np.float32))
    wq = np.ascontiguousarray(np.asarray(inputs["L0_attn_Wq"], dtype=np.float32))
    wk = np.ascontiguousarray(np.asarray(inputs["L0_attn_Wk"], dtype=np.float32))
    wv = np.ascontiguousarray(np.asarray(inputs["L0_attn_Wv"], dtype=np.float32))
    wo = np.ascontiguousarray(np.asarray(inputs["L0_attn_Wo"], dtype=np.float32))
    qw = np.ascontiguousarray(np.asarray(inputs["L0_attn_quadrant_weights"],
                                         dtype=np.float32).reshape(1, 4))
    wr = np.ascontiguousarray(np.asarray(inputs["L0_ffn_w_router"], dtype=np.float32))
    br = np.ascontiguousarray(np.asarray(inputs["L0_ffn_b_router"],
                                         dtype=np.float32).reshape(1, E))
    w1 = np.ascontiguousarray(np.asarray(inputs["L0_ffn_w1"], dtype=np.float32))
    b1 = np.ascontiguousarray(np.asarray(inputs["L0_ffn_b1"], dtype=np.float32))
    w2 = np.ascontiguousarray(np.asarray(inputs["L0_ffn_w2"], dtype=np.float32))
    b2 = np.ascontiguousarray(np.asarray(inputs["L0_ffn_b2"], dtype=np.float32))

    x_full = np.ascontiguousarray(x.reshape(NTOK, D))
    ident = np.eye(128, dtype=np.float32)
    ustrict = np.triu(np.ones((128, 128), np.float32), k=1)
    m16p = np.zeros((128, 128), np.float32)
    for tp in range(NT):
        for ee in range(8):
            for tt in range(NT):
                if tp < tt:
                    m16p[tp * 8 + ee, tt * 8 + ee] = 1.0
    mbdm = np.zeros((4, 128, 128), np.float32)
    for h in range(2):
        base = 64 * h
        for j in range(32):
            mbdm[0][base + j, base + j] = 1.0
            mbdm[1][base + j, base + 32 + j] = 1.0
            mbdm[2][base + 32 + j, base + j] = 1.0
            mbdm[3][base + 32 + j, base + 32 + j] = 1.0
    mbdm = mbdm.reshape(512, 128)
    selmask = np.zeros((2, 128), np.float32)
    selmask[0, 0:64] = 1.0
    selmask[1, 64:128] = 1.0

    in_maps = []
    for b in range(B):
        maskb = np.zeros((E, 1), np.float32)
        maskb[:b, 0] = 1.0
        in_maps.append(dict(
            x_own=np.ascontiguousarray(x[b]),
            x_full=x_full,
            wq=wq, wk=wk, wv=wv, wo=wo, qw=qw, wr=wr, br=br,
            w1e=np.ascontiguousarray(w1[b]),
            b1e=np.ascontiguousarray(b1[b].reshape(1, FF)),
            w2e=np.ascontiguousarray(w2[b]),
            b2e=np.ascontiguousarray(b2[b].reshape(1, D)),
            cid=np.array([[float(b)]], np.float32),
            maskb=maskb,
            ident=ident, ustrict=ustrict, m16p=m16p, mbdm=mbdm,
            selmask=selmask,
        ))
    return in_maps


def kernel(**inputs):
    in_maps = _stage_inputs(inputs)
    nc = build(dbg=False)
    res = run_bass_kernel_spmd(nc, in_maps, core_ids=list(range(B)))
    y = np.stack([np.asarray(r["y"], dtype=np.float32) for r in res.results], axis=0)
    return y
